# version 26
# baseline (speedup 1.0000x reference)
import numpy as np
import concourse.bass as bass
import concourse.mybir as mybir
from concourse.bass_utils import run_bass_kernel_spmd

F32 = mybir.dt.float32
BF16 = mybir.dt.bfloat16
I8 = mybir.dt.int8
AF = mybir.ActivationFunctionType
ALU = mybir.AluOpType
AX = mybir.AxisListType.X

T = 2048
NT = 16
D = 1024
ND = 8
HD = 512
NHD = 4
H = 8
L = 256
NCH = 8
I_FFN = 4096
NI = 32
N_CORES = 8
EPS = 1e-5
QPL = 12288
HG = 4
NHG = NT // HG
WCNT = 1024 * 4096


class Ctr:
    def __init__(self, nc, name):
        self.cm = nc.semaphore(name)
        self.h = self.cm.__enter__()
        self.v = 0

    def inc(self, instr, n=1):
        instr.then_inc(self.h, n)
        self.v += n
        return instr


def build_nc(stop="all", debug=False):
    nc = bass.Bass("TRN2", target_bir_lowering=False, debug=False)
    _stack = []
    dump_reg = {}

    x_in = nc.dram_tensor("x", [T, D], F32, kind="ExternalInput").ap()
    wq_in = nc.dram_tensor("wq", [D, HD], F32, kind="ExternalInput").ap()
    wk_in = nc.dram_tensor("wk", [D, HD], F32, kind="ExternalInput").ap()
    wv_in = nc.dram_tensor("wv", [D, HD], F32, kind="ExternalInput").ap()
    wo_in = nc.dram_tensor("wo", [HD, D], F32, kind="ExternalInput").ap()
    betas_in = nc.dram_tensor("betas_bc", [128, HD], F32, kind="ExternalInput").ap()
    wg_sl = nc.dram_tensor("wg_slice", [128, I_FFN], F32, kind="ExternalInput").ap()
    wu_sl = nc.dram_tensor("wu_slice", [128, I_FFN], F32, kind="ExternalInput").ap()
    wd_sl = nc.dram_tensor("wd_slice", [512, D], F32, kind="ExternalInput").ap()
    mask_in = nc.dram_tensor("mask_bc", [128, N_CORES], F32, kind="ExternalInput").ap()
    out = nc.dram_tensor("out", [T, D], F32, kind="ExternalOutput").ap()

    ag1_in = nc.dram_tensor("ag1_in", [1, 8], F32).ap()
    ag1_out = nc.dram_tensor("ag1_out", [N_CORES, 8], F32).ap()
    ag2_in = nc.dram_tensor("ag2_in", [1, 128 * QPL], BF16).ap()
    ag2_out = nc.dram_tensor("ag2_out", [N_CORES, 128 * QPL], BF16).ap()
    ag3_in = nc.dram_tensor("ag3_in", [1, 64 * 520], F32).ap()
    ag3_out = nc.dram_tensor("ag3_out", [N_CORES, 64 * 520], F32).ap()

    ag2_v = ag2_out.rearrange("r (p q) -> r p q", p=128)
    ag2i_v = ag2_in.rearrange("r (p q) -> r p q", p=128)
    ag3_v = ag3_out.rearrange("r (p q) -> r p q", p=64)

    dX = Ctr(nc, "dX")
    dS = Ctr(nc, "dS")
    dG = Ctr(nc, "dG")
    xs = [Ctr(nc, "xs0"), Ctr(nc, "xs1"), Ctr(nc, "xs2"), Ctr(nc, "xs3")]
    ws = [Ctr(nc, "ws0"), Ctr(nc, "ws1")]
    x2s = [Ctr(nc, "x2s0"), Ctr(nc, "x2s1")]
    osm = [Ctr(nc, "osm0"), Ctr(nc, "osm1")]
    asm_ = [Ctr(nc, "asm0"), Ctr(nc, "asm1")]
    gus = [Ctr(nc, "gus0"), Ctr(nc, "gus1"), Ctr(nc, "gus2"),
           Ctr(nc, "gus3")]
    wds = [Ctr(nc, "wds0"), Ctr(nc, "wds1")]
    ags = [Ctr(nc, "ags0"), Ctr(nc, "ags1")]
    tS = Ctr(nc, "tS")
    vS = Ctr(nc, "vS")
    sS = Ctr(nc, "sS")
    gS = Ctr(nc, "gS")
    cS = Ctr(nc, "cS")

    def wt(eng, ctr, val=None):
        v = ctr.v if val is None else val
        assert v <= ctr.v, "wait on a future value"
        if v > 0:
            eng.wait_ge(ctr.h, v)

    psum_cms = [nc.psum_tensor(f"pb{i}", [128, 512], F32) for i in range(8)]
    pb = [p.__enter__() for p in psum_cms]
    _stack.extend(psum_cms)
    pbb = [p[:].bitcast(BF16) for p in pb]
    bank_w = [None] * 8
    bank_r = [None] * 8

    def wr_bank(eng, b):
        if bank_r[b] is not None:
            wt(eng, *bank_r[b])

    pers_cm = [
        nc.sbuf_tensor("ident_bf", [128, 128], BF16),
        nc.sbuf_tensor("ones_col", [128, 1], F32),
        nc.sbuf_tensor("ones_row", [1, 128], F32),
        nc.sbuf_tensor("gate_sb", [128, HD], F32),
        nc.sbuf_tensor("gate1m", [128, HD], F32),
        nc.sbuf_tensor("wo_bf", [128, NHD * 1024], BF16),
        nc.sbuf_tensor("xq_tm", [128, NT * D], I8),
        nc.sbuf_tensor("mxc_all", [128, NT], F32),
        nc.sbuf_tensor("mbc", [128, 8], F32),
        nc.sbuf_tensor("mask_sb", [128, N_CORES], F32),
        nc.sbuf_tensor("stage", [128, 2 * D], F32),
        nc.sbuf_tensor("scal", [128, 32], F32),
        nc.sbuf_tensor("srow", [1, 128], F32),
        nc.sbuf_tensor("epsc", [128, 1], F32),
        nc.sbuf_tensor("negone", [128, 1], F32),
        nc.sbuf_tensor("kc", [128, 4], F32),
    ]
    (ident_bf, ones_col, ones_row, gate_sb, gate1m, wo_bf, xq_tm, mxc_all, mbc,
     mask_sb, stage, scal, srow, epsc, negone, kc) = [t.__enter__() for t in pers_cm]
    _stack.extend(pers_cm)

    ga1_cm = [
        nc.sbuf_tensor("q_fm", [128, NHD * T], BF16),
        nc.sbuf_tensor("k_fm", [128, NHD * T], BF16),
        nc.sbuf_tensor("v_tm", [128, NT * 520], BF16),
    ]
    q_fm, k_fm, v_tm = [t.__enter__() for t in ga1_cm]
    _stack.extend(ga1_cm)

    gq_cm = [
        nc.sbuf_tensor("wsl", [128, 2 * 2048], F32),
        nc.sbuf_tensor("q_tmp", [128, 2048], F32),
        nc.sbuf_tensor("q_i8", [128, 2 * 2048], I8),
        nc.sbuf_tensor("q_bf", [128, 2 * 2048], BF16),
    ]
    wsl, q_tmp, q_i8, q_bf = [t.__enter__() for t in gq_cm]
    _stack.extend(gq_cm)

    NXR = 4
    gp1_cm = [
        nc.sbuf_tensor("x_sb", [128, NXR * D], F32),
        nc.sbuf_tensor("xn_sc", [128, 2 * D], F32),
        nc.sbuf_tensor("xnbf_sc", [128, D], BF16),
        nc.sbuf_tensor("p1st", [128, 48], F32),
    ]
    x_sb, xn_sc, xnbf_sc, p1st = [t.__enter__() for t in gp1_cm]
    _stack.extend(gp1_cm)

    xnfm_cm = [nc.sbuf_tensor("xn_fm", [128, ND * T], BF16)]
    xn_fm = xnfm_cm[0].__enter__()
    _stack.extend(xnfm_cm)

    gb_cm = [
        nc.sbuf_tensor("wq_bf", [128, ND * 512], BF16),
        nc.sbuf_tensor("wk_bf", [128, ND * 512], BF16),
        nc.sbuf_tensor("wv_bf", [128, ND * 512], BF16),
    ]
    wq_bf, wk_bf, wv_bf = [t.__enter__() for t in gb_cm]
    _stack.extend(gb_cm)

    blk_cm = nc.Block()
    blk = blk_cm.__enter__()

    all_dma_sems = [dX, dS, dG] + xs + ws + x2s + osm + asm_ + gus + wds + ags

    def _finish():
        if debug and dump_reg:
            @blk.sync
            def _(x):
                for ctr in [tS, vS, sS, gS, cS] + all_dma_sems:
                    if ctr.v:
                        x.wait_ge(ctr.h, ctr.v)
                for name, ap in dump_reg.items():
                    o = nc.dram_tensor("dbg_" + name, list(ap.shape),
                                       ap.dtype, kind="ExternalOutput")
                    x.dma_start(out=o.ap(), in_=ap).then_inc(dX.h, 16)
                    dX.v += 16

        @blk.sync
        def _(x):
            for sem in all_dma_sems:
                if sem.v:
                    x.wait_ge(sem.h, sem.v)
        blk_cm.__exit__(None, None, None)
        for t in reversed(_stack):
            t.__exit__(None, None, None)
        for ctr in [tS, vS, sS, gS, cS] + all_dma_sems:
            ctr.cm.__exit__(None, None, None)
        return nc

    @blk.gpsimd
    def _(g):
        g.memset(ident_bf[:], 0.0)
        g.affine_select(out=ident_bf[:], in_=ident_bf[:],
                        compare_op=ALU.not_equal, fill=1.0,
                        base=0, pattern=[[-1, 128]], channel_multiplier=1)
        g.memset(ones_col[:], 1.0)
        g.memset(epsc[:], EPS)
        g.memset(negone[:], -1.0)
        gS.inc(g.memset(ones_row[:], 1.0))
        g.dma_start(out=wo_bf[:].rearrange("p (b c) -> p b c", b=4),
                    in_=wo_in.rearrange("(b p) c -> p b c", p=128)).then_inc(dG.h, 16)
        g.dma_start(out=mask_sb[:], in_=mask_in[:]).then_inc(dG.h, 16)
        g.dma_start(out=gate_sb[:], in_=betas_in[:]).then_inc(dG.h, 16)
        g.dma_start(out=wq_bf[:].rearrange("p (b c) -> p b c", b=8),
                    in_=wq_in.rearrange("(b p) c -> p b c", p=128)).then_inc(dG.h, 16)
        g.dma_start(out=wk_bf[:].rearrange("p (b c) -> p b c", b=8),
                    in_=wk_in.rearrange("(b p) c -> p b c", p=128)).then_inc(dG.h, 16)
        g.dma_start(out=wv_bf[:].rearrange("p (b c) -> p b c", b=8),
                    in_=wv_in.rearrange("(b p) c -> p b c", p=128)).then_inc(dG.h, 16)
        dG.v += 96
    g_setup = gS.v
    dg_setup = dG.v
    dg_wqkv = dG.v

    x_ld = [None] * NT

    @blk.sync
    def _(s):
        for tb in range(NXR):
            sem = xs[tb % NXR]
            s.dma_start(out=x_sb[:, tb * D:(tb + 1) * D],
                        in_=x_in[tb * 128:(tb + 1) * 128, :]).then_inc(sem.h, 16)
            sem.v += 16
            x_ld[tb] = (sem, sem.v)

    v_stats = [0] * NT
    for tb in range(NT):
        xap = x_sb[:, (tb % NXR) * D:(tb % NXR + 1) * D]
        xsq = xn_sc[:, (tb % 2) * D:(tb % 2 + 1) * D]

        @blk.scalar
        def _(s, xap=xap, xsq=xsq, tb=tb):
            wt(s, *x_ld[tb])
            if tb >= 2:
                wt(s, vS, v_stats[tb - 2])
            sS.inc(s.activation(xsq, xap, AF.Square,
                                accum_out=scal[:, 20 + tb % 2:21 + tb % 2]))
        s_sq = sS.v

        @blk.vector
        def _(v, xap=xap, tb=tb, s_sq=s_sq):
            wt(v, *x_ld[tb])
            v.reduce_sum(scal[:, 28:29], xap, axis=AX)
            v.drain()
            wt(v, sS, s_sq)
            v.tensor_scalar_mul(p1st[:, tb:tb + 1], scal[:, 28:29], 1.0 / D)
            v.tensor_scalar_mul(scal[:, 30:31], scal[:, 20 + tb % 2:21 + tb % 2],
                                1.0 / D)
            v.drain()
            vS.inc(v.scalar_tensor_tensor(p1st[:, 16 + tb:17 + tb],
                                          p1st[:, tb:tb + 1], p1st[:, tb:tb + 1],
                                          scal[:, 30:31], ALU.mult, ALU.subtract))
        v_stats[tb] = vS.v

        if tb + NXR < NT:
            @blk.sync
            def _(s, tb=tb):
                wt(s, vS, v_stats[tb])
                nb = tb + NXR
                sem = xs[nb % NXR]
                s.dma_start(out=x_sb[:, (nb % NXR) * D:(nb % NXR + 1) * D],
                            in_=x_in[nb * 128:(nb + 1) * 128, :]).then_inc(sem.h, 16)
                sem.v += 16
                x_ld[nb] = (sem, sem.v)

    qunits = []
    for hh in range(2):
        qunits.append((wg_sl[:, hh * 2048:(hh + 1) * 2048], 0, hh * 2048, False))
    for hh in range(2):
        qunits.append((wu_sl[:, hh * 2048:(hh + 1) * 2048], 1, 4096 + hh * 2048,
                       False))
    for hh in range(2):
        src = wd_sl[hh * 256:(hh + 1) * 256, :].rearrange("(b p) c -> p b c", p=128)
        qunits.append((src, 2, 8192 + hh * 2048, True))

    v_q1 = [0] * len(qunits)
    ds_q1 = [0] * len(qunits)

    for i, (src, m, ofs, is3d) in enumerate(qunits):
        @blk.scalar
        def _(s, src=src, i=i, is3d=is3d):
            if i >= 2:
                wt(s, vS, v_q1[i - 2])
            dst = wsl[:, (i % 2) * 2048:(i % 2 + 1) * 2048]
            if is3d:
                dst = dst.rearrange("p (b c) -> p b c", b=2)
            sem = ws[i % 2]
            s.dma_start(out=dst, in_=src).then_inc(sem.h, 16)
            sem.v += 16
            ds_q1[i] = (sem, sem.v)

        @blk.vector
        def _(v, i=i):
            wt(v, *ds_q1[i])
            vS.inc(v.reduce_sum(scal[:, i:i + 1],
                                wsl[:, (i % 2) * 2048:(i % 2 + 1) * 2048],
                                axis=AX, apply_absolute_value=True))
            v_q1[i] = vS.v
    v_asum = vS.v

    @blk.tensor
    def _(t):
        wt(t, vS, v_asum)
        wt(t, gS, g_setup)
        tS.inc(t.matmul(pb[0][:][0:6, 0:1], scal[:, 0:6], ones_col[:],
                        start=True, stop=True))
    bank_w[0] = (tS, tS.v)

    @blk.vector
    def _(v):
        wt(v, tS, tS.v)
        vS.inc(v.tensor_copy(scal[0:6, 6:7], pb[0][:][0:6, 0:1]))
    bank_r[0] = (vS, vS.v)
    v_psums = vS.v

    @blk.gpsimd
    def _(g):
        wt(g, vS, v_psums)
        g.dma_start(out=ag1_in.rearrange("r c -> c r")[0:6],
                    in_=scal[0:6, 6:7]).then_inc(dG.h, 16)
        dG.v += 16
        wt(g, dG, dG.v)
        cS.inc(g.collective_compute(
            "AllGather", ALU.bypass, replica_groups=[list(range(N_CORES))],
            ins=[ag1_in.opt()], outs=[ag1_out.opt()]))
        wt(g, cS, cS.v)
        g.dma_start(out=srow[0:1, 0:64],
                    in_=ag1_out.rearrange("a b -> (a b)")).then_inc(dG.h, 16)
        dG.v += 16
    dg_ag1 = dG.v

    @blk.vector
    def _(v):
        wt(v, dG, dg_ag1)
        for i in range(4):
            v.tensor_tensor(srow[0:1, 64 + 6 * i:70 + 6 * i],
                            srow[0:1, 16 * i:16 * i + 6],
                            srow[0:1, 16 * i + 8:16 * i + 14], ALU.add)
        v.drain()
        for i in range(2):
            v.tensor_tensor(srow[0:1, 88 + 6 * i:94 + 6 * i],
                            srow[0:1, 64 + 12 * i:70 + 12 * i],
                            srow[0:1, 70 + 12 * i:76 + 12 * i], ALU.add)
        v.drain()
        v.tensor_tensor(srow[0:1, 100:106], srow[0:1, 88:94],
                        srow[0:1, 94:100], ALU.add)
        v.drain()
        sr2 = srow[0:1, 100:106].rearrange("p (a b) -> p a b", b=2)
        v.tensor_tensor(srow[0:1, 106:109], sr2[:, :, 0:1].rearrange(
            "p a b -> p (a b)"), sr2[:, :, 1:2].rearrange("p a b -> p (a b)"),
            ALU.add)
        v.drain()
        v.tensor_scalar(srow[0:1, 110:113], srow[0:1, 106:109], 1.0 / WCNT, EPS,
                        ALU.mult, ALU.max)
        vS.inc(v.engine_nop())
    v_means = vS.v

    @blk.tensor
    def _(t):
        wt(t, vS, v_means)
        wr_bank(t, 1)
        tS.inc(t.matmul(pb[1][:][:, 0:3], ones_row[:], srow[0:1, 110:113],
                        start=True, stop=True))
    bank_w[1] = (tS, tS.v)

    @blk.vector
    def _(v):
        wt(v, tS, tS.v)
        v.tensor_copy(mbc[:, 0:3], pb[1][:][:, 0:3])
        v.drain()
        v.tensor_tensor(mbc[:, 4:5], mbc[:, 0:1], mbc[:, 1:2], ALU.mult)
        v.drain()
        v.tensor_tensor(mbc[:, 3:4], mbc[:, 4:5], mbc[:, 0:1], ALU.mult)
        v.drain()
        v.tensor_scalar_mul(kc[:, 0:1], mbc[:, 3:4], 1.0 / (127.0 ** 3))
        v.tensor_scalar_mul(kc[:, 1:2], mbc[:, 2:3], 1.0 / 127.0)
        vS.inc(v.reciprocal(scal[:, 5:8], mbc[:, 0:3]))
    bank_r[1] = (vS, vS.v)
    v_mbc = vS.v

    if stop == "setup":
        return _finish()

    v_q2 = [0] * len(qunits)
    g_q2 = [0] * len(qunits)
    ds_q2 = [0] * len(qunits)
    for i, (src, m, ofs, is3d) in enumerate(qunits):
        @blk.scalar
        def _(s, src=src, i=i, is3d=is3d):
            wt(s, vS, v_q1[len(qunits) - 1] if i < 2 else v_q2[i - 2])
            dst = wsl[:, (i % 2) * 2048:(i % 2 + 1) * 2048]
            dst3 = dst.rearrange("p (b c) -> p b c", b=2) if is3d else dst
            sem = ws[i % 2]
            s.dma_start(out=dst3, in_=src).then_inc(sem.h, 16)
            sem.v += 16
            ds_q2[i] = (sem, sem.v)

        @blk.vector
        def _(v, i=i, m=m):
            wt(v, *ds_q2[i])
            wt(v, vS, v_mbc)
            if i >= 2:
                wt(v, gS, g_q2[i - 2])
            sl = (i % 2) * 2048
            v.tensor_scalar(q_tmp[:], wsl[:, sl:sl + 2048],
                            scal[:, 5 + m:6 + m], 1.27, ALU.mult, ALU.min)
            v.drain()
            vS.inc(v.tensor_scalar_max(q_i8[:, sl:sl + 2048], q_tmp[:], -1.27))
            v_q2[i] = vS.v

        @blk.gpsimd
        def _(g, i=i, ofs=ofs):
            wt(g, vS, v_q2[i])
            gS.inc(g.tensor_copy(q_bf[:, (i % 2) * 2048:(i % 2 + 1) * 2048],
                                 q_i8[:, (i % 2) * 2048:(i % 2 + 1) * 2048]))
            g_q2[i] = gS.v
            g.dma_start(out=ag2i_v[0:1, :, ofs:ofs + 2048].rearrange(
                            "r p q -> (r p) q"),
                        in_=q_bf[:, (i % 2) * 2048:(i % 2 + 1) * 2048]
                        ).then_inc(dG.h, 16)
            dG.v += 16
    dg_q = dG.v

    @blk.gpsimd
    def _(g):
        wt(g, dG, dg_q)
        cS.inc(g.collective_compute(
            "AllGather", ALU.bypass, replica_groups=[list(range(N_CORES))],
            ins=[ag2_in.opt()], outs=[ag2_out.opt()]))
    c_tern = cS.v

    if stop == "quant":
        return _finish()

    @blk.scalar
    def _(s):
        wt(s, vS, vS.v)
        wt(s, gS, g_setup)
        s.activation(scal[:, 0:16], p1st[:, 16:32], AF.Ln, bias=epsc[:],
                     scale=-1.0)
        s.drain()
        sS.inc(s.activation(p1st[:, 32:48], scal[:, 0:16], AF.Exp, scale=-0.5))
    s_rstd_all = sS.v

    x_ld2 = [None] * NT

    @blk.sync
    def _(s):
        for tb in range(NXR):
            wt(s, vS, v_stats[NT - NXR + tb])
            sem = xs[tb % NXR]
            s.dma_start(out=x_sb[:, (tb % NXR) * D:(tb % NXR + 1) * D],
                        in_=x_in[tb * 128:(tb + 1) * 128, :]).then_inc(sem.h, 16)
            sem.v += 16
            x_ld2[tb] = (sem, sem.v)

    @blk.gpsimd
    def _(g):
        vt = v_tm[:].rearrange("p (a b c) -> p a b c", b=H, c=65)
        gS.inc(g.memset(vt[:, :, :, 64:65], 1.0))
    g_vones = gS.v

    rr = [2]

    def next_bank():
        b = rr[0]
        rr[0] = 2 + (rr[0] - 2 + 1) % 6
        return b

    s_v_evac = [0] * NT

    def emit_qk(tg):
        for (w_bf, dst) in ((wq_bf, q_fm), (wk_bf, k_fm)):
            for hb in range(NHD):
                b = next_bank()

                @blk.tensor
                def _(t, w_bf=w_bf, hb=hb, tg=tg, b=b):
                    wt(t, vS, v_xnfm[tg * 4 + 3])
                    wt(t, dG, dg_wqkv)
                    wr_bank(t, b)
                    for db in range(ND):
                        ins = t.matmul(
                            pb[b][:][:],
                            w_bf[:, db * 512 + hb * 128:db * 512 + (hb + 1) * 128],
                            xn_fm[:, db * T + tg * 512:db * T + (tg + 1) * 512],
                            start=(db == 0), stop=(db == ND - 1))
                    tS.inc(ins)
                bank_w[b] = (tS, tS.v)
                t_mm = tS.v

                @blk.scalar
                def _(s, dst=dst, hb=hb, tg=tg, b=b, t_mm=t_mm):
                    wt(s, tS, t_mm)
                    sS.inc(s.copy(dst[:, hb * T + tg * 512:hb * T + (tg + 1) * 512],
                                  pb[b][:][:]))
                bank_r[b] = (sS, sS.v)

    def emit_v(tb):
        b = next_bank()

        @blk.tensor
        def _(t, tb=tb, b=b):
            wt(t, vS, v_xnfm[tb])
            wt(t, dG, dg_wqkv)
            wr_bank(t, b)
            for db in range(ND):
                ins = t.matmul(pb[b][:][:],
                               xn_fm[:, db * T + tb * 128:db * T + (tb + 1) * 128],
                               wv_bf[:, db * 512:(db + 1) * 512],
                               start=(db == 0), stop=(db == ND - 1))
            tS.inc(ins)
        bank_w[b] = (tS, tS.v)
        t_mm = tS.v

        @blk.vector
        def _(v, tb=tb, b=b, t_mm=t_mm):
            wt(v, tS, t_mm)
            wt(v, gS, g_vones)
            dst = v_tm[:, tb * 520:(tb + 1) * 520].rearrange(
                "p (a b) -> p a b", b=65)[:, :, 0:64]
            src = pb[b][:][:, 0:512].rearrange("p (a b) -> p a b", b=64)
            vS.inc(v.tensor_copy(dst, src))
        bank_r[b] = (vS, vS.v)
        s_v_evac[tb] = (vS, vS.v)

    v_xnfm = [0] * NT
    t_xntr = [0] * NT
    s_xnq_done = [0] * NT
    for tb in range(NT):
        xap = x_sb[:, (tb % NXR) * D:(tb % NXR + 1) * D]
        xn = xn_sc[:, (tb % 2) * D:(tb % 2 + 1) * D]

        @blk.vector
        def _(v, xap=xap, xn=xn, tb=tb):
            wt(v, *x_ld2[tb])
            wt(v, sS, s_rstd_all)
            if tb >= 2:
                wt(v, vS, v_xnfm[tb - 2])
            if tb >= 1:
                wt(v, sS, s_xnq_done[tb - 1])
            v.tensor_scalar(xn[:], xap, p1st[:, tb:tb + 1], p1st[:, 32 + tb:33 + tb],
                            ALU.subtract, ALU.mult)
            v.drain()
            v.reduce_max(scal[:, 28:29], xn[:], axis=AX, apply_absolute_value=True)
            v.drain()
            v.tensor_scalar_max(mxc_all[:, tb:tb + 1], scal[:, 28:29], EPS)
            v.drain()
            v.reciprocal(scal[:, 26 + tb % 2:27 + tb % 2], mxc_all[:, tb:tb + 1])
            v.drain()
            v.tensor_scalar_mul(scal[:, 24 + tb % 2:25 + tb % 2],
                                scal[:, 26 + tb % 2:27 + tb % 2], 127.0)
            vS.inc(v.engine_nop())
        v_xnq = vS.v

        @blk.scalar
        def _(s, xn=xn, tb=tb, v_xnq=v_xnq):
            wt(s, vS, v_xnq)
            if tb >= 1:
                wt(s, tS, t_xntr[tb - 1])
            s.activation(xq_tm[:, tb * D:(tb + 1) * D], xn[:], AF.Identity,
                         scale=scal[:, 24 + tb % 2:25 + tb % 2])
            sS.inc(s.copy(xnbf_sc[:], xn[:]))
        s_xnq_done[tb] = sS.v
        v_xnq = (sS, s_xnq_done[tb])

        if tb + NXR < NT:
            @blk.sync
            def _(s, tb=tb):
                wt(s, vS, vS.v)
                nb = tb + NXR
                sem = xs[nb % NXR]
                s.dma_start(out=x_sb[:, (nb % NXR) * D:(nb % NXR + 1) * D],
                            in_=x_in[nb * 128:(nb + 1) * 128, :]).then_inc(sem.h, 16)
                sem.v += 16
                x_ld2[nb] = (sem, sem.v)

        for w in range(2):
            b = w

            @blk.tensor
            def _(t, w=w, b=b, tb=tb):
                wt(t, sS, s_xnq_done[tb])
                wr_bank(t, b)
                for j in range(4):
                    ins = t.transpose(pbb[b][:, j * 128:(j + 1) * 128],
                                      xnbf_sc[:, (w * 4 + j) * 128:(w * 4 + j + 1) * 128],
                                      ident_bf[:])
                tS.inc(ins)
            bank_w[b] = (tS, tS.v)
            t_tr = tS.v
            if w == 1:
                t_xntr[tb] = tS.v

            @blk.vector
            def _(v, w=w, b=b, tb=tb, t_tr=t_tr):
                wt(v, tS, t_tr)
                dst = xn_fm[:, w * 4 * T:(w * 4 + 4) * T].rearrange(
                    "p (a b) -> p a b", b=T)[:, :, tb * 128:(tb + 1) * 128]
                src = pbb[b][:, 0:512].rearrange("p (a b) -> p a b", b=128)
                vS.inc(v.tensor_copy(dst, src))
            bank_r[b] = (vS, vS.v)
            if w == 1:
                v_xnfm[tb] = vS.v
        v_xnfm_all = vS.v
        if tb % 4 == 3:
            emit_qk(tb // 4)
            for vtb in range(tb - 3, tb + 1):
                emit_v(vtb)

    dump_reg.update(xq_tm=xq_tm[:], mxc=mxc_all[:], xn_fm=xn_fm[:],
                    x_sb=x_sb[:], xn_sc=xn_sc[:], scal=scal[:])
    if stop == "p1":
        return _finish()

    @blk.scalar
    def _(s):
        wt(s, dG, dg_setup)
        sS.inc(s.activation(stage[:, 0:HD], gate_sb[:], AF.Exp))
    s_gexp = sS.v

    @blk.vector
    def _(v):
        wt(v, sS, s_gexp)
        v.tensor_scalar_add(stage[:, HD:2 * HD], stage[:, 0:HD], 1.0)
        v.drain()
        v.reciprocal(stage[:, 3 * HD:4 * HD], stage[:, HD:2 * HD])
        v.drain()
        v.tensor_tensor(gate_sb[:], stage[:, 0:HD], stage[:, 3 * HD:4 * HD],
                        ALU.mult)
        v.drain()
        v.drain()
        vS.inc(v.tensor_scalar(gate1m[:], gate_sb[:], -1.0, 1.0,
                               ALU.mult, ALU.add))
    v_gate = vS.v

    if stop == "gate":
        return _finish()

    s_qk_done = sS.v
    s_v_done = (vS, vS.v)

    dump_reg.update(q_fm=q_fm[:], k_fm=k_fm[:], v_tm=v_tm[:],
                    gate=gate_sb[:], mbc=mbc[:])
    if stop == "p2":
        return _finish()

    for t in reversed(gb_cm):
        t.__exit__(None, None, None)
        _stack.remove(t)
    xnfm_cm[0].__exit__(None, None, None)
    _stack.remove(xnfm_cm[0])
    for t in reversed(gp1_cm):
        t.__exit__(None, None, None)
        _stack.remove(t)
    for t in reversed(gq_cm):
        t.__exit__(None, None, None)
        _stack.remove(t)

    ga2_cm = [
        nc.sbuf_tensor("sk_fm", [128, 2 * 1024], BF16),
        nc.sbuf_tensor("sk_tm", [128, 2 * 1024], BF16),
        nc.sbuf_tensor("att_tm", [128, NT * HD], BF16),
        nc.sbuf_tensor("e_sb", [128, 16 * 256], BF16),
        nc.sbuf_tensor("eT_sb", [128, 16 * 256], BF16),
        nc.sbuf_tensor("elu_sc", [128, 12 * 256], BF16),
        nc.sbuf_tensor("ssum", [128, 16], F32),
        nc.sbuf_tensor("rsum", [128, 16], F32),
        nc.sbuf_tensor("memz", [128, H * 9 * 65], F32),
        nc.sbuf_tensor("memz_bf", [128, NCH * 520], BF16),
        nc.sbuf_tensor("corr", [128, 2 * 520], F32),
        nc.sbuf_tensor("agt", [128, 2 * 520], F32),
        nc.sbuf_tensor("att_fm", [128, NHD * T], BF16),
        nc.sbuf_tensor("mix_sc", [128, 3 * HD], F32),
        nc.sbuf_tensor("x2_sb", [128, 2 * D], F32),
    ]
    (sk_fm, sk_tm, att_tm, e_sb, eT_sb, elu_sc, ssum, rsum, memz, memz_bf,
     corr, agt, att_fm, mix_sc, x2_sb) = [t.__enter__() for t in ga2_cm]
    _stack.extend(ga2_cm)


    @blk.vector
    def _(v):
        wt(v, *s_v_done)
        mz = memz[:].rearrange("p (a b c) -> p a b c", b=9, c=65)
        vS.inc(v.memset(mz[0:64, :, 0:1, :], 0.0))
    v_mz0 = vS.v

    v_sk = [0] * NCH
    v_snap = [0] * (NCH + 1)
    v_snap[0] = v_mz0
    t_skTv = [0] * NCH
    t_sk_tr = [0] * NCH
    s_kexp = [0] * NCH
    for c in range(NCH):
        @blk.vector
        def _(v, c=c):
            wt(v, *s_v_done)
            if c >= 1:
                wt(v, sS, s_kexp[c - 1])
            for hb in range(NHD):
                col = hb * T + c * 256
                v.tensor_single_scalar(elu_sc[:, hb * 256:(hb + 1) * 256],
                                       k_fm[:, col:col + 256], 0.0, ALU.min)
            vS.inc(v.engine_nop())
        v_min = vS.v

        @blk.scalar
        def _(s, v_min=v_min):
            wt(s, vS, v_min)
            for hb in range(NHD):
                ins = s.activation(elu_sc[:, 1024 + hb * 256:1024 + (hb + 1) * 256],
                                   elu_sc[:, hb * 256:(hb + 1) * 256], AF.Exp)
            sS.inc(ins)
        s_exp = sS.v
        s_kexp[c] = s_exp

        @blk.vector
        def _(v, c=c, s_exp=s_exp):
            wt(v, sS, s_exp)
            if c >= 2:
                wt(v, tS, t_sk_tr[c - 2])
            for hb in range(NHD):
                col = hb * T + c * 256
                v.tensor_single_scalar(elu_sc[:, 2048 + hb * 256:2048 + (hb + 1) * 256],
                                       k_fm[:, col:col + 256], 0.0, ALU.max)
            v.drain()
            for hb in range(NHD):
                scol2 = (c % 2) * 1024 + hb * 256
                v.tensor_tensor(sk_fm[:, scol2:scol2 + 256],
                                elu_sc[:, 1024 + hb * 256:1024 + (hb + 1) * 256],
                                elu_sc[:, 2048 + hb * 256:2048 + (hb + 1) * 256],
                                ALU.add)
            vS.inc(v.engine_nop())
        v_sk[c] = vS.v

        for half in range(2):
            b = 6 + half

            @blk.tensor
            def _(t, c=c, half=half, b=b):
                wt(t, vS, v_sk[c])
                wr_bank(t, b)
                for hp in range(NHD):
                    col = (c % 2) * 1024 + hp * 256 + half * 128
                    ins = t.transpose(pbb[b][:, hp * 128:(hp + 1) * 128],
                                      sk_fm[:, col:col + 128], ident_bf[:])
                tS.inc(ins)
            bank_w[b] = (tS, tS.v)
            t_tr = tS.v
            t_sk_tr[c] = tS.v

            @blk.vector
            def _(v, c=c, half=half, b=b, t_tr=t_tr):
                wt(v, tS, t_tr)
                if c >= 2:
                    wt(v, tS, t_skTv[c - 2])
                vS.inc(v.tensor_copy(
                    sk_tm[:, (c % 2) * 1024 + half * 512:(c % 2) * 1024 + (half + 1) * 512],
                    pbb[b][:, 0:512]))
            bank_r[b] = (vS, vS.v)
        v_sktm = vS.v

        for gix in range(2):
            b = 4 + gix

            @blk.tensor
            def _(t, c=c, gix=gix, b=b, v_sktm=v_sktm):
                wt(t, vS, v_sktm)
                wt(t, *s_v_evac[2 * c + 1])
                wr_bank(t, b)
                for hh in range(4):
                    h = gix * 4 + hh
                    for half in range(2):
                        lcol = (c % 2) * 1024 + half * 512 + (h // 2) * 128 + (h % 2) * 64
                        ins = t.matmul(
                            pb[b][:][0:64, hh * 72:hh * 72 + 65],
                            sk_tm[:, lcol:lcol + 64],
                            v_tm[:, (2 * c + half) * 520 + h * 65:
                                 (2 * c + half) * 520 + (h + 1) * 65],
                            start=(half == 0), stop=(half == 1))
                tS.inc(ins)
            bank_w[b] = (tS, tS.v)
            t_ct = tS.v
            t_skTv[c] = tS.v

            @blk.vector
            def _(v, c=c, gix=gix, b=b, t_ct=t_ct):
                wt(v, tS, t_ct)
                wt(v, vS, v_snap[c])
                mz = memz[0:64, :].rearrange("p (a b) -> p a b", b=9 * 65)
                dst = mz[:, gix * 4:(gix + 1) * 4, (c + 1) * 65:(c + 2) * 65]
                src0 = mz[:, gix * 4:(gix + 1) * 4, c * 65:(c + 1) * 65]
                src1 = pb[b][:][0:64, 0:288].rearrange(
                    "p (a b) -> p a b", b=72)[:, :, 0:65]
                vS.inc(v.tensor_tensor(dst, src0, src1, ALU.add))
            bank_r[b] = (vS, vS.v)
        v_snap[c + 1] = vS.v

    @blk.gpsimd
    def _(g):
        wt(g, vS, v_snap[NCH])
        mz = memz[0:64, :].rearrange("p (a b) -> p a b", b=9 * 65)
        g.dma_start(out=ag3_in.rearrange("r (p q) -> r p q", p=64)[0:1].rearrange(
                        "r p q -> (r p) q").rearrange("p (a b) -> p a b", b=65),
                    in_=mz[:, 0:8, 8 * 65:9 * 65]).then_inc(dG.h, 16)
        dG.v += 16
        wt(g, dG, dG.v)
        cS.inc(g.collective_compute(
            "AllGather", ALU.bypass, replica_groups=[list(range(N_CORES))],
            ins=[ag3_in.opt()], outs=[ag3_out.opt()]))
    c_mem = cS.v

    dump_reg.pop("xn_fm", None)
    dump_reg.update(memz=memz[:], sk_fm=sk_fm[:])
    if stop == "p3a":
        return _finish()

    s_att_evac = [0] * NT
    s_qexp = [0] * NCH
    p3b_lim = NCH
    if stop.startswith("p3b:"):
        p3b_lim = int(stop.split(":")[1])

    def p3_sbank(g, h, qh):
        return 4 * g + qh * 2 + (h % 2)

    def p3_scol(h):
        return ((h % 4) // 2) * 256

    grp_state = {}
    for c in range(NCH):
        if c >= p3b_lim:
            continue
        for g in range(2):
            heads = range(4 * g, 4 * g + 4)

            @blk.tensor
            def _(t, c=c, g=g, heads=heads):
                wt(t, sS, s_qk_done)
                for h in heads:
                    for qh in range(2):
                        wr_bank(t, p3_sbank(g, h, qh))
                for h in heads:
                    for qh in range(2):
                        pr = (h % 2) * 64
                        qcol = (h // 2) * T + c * 256 + qh * 128
                        kcol = (h // 2) * T + c * 256
                        ins = t.matmul(
                            pb[p3_sbank(g, h, qh)][:][:, p3_scol(h):p3_scol(h) + 256],
                            q_fm[pr:pr + 64, qcol:qcol + 128],
                            k_fm[pr:pr + 64, kcol:kcol + 256],
                            start=True, stop=True)
                tS.inc(ins)
            for h in heads:
                for qh in range(2):
                    bank_w[p3_sbank(g, h, qh)] = (tS, tS.v)
            t_sc = tS.v

            @blk.vector
            def _(v, c=c, g=g, heads=heads, t_sc=t_sc):
                wt(v, tS, t_sc)
                for h in heads:
                    for qh in range(2):
                        i = h * 2 + qh
                        v.reduce_max(scal[:, i:i + 1],
                                     pb[p3_sbank(g, h, qh)][:][:, p3_scol(h):p3_scol(h) + 256],
                                     axis=AX)
                v.drain()
                for h in heads:
                    for qh in range(2):
                        i = h * 2 + qh
                        v.tensor_scalar_mul(scal[:, 16 + i:17 + i],
                                            scal[:, i:i + 1], -0.125)
                vS.inc(v.engine_nop())
            v_mx = vS.v

            @blk.scalar
            def _(s, c=c, g=g, heads=heads, v_mx=v_mx):
                wt(s, vS, v_mx)
                for h in heads:
                    for qh in range(2):
                        i = h * 2 + qh
                        ins = s.activation(
                            e_sb[:, i * 256:(i + 1) * 256],
                            pb[p3_sbank(g, h, qh)][:][:, p3_scol(h):p3_scol(h) + 256],
                            AF.Exp, bias=scal[:, 16 + i:17 + i], scale=0.125,
                            accum_out=ssum[:, i:i + 1])
                sS.inc(ins)
            for h in heads:
                for qh in range(2):
                    bank_r[p3_sbank(g, h, qh)] = (sS, sS.v)
            s_exp = sS.v

            @blk.vector
            def _(v, g=g, s_exp=s_exp):
                wt(v, sS, s_exp)
                vS.inc(v.reciprocal(rsum[:, 8 * g:8 * g + 8], ssum[:, 8 * g:8 * g + 8]))
            v_rs = vS.v

            @blk.tensor
            def _(t, g=g, s_exp=s_exp):
                wt(t, sS, s_exp)
                wr_bank(t, 4 * g)
                wr_bank(t, 4 * g + 2)
                for k in range(16):
                    i = 8 * g + k // 2
                    j = k % 2
                    b = 4 * g + 2 * (k // 8)
                    ins = t.transpose(pbb[b][:, (k % 8) * 128:(k % 8 + 1) * 128],
                                      e_sb[:, i * 256 + j * 128:i * 256 + (j + 1) * 128],
                                      ident_bf[:])
                tS.inc(ins)
            bank_w[4 * g] = (tS, tS.v)
            bank_w[4 * g + 2] = (tS, tS.v)
            t_eT = tS.v

            @blk.vector
            def _(v, g=g, t_eT=t_eT):
                wt(v, tS, t_eT)
                v.tensor_copy(eT_sb[:, g * 2048:g * 2048 + 1024],
                              pbb[4 * g][:, 0:1024])
                vS.inc(v.tensor_copy(eT_sb[:, g * 2048 + 1024:g * 2048 + 2048],
                                     pbb[4 * g + 2][:, 0:1024]))
            bank_r[4 * g] = (vS, vS.v)
            bank_r[4 * g + 2] = (vS, vS.v)
            v_eT = vS.v

            @blk.tensor
            def _(t, c=c, g=g, heads=heads, v_eT=v_eT):
                wt(t, vS, v_eT)
                wt(t, *s_v_evac[2 * c + 1])
                wr_bank(t, 4 * g + 1)
                wr_bank(t, 4 * g + 3)
                for h in heads:
                    for qh in range(2):
                        i = h * 2 + qh
                        k0 = (i - 8 * g) * 2
                        for kb in range(2):
                            k = k0 + kb
                            col = g * 2048 + (k // 8) * 1024 + (k % 8) * 128
                            ins = t.matmul(
                                pb[4 * g + 1 + 2 * qh][:][:, (h - 4 * g) * 64:(h - 4 * g + 1) * 64],
                                eT_sb[:, col:col + 128],
                                v_tm[:, (2 * c + kb) * 520 + h * 65:
                                     (2 * c + kb) * 520 + h * 65 + 64],
                                start=(kb == 0), stop=(kb == 1))
                tS.inc(ins)
            bank_w[4 * g + 1] = (tS, tS.v)
            bank_w[4 * g + 3] = (tS, tS.v)
            t_av = tS.v
            grp_state[g] = t_av

            @blk.scalar
            def _(s, c=c, g=g, heads=heads, t_av=t_av):
                wt(s, tS, t_av)
                for qh in range(2):
                    tb = 2 * c + qh
                    for h in heads:
                        ins = s.activation(
                            att_tm[:, tb * HD + h * 64:tb * HD + (h + 1) * 64],
                            pb[4 * g + 1 + 2 * qh][:][:, (h - 4 * g) * 64:(h - 4 * g + 1) * 64],
                            AF.Identity,
                            scale=rsum[:, h * 2 + qh:h * 2 + qh + 1])
                sS.inc(ins)
            bank_r[4 * g + 1] = (sS, sS.v)
            bank_r[4 * g + 3] = (sS, sS.v)
            if g == 1:
                s_att_evac[2 * c] = sS.v
                s_att_evac[2 * c + 1] = sS.v
        t_av = tS.v

        @blk.vector
        def _(v, c=c, t_av=t_av):
            wt(v, tS, t_av)
            if c >= 1:
                wt(v, sS, s_qexp[c - 1])
            for hb in range(NHD):
                col = hb * T + c * 256
                v.tensor_single_scalar(elu_sc[:, hb * 256:(hb + 1) * 256],
                                       q_fm[:, col:col + 256], 0.0, ALU.min)
            vS.inc(v.engine_nop())
        v_min = vS.v

        @blk.scalar
        def _(s, v_min=v_min):
            wt(s, vS, v_min)
            for hb in range(NHD):
                ins = s.activation(elu_sc[:, 1024 + hb * 256:1024 + (hb + 1) * 256],
                                   elu_sc[:, hb * 256:(hb + 1) * 256], AF.Exp)
            sS.inc(ins)
        s_eexp = sS.v
        s_qexp[c] = s_eexp

        @blk.vector
        def _(v, c=c, s_eexp=s_eexp):
            wt(v, sS, s_eexp)
            for hb in range(NHD):
                col = hb * T + c * 256
                v.tensor_single_scalar(elu_sc[:, 2048 + hb * 256:2048 + (hb + 1) * 256],
                                       q_fm[:, col:col + 256], 0.0, ALU.max)
            v.drain()
            for hb in range(NHD):
                col = hb * T + c * 256
                v.tensor_tensor(q_fm[:, col:col + 256],
                                elu_sc[:, 1024 + hb * 256:1024 + (hb + 1) * 256],
                                elu_sc[:, 2048 + hb * 256:2048 + (hb + 1) * 256],
                                ALU.add)
            vS.inc(v.engine_nop())
    v_sq_done = vS.v

    dump_reg.update(att_tm=att_tm[:], sq_fm=q_fm[:], ssum=ssum[:],
                    rsum=rsum[:], e_sb=e_sb[:])
    if stop == "p3b" or stop.startswith("p3b:"):
        return _finish()

    dg_ag3 = [None] * N_CORES
    v_stt = [0] * N_CORES

    @blk.vector
    def _(v):
        v.memset(corr[0:64, 0:520], 0.0)
        v.drain()
        ca = corr[0:64, 0:520].rearrange("p (a b) -> p a b", b=65)
        v.memset(ca[:, :, 64:65], 1.0 / 64.0)
        vS.inc(v.drain())
    v_corr0 = vS.v

    for cc in range(N_CORES):
        @blk.gpsimd
        def _(g, cc=cc):
            if cc == 0:
                wt(g, cS, c_mem)
            if cc >= 2:
                wt(g, vS, v_stt[cc - 2])
            sem = ags[cc % 2]
            g.dma_start(out=agt[0:64, (cc % 2) * 520:(cc % 2 + 1) * 520],
                        in_=ag3_v[cc:cc + 1].rearrange("r p q -> (r p) q")
                        ).then_inc(sem.h, 16)
            sem.v += 16
            dg_ag3[cc] = (sem, sem.v)

        @blk.vector
        def _(v, cc=cc):
            wt(v, *dg_ag3[cc])
            src = (cc % 2) * 520
            d = ((cc + 1) % 2) * 520
            v.scalar_tensor_tensor(
                corr[0:64, d:d + 520], agt[0:64, src:src + 520],
                mask_sb[0:64, cc:cc + 1], corr[0:64, (cc % 2) * 520:(cc % 2) * 520 + 520],
                ALU.mult, ALU.add)
            vS.inc(v.drain())
            v_stt[cc] = vS.v

    @blk.vector
    def _(v):
        wt(v, vS, v_snap[NCH])
        mz = memz[0:64, :].rearrange("p (a b) -> p a b", b=9 * 65)
        for c in range(NCH):
            v.tensor_tensor(
                memz_bf[0:64, c * 520:(c + 1) * 520].rearrange(
                    "p (a b) -> p a b", b=65),
                mz[:, 0:8, c * 65:(c + 1) * 65],
                corr[0:64, 0:520].rearrange("p (a b) -> p a b", b=65),
                ALU.add)
        vS.inc(v.engine_nop())
    v_corr = vS.v

    @blk.sync
    def _(s):
        wt(s, vS, v_corr)
        s.dma_start(out=memz_bf[64:128, :], in_=memz_bf[0:64, :]).then_inc(dX.h, 16)
        dX.v += 16
    dx_dup = dX.v

    dump_reg.update(memz_bf=memz_bf[:], corr=corr[:])
    if stop == "p5a":
        return _finish()

    x2_ld = [None] * NT

    @blk.sync
    def _(s):
        for tb in range(2):
            sem = x2s[tb % 2]
            s.dma_start(out=x2_sb[:, tb * D:(tb + 1) * D],
                        in_=x_in[tb * 128:(tb + 1) * 128, :]).then_inc(sem.h, 16)
            sem.v += 16
            x2_ld[tb] = (sem, sem.v)

    dx_out = [None] * NT
    p5b_lim = NT
    p5b_step = "st"
    if stop.startswith("p5b:"):
        parts = stop.split(":")
        p5b_lim = int(parts[1])
        p5b_step = parts[2] if len(parts) > 2 else "st"
    for c in range(NCH):
        for qh in range(2):
            tb = 2 * c + qh
            if tb >= p5b_lim:
                continue
            last = (tb == p5b_lim - 1) and stop.startswith("p5b:")
            mb0 = 0
            for par in range(2):
                b = mb0 + par

                @blk.tensor
                def _(t, c=c, qh=qh, par=par, b=b):
                    wt(t, vS, v_sq_done)
                    wt(t, dX, dx_dup)
                    wr_bank(t, b)
                    pr = par * 64
                    for hh in range(4):
                        h = 2 * hh + par
                        qcol = (h // 2) * T + c * 256 + qh * 128
                        ins = t.matmul(
                            pb[b][:][:, hh * 72:hh * 72 + 65],
                            q_fm[pr:pr + 64, qcol:qcol + 128],
                            memz_bf[pr:pr + 64, c * 520 + h * 65:c * 520 + (h + 1) * 65],
                            start=True, stop=True)
                    tS.inc(ins)
                bank_w[b] = (tS, tS.v)
            t_mem = tS.v
            if last and p5b_step.startswith("mem"):
                continue

            @blk.vector
            def _(v, tb=tb, mb0=mb0, t_mem=t_mem):
                wt(v, tS, t_mem)
                for par in range(2):
                    b = mb0 + par
                    for hh in range(4):
                        h = 2 * hh + par
                        v.reciprocal(scal[:, 21 + h:22 + h],
                                     pb[b][:][:, hh * 72 + 64:hh * 72 + 65])
                v.drain()
                for par in range(2):
                    b = mb0 + par
                    for hh in range(4):
                        h = 2 * hh + par
                        v.tensor_scalar_mul(mix_sc[:, h * 64:(h + 1) * 64],
                                            pb[b][:][:, hh * 72:hh * 72 + 64],
                                            scal[:, 21 + h:22 + h])
                wt(v, sS, s_att_evac[tb])
                wt(v, vS, v_gate)
                v.tensor_tensor(mix_sc[:, HD:2 * HD],
                                att_tm[:, tb * HD:(tb + 1) * HD], gate1m[:],
                                ALU.mult)
                v.tensor_tensor(mix_sc[:, 2 * HD:3 * HD], mix_sc[:, 0:HD],
                                gate_sb[:], ALU.mult)
                v.drain()
                vS.inc(v.tensor_tensor(att_tm[:, tb * HD:(tb + 1) * HD],
                                       mix_sc[:, HD:2 * HD],
                                       mix_sc[:, 2 * HD:3 * HD], ALU.add))
            bank_r[mb0] = (vS, vS.v)
            bank_r[mb0 + 1] = (vS, vS.v)
            v_mix = vS.v
            if last and p5b_step == "mix":
                continue

            b = 4 + (tb % 2)

            @blk.tensor
            def _(t, tb=tb, b=b, v_mix=v_mix):
                wt(t, vS, v_mix)
                wr_bank(t, b)
                for j in range(NHD):
                    ins = t.transpose(pbb[b][:, j * 128:(j + 1) * 128],
                                      att_tm[:, tb * HD + j * 128:tb * HD + (j + 1) * 128],
                                      ident_bf[:])
                tS.inc(ins)
            bank_w[b] = (tS, tS.v)
            t_atr = tS.v

            @blk.vector
            def _(v, tb=tb, b=b, t_atr=t_atr):
                wt(v, tS, t_atr)
                dst = att_fm[:, 0:4 * T].rearrange(
                    "p (a b) -> p a b", b=T)[:, :, tb * 128:(tb + 1) * 128]
                src = pbb[b][:, 0:512].rearrange("p (a b) -> p a b", b=128)
                vS.inc(v.tensor_copy(dst, src))
            bank_r[b] = (vS, vS.v)
            v_afm = vS.v
            if last and p5b_step == "tr":
                continue

            wob = (6, 7) if tb % 2 == 0 else (2, 3)

            @blk.tensor
            def _(t, tb=tb, wob=wob, v_afm=v_afm):
                wt(t, vS, v_afm)
                wt(t, dG, dg_setup)
                wr_bank(t, wob[0])
                wr_bank(t, wob[1])
                for nb in range(2):
                    for hb in range(NHD):
                        ins = t.matmul(
                            pb[wob[nb]][:],
                            att_fm[:, hb * T + tb * 128:hb * T + (tb + 1) * 128],
                            wo_bf[:, hb * 1024 + nb * 512:hb * 1024 + (nb + 1) * 512],
                            start=(hb == 0), stop=(hb == NHD - 1))
                tS.inc(ins)
            bank_w[wob[0]] = (tS, tS.v)
            bank_w[wob[1]] = (tS, tS.v)
            t_wo = tS.v
            if last and p5b_step == "wo":
                continue

            @blk.vector
            def _(v, tb=tb, wob=wob, t_wo=t_wo):
                wt(v, tS, t_wo)
                wt(v, *x2_ld[tb])
                if tb >= 2:
                    wt(v, *dx_out[tb - 2])
                sl = (tb % 2) * D
                v.tensor_tensor(stage[:, sl:sl + 512], pb[wob[0]][:],
                                x2_sb[:, sl:sl + 512], ALU.add)
                vS.inc(v.tensor_tensor(stage[:, sl + 512:sl + D], pb[wob[1]][:],
                                       x2_sb[:, sl + 512:sl + D], ALU.add))
            bank_r[wob[0]] = (vS, vS.v)
            bank_r[wob[1]] = (vS, vS.v)
            v_stage = vS.v

            @blk.sync
            def _(s, tb=tb, v_stage=v_stage):
                wt(s, vS, v_stage)
                sl = (tb % 2) * D
                sem = osm[tb % 2]
                s.dma_start(out=out[tb * 128:(tb + 1) * 128, :],
                            in_=stage[:, sl:sl + D]).then_inc(sem.h, 16)
                sem.v += 16
                dx_out[tb] = (sem, sem.v)
                if tb + 2 < NT:
                    nb = tb + 2
                    sem2 = x2s[nb % 2]
                    s.dma_start(out=x2_sb[:, (nb % 2) * D:(nb % 2 + 1) * D],
                                in_=x_in[nb * 128:(nb + 1) * 128, :]).then_inc(sem2.h, 16)
                    sem2.v += 16
                    x2_ld[nb] = (sem2, sem2.v)
    dx_attn_out = [(osm[0], osm[0].v), (osm[1], osm[1].v),
                   (x2s[0], x2s[0].v), (x2s[1], x2s[1].v)]

    def wt_attn_out(eng):
        for sem, vv in dx_attn_out:
            if vv:
                eng.wait_ge(sem.h, vv)

    dump_reg.update(att_mix=att_tm[:])
    if stop == "p5b" or stop.startswith("p5b:"):
        return _finish()

    for t in reversed(ga2_cm):
        t.__exit__(None, None, None)
        _stack.remove(t)
    for t in reversed(ga1_cm):
        t.__exit__(None, None, None)
        _stack.remove(t)

    v_bar = vS.v
    s_bar = sS.v
    t_bar = tS.v

    p4_cm = [
        nc.sbuf_tensor("h_hg", [128, HG * I_FFN], F32),
        nc.sbuf_tensor("hq_i8", [128, I_FFN], I8),
        nc.sbuf_tensor("hq_bf", [128, I_FFN], BF16),
        nc.sbuf_tensor("hq_fm", [128, 2 * I_FFN], BF16),
        nc.sbuf_tensor("xq_fm", [128, HG * ND * 128], BF16),
        nc.sbuf_tensor("xqbf_t", [128, D], BF16),
        nc.sbuf_tensor("wgu_buf", [128, 4 * I_FFN], BF16),
        nc.sbuf_tensor("wd_buf", [128, 2 * 4 * 1024], BF16),
        nc.sbuf_tensor("hsc", [128, 4 * 512], F32),
        nc.sbuf_tensor("hsc2", [128, 512], F32),
        nc.sbuf_tensor("sfin", [128, HG], F32),
        nc.sbuf_tensor("p4mx", [128, HG], F32),
    ]
    (h_hg, hq_i8, hq_bf, hq_fm, xq_fm, xqbf_t, wgu_buf, wd_buf, hsc, hsc2,
     sfin, p4mx) = [t.__enter__() for t in p4_cm]
    _stack.extend(p4_cm)

    NOC = 8
    ds_slab = {}
    t_guslab_done = {}
    hq_state = {"t_hqbf_readers": 0, "t_down_prev": 0}
    ds_wd = {}
    t_wd_done = {}
    dg_out = {}

    def emit_gu_dma(hg, oc, mat):
        sl = ((oc % 2) * 2 + mat) * I_FFN

        @blk.scalar
        def _(s, hg=hg, oc=oc, mat=mat, sl=sl):
            wt(s, cS, c_tern)
            wt(s, vS, v_bar)
            wt_attn_out(s)
            key = (hg, oc, mat)
            gi = hg * NOC + oc
            if gi >= 2:
                pkey = ((gi - 2) // NOC, (gi - 2) % NOC, mat)
                assert pkey in t_guslab_done, pkey
                wt(s, tS, t_guslab_done[pkey])
            src = ag2_v[0:8, :, mat * I_FFN + oc * 512:mat * I_FFN + (oc + 1) * 512
                        ].rearrange("b p c -> p b c")
            dst = wgu_buf[:, sl:sl + I_FFN].rearrange("p (b c) -> p b c", b=8)
            sem = gus[(oc % 2) * 2 + mat] if len(gus) == 4 else gus[oc % 2]
            s.dma_start(out=dst, in_=src).then_inc(sem.h, 16)
            sem.v += 16
            ds_slab[key] = (sem, sem.v)

    v_hevac = {}
    v_h_done = {}

    for hg in range(NHG):
        v_xqfm = {}
        for ti in range(HG):
            tb = hg * HG + ti

            @blk.vector
            def _(v, tb=tb):
                wt(v, vS, v_bar)
                if hg > 0 or ti > 0:
                    wt(v, tS, hq_state["t_hqbf_readers"] or t_bar)
                vS.inc(v.tensor_copy(xqbf_t[:], xq_tm[:, tb * D:(tb + 1) * D]))
            v_cast = vS.v

            for w in range(2):
                b = w

                @blk.tensor
                def _(t, ti=ti, w=w, b=b, v_cast=v_cast):
                    wt(t, vS, v_cast)
                    wt(t, sS, s_bar)
                    wr_bank(t, b)
                    for j in range(4):
                        ins = t.transpose(pbb[b][:, j * 128:(j + 1) * 128],
                                          xqbf_t[:, (w * 4 + j) * 128:(w * 4 + j + 1) * 128],
                                          ident_bf[:])
                    tS.inc(ins)
                bank_w[b] = (tS, tS.v)
                t_tr = tS.v

                @blk.vector
                def _(v, ti=ti, w=w, b=b, t_tr=t_tr):
                    wt(v, tS, t_tr)
                    dst = xq_fm[:, (ti * ND + w * 4) * 128:(ti * ND + w * 4 + 4) * 128]
                    vS.inc(v.tensor_copy(dst, pbb[b][:, 0:512]))
                bank_r[b] = (vS, vS.v)
            v_xqfm[ti] = vS.v
            hq_state["t_hqbf_readers"] = tS.v

        emit_gu_dma(hg, 0, 0)
        emit_gu_dma(hg, 0, 1)
        for oc in range(NOC):
            if oc + 1 < NOC:
                emit_gu_dma(hg, oc + 1, 0)
                emit_gu_dma(hg, oc + 1, 1)
            for mat in range(2):
                sl = ((oc % 2) * 2 + mat) * I_FFN
                for ti in range(HG):
                    b = mat * 4 + ti

                    @blk.tensor
                    def _(t, hg=hg, oc=oc, mat=mat, ti=ti, b=b, sl=sl):
                        wt(t, vS, v_xqfm[ti])
                        wt(t, *ds_slab[(hg, oc, mat)])
                        wr_bank(t, b)
                        for db in range(ND):
                            ins = t.matmul(
                                pb[b][:][:],
                                xq_fm[:, (ti * ND + db) * 128:(ti * ND + db + 1) * 128],
                                wgu_buf[:, sl + db * 512:sl + (db + 1) * 512],
                                start=(db == 0), stop=(db == ND - 1))
                        tS.inc(ins)
                    bank_w[b] = (tS, tS.v)
                    if ti == HG - 1:
                        t_guslab_done[(hg, oc, mat)] = tS.v

            for ti in range(HG):
                @blk.vector
                def _(v, hg=hg, oc=oc, ti=ti):
                    wt(v, bank_w[ti][0], bank_w[ti][1])
                    wt(v, bank_w[4 + ti][0], bank_w[4 + ti][1])
                    hs = hsc[:, 0:512]
                    hs2 = hsc2[:, 0:512]
                    v.tensor_single_scalar(hs, pb[ti][:], 0.0, ALU.max)
                    v.tensor_tensor(hs2, hs, hs, ALU.mult)
                    vS.inc(v.tensor_tensor(
                        h_hg[:, ti * I_FFN + oc * 512:ti * I_FFN + (oc + 1) * 512],
                        hs2, pb[4 + ti][:], ALU.mult))
                bank_r[ti] = (vS, vS.v)
                bank_r[4 + ti] = (vS, vS.v)
                v_hevac[(hg, oc, ti)] = vS.v

        for sub in range(2):
            def emit_wd_dma(hg, sub, islab):
                sl = (islab % 2) * 4 * 1024

                @blk.scalar
                def _(s, hg=hg, sub=sub, islab=islab, sl=sl):
                    wt(s, cS, c_tern)
                    wt(s, vS, v_bar)
                    wt_attn_out(s)
                    idx = (hg * 2 + sub) * 8 + islab
                    if idx >= 2:
                        p = idx - 2
                        pkey = (p // 16, (p // 8) % 2, p % 8)
                        assert pkey in t_wd_done, pkey
                        wt(s, tS, t_wd_done[pkey])
                    sem = wds[islab % 2]
                    src = ag2_v[islab:islab + 1, :,
                                2 * I_FFN:2 * I_FFN + 4096].rearrange(
                                    "r p q -> (r p) q")
                    s.dma_start(out=wd_buf[:, sl:sl + 4096],
                                in_=src).then_inc(sem.h, 16)
                    sem.v += 16
                    ds_wd[(hg, sub, islab)] = (sem, sem.v)
            emit_wd_dma(hg, sub, 0)
            emit_wd_dma(hg, sub, 1)

            for tt in range(2):
                ti = sub * 2 + tt
                tb = hg * HG + ti

                @blk.vector
                def _(v, hg=hg, ti=ti, tb=tb):
                    wt(v, vS, v_hevac[(hg, NOC - 1, ti)])
                    if hq_state.get("s_hq_prev"):
                        wt(v, sS, hq_state["s_hq_prev"])
                    v.reduce_max(scal[:, 23:24], h_hg[:, ti * I_FFN:(ti + 1) * I_FFN],
                                 axis=AX, apply_absolute_value=True)
                    v.drain()
                    mx = mxc_all[:, tb:tb + 1]
                    v.scalar_tensor_tensor(scal[:, 25:26], mx, mx, mx,
                                           ALU.mult, ALU.mult)
                    v.drain()
                    v.scalar_tensor_tensor(scal[:, 26:27], scal[:, 25:26],
                                           kc[:, 0:1], scal[:, 23:24],
                                           ALU.mult, ALU.mult)
                    v.tensor_tensor(scal[:, 25:26], scal[:, 25:26], kc[:, 0:1],
                                    ALU.mult)
                    v.drain()
                    v.tensor_scalar_max(scal[:, 27:28], scal[:, 26:27], EPS)
                    v.drain()
                    v.reciprocal(scal[:, 29:30], scal[:, 27:28])
                    v.tensor_tensor(sfin[:, ti:ti + 1], scal[:, 27:28], kc[:, 1:2],
                                    ALU.mult)
                    v.drain()
                    v.scalar_tensor_tensor(scal[:, 31:32], scal[:, 29:30],
                                           127.0, scal[:, 25:26],
                                           ALU.mult, ALU.mult)
                    vS.inc(v.engine_nop())
                v_sh = vS.v
                hq_state["g_mx_prev"] = (vS, v_sh)

                @blk.scalar
                def _(s, ti=ti, v_sh=v_sh):
                    wt(s, vS, v_sh)
                    sS.inc(s.activation(hq_i8[:], h_hg[:, ti * I_FFN:(ti + 1) * I_FFN],
                                        AF.Identity, scale=scal[:, 31:32]))
                s_hq = sS.v
                hq_state["s_hq_prev"] = s_hq

                @blk.vector
                def _(v, s_hq=s_hq):
                    wt(v, sS, s_hq)
                    wt(v, tS, hq_state["t_hqbf_readers"])
                    vS.inc(v.tensor_copy(hq_bf[:], hq_i8[:]))
                v_hqbf = (vS, vS.v)

                for w in range(8):
                    b = w % 2

                    @blk.tensor
                    def _(t, w=w, b=b, v_hqbf=v_hqbf):
                        wt(t, *v_hqbf)
                        wr_bank(t, b)
                        for j in range(4):
                            ins = t.transpose(pbb[b][:, j * 128:(j + 1) * 128],
                                              hq_bf[:, (w * 4 + j) * 128:(w * 4 + j + 1) * 128],
                                              ident_bf[:])
                        tS.inc(ins)
                    bank_w[b] = (tS, tS.v)
                    t_tr = tS.v

                    @blk.vector
                    def _(v, tt=tt, w=w, b=b, t_tr=t_tr):
                        wt(v, tS, t_tr)
                        wt(v, tS, hq_state["t_down_prev"])
                        dst = hq_fm[:, tt * I_FFN + w * 512:tt * I_FFN + (w + 1) * 512]
                        vS.inc(v.tensor_copy(dst, pbb[b][:, 0:512]))
                    bank_r[b] = (vS, vS.v)
                hq_state["t_hqbf_readers"] = tS.v
                v_h_done[(hg, ti)] = vS.v

            for islab in range(8):
                sl = (islab % 2) * 4 * 1024

                @blk.tensor
                def _(t, hg=hg, sub=sub, islab=islab, sl=sl):
                    wt(t, *ds_wd[(hg, sub, islab)])
                    for tt in range(2):
                        wt(t, vS, v_h_done[(hg, sub * 2 + tt)])
                        if islab == 0:
                            wr_bank(t, 4 + 2 * tt)
                            wr_bank(t, 5 + 2 * tt)
                    for ib in range(4):
                        iblk = islab * 4 + ib
                        for tt in range(2):
                            for nb in range(2):
                                ins = t.matmul(
                                    pb[4 + 2 * tt + nb][:],
                                    hq_fm[:, tt * I_FFN + iblk * 128:tt * I_FFN + (iblk + 1) * 128],
                                    wd_buf[:, sl + ib * 1024 + nb * 512:sl + ib * 1024 + (nb + 1) * 512],
                                    start=(iblk == 0), stop=(iblk == NI - 1))
                    tS.inc(ins)
                t_wd_done[(hg, sub, islab)] = tS.v
                if islab + 2 < 8:
                    emit_wd_dma(hg, sub, islab + 2)
                if islab == 7:
                    for bb in (4, 5, 6, 7):
                        bank_w[bb] = (tS, tS.v)
            t_dn = tS.v
            hq_state["t_down_prev"] = t_dn

            for tt in range(2):
                ti = sub * 2 + tt
                tb = hg * HG + ti

                @blk.scalar
                def _(s, tb=tb, ti=ti, tt=tt, t_dn=t_dn):
                    wt(s, tS, t_dn)
                    wt_attn_out(s)
                    if tb >= 2 and (tb - 2) in dg_out:
                        wt(s, *dg_out[tb - 2])
                    sl = (tb % 2) * D
                    s.activation(stage[:, sl:sl + 512], pb[4 + 2 * tt][:], AF.Identity,
                                 scale=sfin[:, ti:ti + 1])
                    sS.inc(s.activation(stage[:, sl + 512:sl + D], pb[5 + 2 * tt][:],
                                        AF.Identity, scale=sfin[:, ti:ti + 1]))
                bank_r[4 + 2 * tt] = (sS, sS.v)
                bank_r[5 + 2 * tt] = (sS, sS.v)
                s_dn = sS.v

                @blk.gpsimd
                def _(g, tb=tb, s_dn=s_dn):
                    wt(g, sS, s_dn)
                    wt_attn_out(g)
                    sem = asm_[tb % 2]
                    sl = (tb % 2) * D
                    g.dma_start(out=out[tb * 128:(tb + 1) * 128, :],
                                in_=stage[:, sl:sl + D],
                                accum_op=ALU.add).then_inc(sem.h, 16)
                    sem.v += 16
                    dg_out[tb] = (sem, sem.v)

    dump_reg.update(h_hg=h_hg[:], hq_fm=hq_fm[:], sfin=sfin[:], kc=kc[:],
                    xq_fm=xq_fm[:], wgu=wgu_buf[:], wd=wd_buf[:])
    dump_reg.pop("memz", None)
    dump_reg.pop("att_tm", None)
    dump_reg.pop("sq_fm", None)
    return _finish()


_NC_CACHE = None
_STOP = "all"
_DEBUG = False


def kernel(L0_attn_x, L0_attn_Wq, L0_attn_Wk, L0_attn_Wv, L0_attn_betas,
           L0_attn_Wo, L0_ffn_w_gate, L0_ffn_w_up, L0_ffn_w_down):
    global _NC_CACHE
    x = np.ascontiguousarray(np.asarray(L0_attn_x, np.float32).reshape(16384, D))
    wq = np.ascontiguousarray(np.asarray(L0_attn_Wq, np.float32))
    wk = np.ascontiguousarray(np.asarray(L0_attn_Wk, np.float32))
    wv = np.ascontiguousarray(np.asarray(L0_attn_Wv, np.float32))
    wo = np.ascontiguousarray(np.asarray(L0_attn_Wo, np.float32))
    betas = np.asarray(L0_attn_betas, np.float32).reshape(HD)
    betas_bc = np.ascontiguousarray(np.broadcast_to(betas[None, :], (128, HD)))
    wgT = np.ascontiguousarray(np.asarray(L0_ffn_w_gate, np.float32).T)
    wuT = np.ascontiguousarray(np.asarray(L0_ffn_w_up, np.float32).T)
    wdT = np.ascontiguousarray(np.asarray(L0_ffn_w_down, np.float32).T)

    if _NC_CACHE is None:
        _NC_CACHE = build_nc(stop=_STOP, debug=_DEBUG)
    nc = _NC_CACHE

    in_maps = []
    for c in range(N_CORES):
        mask = np.ascontiguousarray(
            np.broadcast_to((np.arange(N_CORES) < c).astype(np.float32)[None, :],
                            (128, N_CORES)))
        in_maps.append({
            "x": np.ascontiguousarray(x[c * T:(c + 1) * T]),
            "wq": wq, "wk": wk, "wv": wv, "wo": wo,
            "betas_bc": betas_bc,
            "wg_slice": np.ascontiguousarray(wgT[c * 128:(c + 1) * 128]),
            "wu_slice": np.ascontiguousarray(wuT[c * 128:(c + 1) * 128]),
            "wd_slice": np.ascontiguousarray(wdT[c * 512:(c + 1) * 512]),
            "mask_bc": mask,
        })
    res = run_bass_kernel_spmd(nc, in_maps, core_ids=list(range(N_CORES)))
    if _DEBUG:
        global _LAST_RES
        _LAST_RES = res
    outp = np.concatenate([r["out"] for r in res.results], axis=0)
    return outp.reshape(1, 16384, D)


# version 27
# speedup vs baseline: 1.0229x; 1.0229x over previous
import numpy as np
import concourse.bass as bass
import concourse.mybir as mybir
from concourse.bass_utils import run_bass_kernel_spmd

F32 = mybir.dt.float32
BF16 = mybir.dt.bfloat16
I8 = mybir.dt.int8
AF = mybir.ActivationFunctionType
ALU = mybir.AluOpType
AX = mybir.AxisListType.X

T = 2048
NT = 16
D = 1024
ND = 8
HD = 512
NHD = 4
H = 8
L = 256
NCH = 8
I_FFN = 4096
NI = 32
N_CORES = 8
EPS = 1e-5
QPL = 12288
HG = 4
NHG = NT // HG
WCNT = 1024 * 4096


class Ctr:
    def __init__(self, nc, name):
        self.cm = nc.semaphore(name)
        self.h = self.cm.__enter__()
        self.v = 0

    def inc(self, instr, n=1):
        instr.then_inc(self.h, n)
        self.v += n
        return instr


def build_nc(stop="all", debug=False):
    nc = bass.Bass("TRN2", target_bir_lowering=False, debug=False)
    _stack = []
    dump_reg = {}

    x_in = nc.dram_tensor("x", [T, D], F32, kind="ExternalInput").ap()
    wq_in = nc.dram_tensor("wq", [D, HD], F32, kind="ExternalInput").ap()
    wk_in = nc.dram_tensor("wk", [D, HD], F32, kind="ExternalInput").ap()
    wv_in = nc.dram_tensor("wv", [D, HD], F32, kind="ExternalInput").ap()
    wo_in = nc.dram_tensor("wo", [HD, D], F32, kind="ExternalInput").ap()
    betas_in = nc.dram_tensor("betas_bc", [128, HD], F32, kind="ExternalInput").ap()
    wg_sl = nc.dram_tensor("wg_slice", [128, I_FFN], F32, kind="ExternalInput").ap()
    wu_sl = nc.dram_tensor("wu_slice", [128, I_FFN], F32, kind="ExternalInput").ap()
    wd_sl = nc.dram_tensor("wd_slice", [512, D], F32, kind="ExternalInput").ap()
    mask_in = nc.dram_tensor("mask_bc", [128, N_CORES], F32, kind="ExternalInput").ap()
    out = nc.dram_tensor("out", [T, D], F32, kind="ExternalOutput").ap()

    ag1_in = nc.dram_tensor("ag1_in", [1, 8], F32).ap()
    ag1_out = nc.dram_tensor("ag1_out", [N_CORES, 8], F32).ap()
    ag2_in = nc.dram_tensor("ag2_in", [1, 128 * QPL], BF16).ap()
    ag2_out = nc.dram_tensor("ag2_out", [N_CORES, 128 * QPL], BF16).ap()
    ag3_in = nc.dram_tensor("ag3_in", [1, 64 * 520], F32).ap()
    ag3_out = nc.dram_tensor("ag3_out", [N_CORES, 64 * 520], F32).ap()

    ag2_v = ag2_out.rearrange("r (p q) -> r p q", p=128)
    ag2i_v = ag2_in.rearrange("r (p q) -> r p q", p=128)
    ag3_v = ag3_out.rearrange("r (p q) -> r p q", p=64)

    dX = Ctr(nc, "dX")
    dS = Ctr(nc, "dS")
    dG = Ctr(nc, "dG")
    xs = [Ctr(nc, "xs0"), Ctr(nc, "xs1"), Ctr(nc, "xs2"), Ctr(nc, "xs3")]
    ws = [Ctr(nc, "ws0"), Ctr(nc, "ws1")]
    x2s = [Ctr(nc, "x2s0"), Ctr(nc, "x2s1")]
    osm = [Ctr(nc, "osm0"), Ctr(nc, "osm1")]
    asm_ = [Ctr(nc, "asm0"), Ctr(nc, "asm1")]
    gus = [Ctr(nc, "gus0"), Ctr(nc, "gus1"), Ctr(nc, "gus2"),
           Ctr(nc, "gus3")]
    wds = [Ctr(nc, "wds0"), Ctr(nc, "wds1")]
    ags = [Ctr(nc, "ags0"), Ctr(nc, "ags1")]
    tS = Ctr(nc, "tS")
    vS = Ctr(nc, "vS")
    sS = Ctr(nc, "sS")
    gS = Ctr(nc, "gS")
    cS = Ctr(nc, "cS")

    def wt(eng, ctr, val=None):
        v = ctr.v if val is None else val
        assert v <= ctr.v, "wait on a future value"
        if v > 0:
            eng.wait_ge(ctr.h, v)

    psum_cms = [nc.psum_tensor(f"pb{i}", [128, 512], F32) for i in range(8)]
    pb = [p.__enter__() for p in psum_cms]
    _stack.extend(psum_cms)
    pbb = [p[:].bitcast(BF16) for p in pb]
    bank_w = [None] * 8
    bank_r = [None] * 8

    def wr_bank(eng, b):
        if bank_r[b] is not None:
            wt(eng, *bank_r[b])

    pers_cm = [
        nc.sbuf_tensor("ident_bf", [128, 128], BF16),
        nc.sbuf_tensor("ones_col", [128, 1], F32),
        nc.sbuf_tensor("ones_row", [1, 128], F32),
        nc.sbuf_tensor("gate_sb", [128, HD], F32),
        nc.sbuf_tensor("gate1m", [128, HD], F32),
        nc.sbuf_tensor("wo_bf", [128, NHD * 1024], BF16),
        nc.sbuf_tensor("xq_tm", [128, NT * D], I8),
        nc.sbuf_tensor("mxc_all", [128, NT], F32),
        nc.sbuf_tensor("mbc", [128, 8], F32),
        nc.sbuf_tensor("mask_sb", [128, N_CORES], F32),
        nc.sbuf_tensor("stage", [128, 2 * D], F32),
        nc.sbuf_tensor("scal", [128, 32], F32),
        nc.sbuf_tensor("srow", [1, 128], F32),
        nc.sbuf_tensor("epsc", [128, 1], F32),
        nc.sbuf_tensor("negone", [128, 1], F32),
        nc.sbuf_tensor("kc", [128, 4], F32),
    ]
    (ident_bf, ones_col, ones_row, gate_sb, gate1m, wo_bf, xq_tm, mxc_all, mbc,
     mask_sb, stage, scal, srow, epsc, negone, kc) = [t.__enter__() for t in pers_cm]
    _stack.extend(pers_cm)

    ga1_cm = [
        nc.sbuf_tensor("q_fm", [128, NHD * T], BF16),
        nc.sbuf_tensor("k_fm", [128, NHD * T], BF16),
        nc.sbuf_tensor("v_tm", [128, NT * 520], BF16),
    ]
    q_fm, k_fm, v_tm = [t.__enter__() for t in ga1_cm]
    _stack.extend(ga1_cm)

    gq_cm = [
        nc.sbuf_tensor("wsl", [128, 2 * 2048], F32),
        nc.sbuf_tensor("q_tmp", [128, 2048], F32),
        nc.sbuf_tensor("q_i8", [128, 2 * 2048], I8),
        nc.sbuf_tensor("q_bf", [128, 2 * 2048], BF16),
    ]
    wsl, q_tmp, q_i8, q_bf = [t.__enter__() for t in gq_cm]
    _stack.extend(gq_cm)

    NXR = 4
    gp1_cm = [
        nc.sbuf_tensor("x_sb", [128, NXR * D], F32),
        nc.sbuf_tensor("xn_sc", [128, 2 * D], F32),
        nc.sbuf_tensor("xnbf_sc", [128, D], BF16),
        nc.sbuf_tensor("p1st", [128, 48], F32),
    ]
    x_sb, xn_sc, xnbf_sc, p1st = [t.__enter__() for t in gp1_cm]
    _stack.extend(gp1_cm)

    xnfm_cm = [nc.sbuf_tensor("xn_fm", [128, ND * T], BF16)]
    xn_fm = xnfm_cm[0].__enter__()
    _stack.extend(xnfm_cm)

    gb_cm = [
        nc.sbuf_tensor("wq_bf", [128, ND * 512], BF16),
        nc.sbuf_tensor("wk_bf", [128, ND * 512], BF16),
        nc.sbuf_tensor("wv_bf", [128, ND * 512], BF16),
    ]
    wq_bf, wk_bf, wv_bf = [t.__enter__() for t in gb_cm]
    _stack.extend(gb_cm)

    blk_cm = nc.Block()
    blk = blk_cm.__enter__()

    all_dma_sems = [dX, dS, dG] + xs + ws + x2s + osm + asm_ + gus + wds + ags

    def _finish():
        if debug and dump_reg:
            @blk.sync
            def _(x):
                for ctr in [tS, vS, sS, gS, cS] + all_dma_sems:
                    if ctr.v:
                        x.wait_ge(ctr.h, ctr.v)
                for name, ap in dump_reg.items():
                    o = nc.dram_tensor("dbg_" + name, list(ap.shape),
                                       ap.dtype, kind="ExternalOutput")
                    x.dma_start(out=o.ap(), in_=ap).then_inc(dX.h, 16)
                    dX.v += 16

        @blk.sync
        def _(x):
            for sem in all_dma_sems:
                if sem.v:
                    x.wait_ge(sem.h, sem.v)
        blk_cm.__exit__(None, None, None)
        for t in reversed(_stack):
            t.__exit__(None, None, None)
        for ctr in [tS, vS, sS, gS, cS] + all_dma_sems:
            ctr.cm.__exit__(None, None, None)
        return nc

    @blk.gpsimd
    def _(g):
        g.memset(ident_bf[:], 0.0)
        g.affine_select(out=ident_bf[:], in_=ident_bf[:],
                        compare_op=ALU.not_equal, fill=1.0,
                        base=0, pattern=[[-1, 128]], channel_multiplier=1)
        g.memset(ones_col[:], 1.0)
        g.memset(epsc[:], EPS)
        g.memset(negone[:], -1.0)
        gS.inc(g.memset(ones_row[:], 1.0))
        g.dma_start(out=wo_bf[:].rearrange("p (b c) -> p b c", b=4),
                    in_=wo_in.rearrange("(b p) c -> p b c", p=128)).then_inc(dG.h, 16)
        g.dma_start(out=mask_sb[:], in_=mask_in[:]).then_inc(dG.h, 16)
        g.dma_start(out=gate_sb[:], in_=betas_in[:]).then_inc(dG.h, 16)
        g.dma_start(out=wq_bf[:].rearrange("p (b c) -> p b c", b=8),
                    in_=wq_in.rearrange("(b p) c -> p b c", p=128)).then_inc(dG.h, 16)
        g.dma_start(out=wk_bf[:].rearrange("p (b c) -> p b c", b=8),
                    in_=wk_in.rearrange("(b p) c -> p b c", p=128)).then_inc(dG.h, 16)
        g.dma_start(out=wv_bf[:].rearrange("p (b c) -> p b c", b=8),
                    in_=wv_in.rearrange("(b p) c -> p b c", p=128)).then_inc(dG.h, 16)
        dG.v += 96
    g_setup = gS.v
    dg_setup = dG.v
    dg_wqkv = dG.v

    x_ld = [None] * NT

    @blk.sync
    def _(s):
        for tb in range(NXR):
            sem = xs[tb % NXR]
            s.dma_start(out=x_sb[:, tb * D:(tb + 1) * D],
                        in_=x_in[tb * 128:(tb + 1) * 128, :]).then_inc(sem.h, 16)
            sem.v += 16
            x_ld[tb] = (sem, sem.v)

    v_stats = [0] * NT
    for tb in range(NT):
        xap = x_sb[:, (tb % NXR) * D:(tb % NXR + 1) * D]
        xsq = xn_sc[:, (tb % 2) * D:(tb % 2 + 1) * D]

        @blk.scalar
        def _(s, xap=xap, xsq=xsq, tb=tb):
            wt(s, *x_ld[tb])
            if tb >= 2:
                wt(s, vS, v_stats[tb - 2])
            sS.inc(s.activation(xsq, xap, AF.Square,
                                accum_out=scal[:, 20 + tb % 2:21 + tb % 2]))
        s_sq = sS.v

        @blk.vector
        def _(v, xap=xap, tb=tb, s_sq=s_sq):
            wt(v, *x_ld[tb])
            v.reduce_sum(scal[:, 28:29], xap, axis=AX)
            v.drain()
            wt(v, sS, s_sq)
            v.tensor_scalar_mul(p1st[:, tb:tb + 1], scal[:, 28:29], 1.0 / D)
            v.tensor_scalar_mul(scal[:, 30:31], scal[:, 20 + tb % 2:21 + tb % 2],
                                1.0 / D)
            v.drain()
            vS.inc(v.scalar_tensor_tensor(p1st[:, 16 + tb:17 + tb],
                                          p1st[:, tb:tb + 1], p1st[:, tb:tb + 1],
                                          scal[:, 30:31], ALU.mult, ALU.subtract))
        v_stats[tb] = vS.v

        if tb + NXR < NT:
            @blk.sync
            def _(s, tb=tb):
                wt(s, vS, v_stats[tb])
                nb = tb + NXR
                sem = xs[nb % NXR]
                s.dma_start(out=x_sb[:, (nb % NXR) * D:(nb % NXR + 1) * D],
                            in_=x_in[nb * 128:(nb + 1) * 128, :]).then_inc(sem.h, 16)
                sem.v += 16
                x_ld[nb] = (sem, sem.v)

    qunits = []
    for hh in range(2):
        qunits.append((wg_sl[:, hh * 2048:(hh + 1) * 2048], 0, hh * 2048, False))
    for hh in range(2):
        qunits.append((wu_sl[:, hh * 2048:(hh + 1) * 2048], 1, 4096 + hh * 2048,
                       False))
    for hh in range(2):
        src = wd_sl[hh * 256:(hh + 1) * 256, :].rearrange("(b p) c -> p b c", p=128)
        qunits.append((src, 2, 8192 + hh * 2048, True))

    v_q1 = [0] * len(qunits)
    ds_q1 = [0] * len(qunits)

    for i, (src, m, ofs, is3d) in enumerate(qunits):
        @blk.scalar
        def _(s, src=src, i=i, is3d=is3d):
            if i >= 2:
                wt(s, vS, v_q1[i - 2])
            dst = wsl[:, (i % 2) * 2048:(i % 2 + 1) * 2048]
            if is3d:
                dst = dst.rearrange("p (b c) -> p b c", b=2)
            sem = ws[i % 2]
            s.dma_start(out=dst, in_=src).then_inc(sem.h, 16)
            sem.v += 16
            ds_q1[i] = (sem, sem.v)

        @blk.vector
        def _(v, i=i):
            wt(v, *ds_q1[i])
            vS.inc(v.reduce_sum(scal[:, i:i + 1],
                                wsl[:, (i % 2) * 2048:(i % 2 + 1) * 2048],
                                axis=AX, apply_absolute_value=True))
            v_q1[i] = vS.v
    v_asum = vS.v

    @blk.tensor
    def _(t):
        wt(t, vS, v_asum)
        wt(t, gS, g_setup)
        tS.inc(t.matmul(pb[0][:][0:6, 0:1], scal[:, 0:6], ones_col[:],
                        start=True, stop=True))
    bank_w[0] = (tS, tS.v)

    @blk.vector
    def _(v):
        wt(v, tS, tS.v)
        vS.inc(v.tensor_copy(scal[0:6, 6:7], pb[0][:][0:6, 0:1]))
    bank_r[0] = (vS, vS.v)
    v_psums = vS.v

    @blk.gpsimd
    def _(g):
        wt(g, vS, v_psums)
        g.dma_start(out=ag1_in.rearrange("r c -> c r")[0:6],
                    in_=scal[0:6, 6:7]).then_inc(dG.h, 16)
        dG.v += 16
        wt(g, dG, dG.v)
        cS.inc(g.collective_compute(
            "AllGather", ALU.bypass, replica_groups=[list(range(N_CORES))],
            ins=[ag1_in.opt()], outs=[ag1_out.opt()]))
        wt(g, cS, cS.v)
        g.dma_start(out=srow[0:1, 0:64],
                    in_=ag1_out.rearrange("a b -> (a b)")).then_inc(dG.h, 16)
        dG.v += 16
    dg_ag1 = dG.v

    @blk.vector
    def _(v):
        wt(v, dG, dg_ag1)
        for i in range(4):
            v.tensor_tensor(srow[0:1, 64 + 6 * i:70 + 6 * i],
                            srow[0:1, 16 * i:16 * i + 6],
                            srow[0:1, 16 * i + 8:16 * i + 14], ALU.add)
        v.drain()
        for i in range(2):
            v.tensor_tensor(srow[0:1, 88 + 6 * i:94 + 6 * i],
                            srow[0:1, 64 + 12 * i:70 + 12 * i],
                            srow[0:1, 70 + 12 * i:76 + 12 * i], ALU.add)
        v.drain()
        v.tensor_tensor(srow[0:1, 100:106], srow[0:1, 88:94],
                        srow[0:1, 94:100], ALU.add)
        v.drain()
        sr2 = srow[0:1, 100:106].rearrange("p (a b) -> p a b", b=2)
        v.tensor_tensor(srow[0:1, 106:109], sr2[:, :, 0:1].rearrange(
            "p a b -> p (a b)"), sr2[:, :, 1:2].rearrange("p a b -> p (a b)"),
            ALU.add)
        v.drain()
        v.tensor_scalar(srow[0:1, 110:113], srow[0:1, 106:109], 1.0 / WCNT, EPS,
                        ALU.mult, ALU.max)
        vS.inc(v.engine_nop())
    v_means = vS.v

    @blk.tensor
    def _(t):
        wt(t, vS, v_means)
        wr_bank(t, 1)
        tS.inc(t.matmul(pb[1][:][:, 0:3], ones_row[:], srow[0:1, 110:113],
                        start=True, stop=True))
    bank_w[1] = (tS, tS.v)

    @blk.vector
    def _(v):
        wt(v, tS, tS.v)
        v.tensor_copy(mbc[:, 0:3], pb[1][:][:, 0:3])
        v.drain()
        v.tensor_tensor(mbc[:, 4:5], mbc[:, 0:1], mbc[:, 1:2], ALU.mult)
        v.drain()
        v.tensor_tensor(mbc[:, 3:4], mbc[:, 4:5], mbc[:, 0:1], ALU.mult)
        v.drain()
        v.tensor_scalar_mul(kc[:, 0:1], mbc[:, 3:4], 1.0 / (127.0 ** 3))
        v.tensor_scalar_mul(kc[:, 1:2], mbc[:, 2:3], 1.0 / 127.0)
        vS.inc(v.reciprocal(scal[:, 5:8], mbc[:, 0:3]))
    bank_r[1] = (vS, vS.v)
    v_mbc = vS.v

    if stop == "setup":
        return _finish()

    v_q2 = [0] * len(qunits)
    g_q2 = [0] * len(qunits)
    ds_q2 = [0] * len(qunits)
    for i, (src, m, ofs, is3d) in enumerate(qunits):
        @blk.scalar
        def _(s, src=src, i=i, is3d=is3d):
            wt(s, vS, v_q1[len(qunits) - 1] if i < 2 else v_q2[i - 2])
            dst = wsl[:, (i % 2) * 2048:(i % 2 + 1) * 2048]
            dst3 = dst.rearrange("p (b c) -> p b c", b=2) if is3d else dst
            sem = ws[i % 2]
            s.dma_start(out=dst3, in_=src).then_inc(sem.h, 16)
            sem.v += 16
            ds_q2[i] = (sem, sem.v)

        @blk.vector
        def _(v, i=i, m=m):
            wt(v, *ds_q2[i])
            wt(v, vS, v_mbc)
            if i >= 2:
                wt(v, gS, g_q2[i - 2])
            sl = (i % 2) * 2048
            v.tensor_scalar(q_tmp[:], wsl[:, sl:sl + 2048],
                            scal[:, 5 + m:6 + m], 1.27, ALU.mult, ALU.min)
            v.drain()
            vS.inc(v.tensor_scalar_max(q_i8[:, sl:sl + 2048], q_tmp[:], -1.27))
            v_q2[i] = vS.v

        @blk.gpsimd
        def _(g, i=i, ofs=ofs):
            wt(g, vS, v_q2[i])
            gS.inc(g.tensor_copy(q_bf[:, (i % 2) * 2048:(i % 2 + 1) * 2048],
                                 q_i8[:, (i % 2) * 2048:(i % 2 + 1) * 2048]))
            g_q2[i] = gS.v
            g.dma_start(out=ag2i_v[0:1, :, ofs:ofs + 2048].rearrange(
                            "r p q -> (r p) q"),
                        in_=q_bf[:, (i % 2) * 2048:(i % 2 + 1) * 2048]
                        ).then_inc(dG.h, 16)
            dG.v += 16
    dg_q = dG.v

    @blk.gpsimd
    def _(g):
        wt(g, dG, dg_q)
        cS.inc(g.collective_compute(
            "AllGather", ALU.bypass, replica_groups=[list(range(N_CORES))],
            ins=[ag2_in.opt()], outs=[ag2_out.opt()]))
    c_tern = cS.v

    if stop == "quant":
        return _finish()

    @blk.scalar
    def _(s):
        wt(s, vS, vS.v)
        wt(s, gS, g_setup)
        s.activation(scal[:, 0:16], p1st[:, 16:32], AF.Ln, bias=epsc[:],
                     scale=-1.0)
        s.drain()
        sS.inc(s.activation(p1st[:, 32:48], scal[:, 0:16], AF.Exp, scale=-0.5))
    s_rstd_all = sS.v

    x_ld2 = [None] * NT

    @blk.sync
    def _(s):
        for tb in range(NXR):
            wt(s, vS, v_stats[NT - NXR + tb])
            sem = xs[tb % NXR]
            s.dma_start(out=x_sb[:, (tb % NXR) * D:(tb % NXR + 1) * D],
                        in_=x_in[tb * 128:(tb + 1) * 128, :]).then_inc(sem.h, 16)
            sem.v += 16
            x_ld2[tb] = (sem, sem.v)

    @blk.gpsimd
    def _(g):
        vt = v_tm[:].rearrange("p (a b c) -> p a b c", b=H, c=65)
        gS.inc(g.memset(vt[:, :, :, 64:65], 1.0))
    g_vones = gS.v

    rr = [2]

    def next_bank():
        b = rr[0]
        rr[0] = 2 + (rr[0] - 2 + 1) % 6
        return b

    s_v_evac = [0] * NT

    def emit_qk(tg):
        for (w_bf, dst) in ((wq_bf, q_fm), (wk_bf, k_fm)):
            for hb in range(NHD):
                b = next_bank()

                @blk.tensor
                def _(t, w_bf=w_bf, hb=hb, tg=tg, b=b):
                    wt(t, vS, v_xnfm[tg * 4 + 3])
                    wt(t, dG, dg_wqkv)
                    wr_bank(t, b)
                    for db in range(ND):
                        ins = t.matmul(
                            pb[b][:][:],
                            w_bf[:, db * 512 + hb * 128:db * 512 + (hb + 1) * 128],
                            xn_fm[:, db * T + tg * 512:db * T + (tg + 1) * 512],
                            start=(db == 0), stop=(db == ND - 1))
                    tS.inc(ins)
                bank_w[b] = (tS, tS.v)
                t_mm = tS.v

                @blk.scalar
                def _(s, dst=dst, hb=hb, tg=tg, b=b, t_mm=t_mm):
                    wt(s, tS, t_mm)
                    sS.inc(s.copy(dst[:, hb * T + tg * 512:hb * T + (tg + 1) * 512],
                                  pb[b][:][:]))
                bank_r[b] = (sS, sS.v)

    def emit_v(tb):
        b = next_bank()

        @blk.tensor
        def _(t, tb=tb, b=b):
            wt(t, vS, v_xnfm[tb])
            wt(t, dG, dg_wqkv)
            wr_bank(t, b)
            for db in range(ND):
                ins = t.matmul(pb[b][:][:],
                               xn_fm[:, db * T + tb * 128:db * T + (tb + 1) * 128],
                               wv_bf[:, db * 512:(db + 1) * 512],
                               start=(db == 0), stop=(db == ND - 1))
            tS.inc(ins)
        bank_w[b] = (tS, tS.v)
        t_mm = tS.v

        @blk.vector
        def _(v, tb=tb, b=b, t_mm=t_mm):
            wt(v, tS, t_mm)
            wt(v, gS, g_vones)
            dst = v_tm[:, tb * 520:(tb + 1) * 520].rearrange(
                "p (a b) -> p a b", b=65)[:, :, 0:64]
            src = pb[b][:][:, 0:512].rearrange("p (a b) -> p a b", b=64)
            vS.inc(v.tensor_copy(dst, src))
        bank_r[b] = (vS, vS.v)
        s_v_evac[tb] = (vS, vS.v)

    v_xnfm = [0] * NT
    t_xntr = [0] * NT
    s_xnq_done = [0] * NT
    for tb in range(NT):
        xap = x_sb[:, (tb % NXR) * D:(tb % NXR + 1) * D]
        xn = xn_sc[:, (tb % 2) * D:(tb % 2 + 1) * D]

        @blk.vector
        def _(v, xap=xap, xn=xn, tb=tb):
            wt(v, *x_ld2[tb])
            wt(v, sS, s_rstd_all)
            if tb >= 2:
                wt(v, vS, v_xnfm[tb - 2])
            if tb >= 1:
                wt(v, sS, s_xnq_done[tb - 1])
            v.tensor_scalar(xn[:], xap, p1st[:, tb:tb + 1], p1st[:, 32 + tb:33 + tb],
                            ALU.subtract, ALU.mult)
            v.drain()
            v.reduce_max(scal[:, 28:29], xn[:], axis=AX, apply_absolute_value=True)
            v.drain()
            v.tensor_scalar_max(mxc_all[:, tb:tb + 1], scal[:, 28:29], EPS)
            v.drain()
            v.reciprocal(scal[:, 26 + tb % 2:27 + tb % 2], mxc_all[:, tb:tb + 1])
            v.drain()
            v.tensor_scalar_mul(scal[:, 24 + tb % 2:25 + tb % 2],
                                scal[:, 26 + tb % 2:27 + tb % 2], 127.0)
            vS.inc(v.engine_nop())
        v_xnq = vS.v

        @blk.scalar
        def _(s, xn=xn, tb=tb, v_xnq=v_xnq):
            wt(s, vS, v_xnq)
            if tb >= 1:
                wt(s, tS, t_xntr[tb - 1])
            s.activation(xq_tm[:, tb * D:(tb + 1) * D], xn[:], AF.Identity,
                         scale=scal[:, 24 + tb % 2:25 + tb % 2])
            sS.inc(s.copy(xnbf_sc[:], xn[:]))
        s_xnq_done[tb] = sS.v
        v_xnq = (sS, s_xnq_done[tb])

        if tb + NXR < NT:
            @blk.sync
            def _(s, tb=tb):
                wt(s, vS, vS.v)
                nb = tb + NXR
                sem = xs[nb % NXR]
                s.dma_start(out=x_sb[:, (nb % NXR) * D:(nb % NXR + 1) * D],
                            in_=x_in[nb * 128:(nb + 1) * 128, :]).then_inc(sem.h, 16)
                sem.v += 16
                x_ld2[nb] = (sem, sem.v)

        for w in range(2):
            b = w

            @blk.tensor
            def _(t, w=w, b=b, tb=tb):
                wt(t, sS, s_xnq_done[tb])
                wr_bank(t, b)
                for j in range(4):
                    ins = t.transpose(pbb[b][:, j * 128:(j + 1) * 128],
                                      xnbf_sc[:, (w * 4 + j) * 128:(w * 4 + j + 1) * 128],
                                      ident_bf[:])
                tS.inc(ins)
            bank_w[b] = (tS, tS.v)
            t_tr = tS.v
            if w == 1:
                t_xntr[tb] = tS.v

            @blk.vector
            def _(v, w=w, b=b, tb=tb, t_tr=t_tr):
                wt(v, tS, t_tr)
                dst = xn_fm[:, w * 4 * T:(w * 4 + 4) * T].rearrange(
                    "p (a b) -> p a b", b=T)[:, :, tb * 128:(tb + 1) * 128]
                src = pbb[b][:, 0:512].rearrange("p (a b) -> p a b", b=128)
                vS.inc(v.tensor_copy(dst, src))
            bank_r[b] = (vS, vS.v)
            if w == 1:
                v_xnfm[tb] = vS.v
        v_xnfm_all = vS.v
        if tb % 4 == 3:
            emit_qk(tb // 4)
            for vtb in range(tb - 3, tb + 1):
                emit_v(vtb)

    dump_reg.update(xq_tm=xq_tm[:], mxc=mxc_all[:], xn_fm=xn_fm[:],
                    x_sb=x_sb[:], xn_sc=xn_sc[:], scal=scal[:])
    if stop == "p1":
        return _finish()

    @blk.scalar
    def _(s):
        wt(s, dG, dg_setup)
        sS.inc(s.activation(stage[:, 0:HD], gate_sb[:], AF.Exp))
    s_gexp = sS.v

    @blk.vector
    def _(v):
        wt(v, sS, s_gexp)
        v.tensor_scalar_add(stage[:, HD:2 * HD], stage[:, 0:HD], 1.0)
        v.drain()
        v.reciprocal(stage[:, 3 * HD:4 * HD], stage[:, HD:2 * HD])
        v.drain()
        v.tensor_tensor(gate_sb[:], stage[:, 0:HD], stage[:, 3 * HD:4 * HD],
                        ALU.mult)
        v.drain()
        v.drain()
        vS.inc(v.tensor_scalar(gate1m[:], gate_sb[:], -1.0, 1.0,
                               ALU.mult, ALU.add))
    v_gate = vS.v

    if stop == "gate":
        return _finish()

    s_qk_done = sS.v
    s_v_done = (vS, vS.v)

    dump_reg.update(q_fm=q_fm[:], k_fm=k_fm[:], v_tm=v_tm[:],
                    gate=gate_sb[:], mbc=mbc[:])
    if stop == "p2":
        return _finish()

    for t in reversed(gb_cm):
        t.__exit__(None, None, None)
        _stack.remove(t)
    xnfm_cm[0].__exit__(None, None, None)
    _stack.remove(xnfm_cm[0])
    for t in reversed(gp1_cm):
        t.__exit__(None, None, None)
        _stack.remove(t)
    for t in reversed(gq_cm):
        t.__exit__(None, None, None)
        _stack.remove(t)

    ga2_cm = [
        nc.sbuf_tensor("sk_fm", [128, 2 * 1024], BF16),
        nc.sbuf_tensor("sk_tm", [128, 2 * 1024], BF16),
        nc.sbuf_tensor("att_tm", [128, NT * HD], BF16),
        nc.sbuf_tensor("e_sb", [128, 16 * 256], BF16),
        nc.sbuf_tensor("eT_sb", [128, 16 * 256], BF16),
        nc.sbuf_tensor("elu_sc", [128, 12 * 256], BF16),
        nc.sbuf_tensor("ssum", [128, 16], F32),
        nc.sbuf_tensor("rsum", [128, 16], F32),
        nc.sbuf_tensor("memz", [128, H * 9 * 65], F32),
        nc.sbuf_tensor("memz_bf", [128, NCH * 520], BF16),
        nc.sbuf_tensor("corr", [128, 2 * 520], F32),
        nc.sbuf_tensor("agt", [128, 2 * 520], F32),
        nc.sbuf_tensor("att_fm", [128, NHD * T], BF16),
        nc.sbuf_tensor("mix_sc", [128, 3 * HD], F32),
        nc.sbuf_tensor("x2_sb", [128, 2 * D], F32),
    ]
    (sk_fm, sk_tm, att_tm, e_sb, eT_sb, elu_sc, ssum, rsum, memz, memz_bf,
     corr, agt, att_fm, mix_sc, x2_sb) = [t.__enter__() for t in ga2_cm]
    _stack.extend(ga2_cm)


    @blk.vector
    def _(v):
        wt(v, *s_v_done)
        mz = memz[:].rearrange("p (a b c) -> p a b c", b=9, c=65)
        vS.inc(v.memset(mz[0:64, :, 0:1, :], 0.0))
    v_mz0 = vS.v

    v_sk = [0] * NCH
    v_snap = [0] * (NCH + 1)
    v_snap[0] = v_mz0
    t_skTv = [0] * NCH
    t_sk_tr = [0] * NCH
    s_kexp = [0] * NCH
    for c in range(NCH):
        @blk.vector
        def _(v, c=c):
            wt(v, *s_v_done)
            if c >= 1:
                wt(v, sS, s_kexp[c - 1])
            for hb in range(NHD):
                col = hb * T + c * 256
                v.tensor_single_scalar(elu_sc[:, hb * 256:(hb + 1) * 256],
                                       k_fm[:, col:col + 256], 0.0, ALU.min)
            vS.inc(v.engine_nop())
        v_min = vS.v

        @blk.scalar
        def _(s, v_min=v_min):
            wt(s, vS, v_min)
            for hb in range(NHD):
                ins = s.activation(elu_sc[:, 1024 + hb * 256:1024 + (hb + 1) * 256],
                                   elu_sc[:, hb * 256:(hb + 1) * 256], AF.Exp)
            sS.inc(ins)
        s_exp = sS.v
        s_kexp[c] = s_exp

        @blk.vector
        def _(v, c=c, s_exp=s_exp):
            wt(v, sS, s_exp)
            if c >= 2:
                wt(v, tS, t_sk_tr[c - 2])
            for hb in range(NHD):
                col = hb * T + c * 256
                v.tensor_single_scalar(elu_sc[:, 2048 + hb * 256:2048 + (hb + 1) * 256],
                                       k_fm[:, col:col + 256], 0.0, ALU.max)
            v.drain()
            for hb in range(NHD):
                scol2 = (c % 2) * 1024 + hb * 256
                v.tensor_tensor(sk_fm[:, scol2:scol2 + 256],
                                elu_sc[:, 1024 + hb * 256:1024 + (hb + 1) * 256],
                                elu_sc[:, 2048 + hb * 256:2048 + (hb + 1) * 256],
                                ALU.add)
            vS.inc(v.engine_nop())
        v_sk[c] = vS.v

        for half in range(2):
            b = 6 + half

            @blk.tensor
            def _(t, c=c, half=half, b=b):
                wt(t, vS, v_sk[c])
                wr_bank(t, b)
                for hp in range(NHD):
                    col = (c % 2) * 1024 + hp * 256 + half * 128
                    ins = t.transpose(pbb[b][:, hp * 128:(hp + 1) * 128],
                                      sk_fm[:, col:col + 128], ident_bf[:])
                tS.inc(ins)
            bank_w[b] = (tS, tS.v)
            t_tr = tS.v
            t_sk_tr[c] = tS.v

            @blk.vector
            def _(v, c=c, half=half, b=b, t_tr=t_tr):
                wt(v, tS, t_tr)
                if c >= 2:
                    wt(v, tS, t_skTv[c - 2])
                vS.inc(v.tensor_copy(
                    sk_tm[:, (c % 2) * 1024 + half * 512:(c % 2) * 1024 + (half + 1) * 512],
                    pbb[b][:, 0:512]))
            bank_r[b] = (vS, vS.v)
        v_sktm = vS.v

        for gix in range(2):
            b = 4 + gix

            @blk.tensor
            def _(t, c=c, gix=gix, b=b, v_sktm=v_sktm):
                wt(t, vS, v_sktm)
                wt(t, *s_v_evac[2 * c + 1])
                wr_bank(t, b)
                for hh in range(4):
                    h = gix * 4 + hh
                    for half in range(2):
                        lcol = (c % 2) * 1024 + half * 512 + (h // 2) * 128 + (h % 2) * 64
                        ins = t.matmul(
                            pb[b][:][0:64, hh * 72:hh * 72 + 65],
                            sk_tm[:, lcol:lcol + 64],
                            v_tm[:, (2 * c + half) * 520 + h * 65:
                                 (2 * c + half) * 520 + (h + 1) * 65],
                            start=(half == 0), stop=(half == 1))
                tS.inc(ins)
            bank_w[b] = (tS, tS.v)
            t_ct = tS.v
            t_skTv[c] = tS.v

            @blk.vector
            def _(v, c=c, gix=gix, b=b, t_ct=t_ct):
                wt(v, tS, t_ct)
                wt(v, vS, v_snap[c])
                mz = memz[0:64, :].rearrange("p (a b) -> p a b", b=9 * 65)
                dst = mz[:, gix * 4:(gix + 1) * 4, (c + 1) * 65:(c + 2) * 65]
                src0 = mz[:, gix * 4:(gix + 1) * 4, c * 65:(c + 1) * 65]
                src1 = pb[b][:][0:64, 0:288].rearrange(
                    "p (a b) -> p a b", b=72)[:, :, 0:65]
                vS.inc(v.tensor_tensor(dst, src0, src1, ALU.add))
            bank_r[b] = (vS, vS.v)
        v_snap[c + 1] = vS.v

    @blk.gpsimd
    def _(g):
        wt(g, vS, v_snap[NCH])
        mz = memz[0:64, :].rearrange("p (a b) -> p a b", b=9 * 65)
        g.dma_start(out=ag3_in.rearrange("r (p q) -> r p q", p=64)[0:1].rearrange(
                        "r p q -> (r p) q").rearrange("p (a b) -> p a b", b=65),
                    in_=mz[:, 0:8, 8 * 65:9 * 65]).then_inc(dG.h, 16)
        dG.v += 16
        wt(g, dG, dG.v)
        cS.inc(g.collective_compute(
            "AllGather", ALU.bypass, replica_groups=[list(range(N_CORES))],
            ins=[ag3_in.opt()], outs=[ag3_out.opt()]))
    c_mem = cS.v

    dump_reg.pop("xn_fm", None)
    dump_reg.update(memz=memz[:], sk_fm=sk_fm[:])
    if stop == "p3a":
        return _finish()

    s_att_evac = [0] * NT
    s_qexp = [0] * NCH
    p3b_lim = NCH
    if stop.startswith("p3b:"):
        p3b_lim = int(stop.split(":")[1])

    def p3_sbank(g, h, qh):
        return 4 * g + qh * 2 + (h % 2)

    def p3_scol(h):
        return ((h % 4) // 2) * 256

    grp_state = {}
    for c in range(NCH):
        if c >= p3b_lim:
            continue
        for g in range(2):
            heads = range(4 * g, 4 * g + 4)

            @blk.tensor
            def _(t, c=c, g=g, heads=heads):
                wt(t, sS, s_qk_done)
                for h in heads:
                    for qh in range(2):
                        wr_bank(t, p3_sbank(g, h, qh))
                for h in heads:
                    for qh in range(2):
                        pr = (h % 2) * 64
                        qcol = (h // 2) * T + c * 256 + qh * 128
                        kcol = (h // 2) * T + c * 256
                        ins = t.matmul(
                            pb[p3_sbank(g, h, qh)][:][:, p3_scol(h):p3_scol(h) + 256],
                            q_fm[pr:pr + 64, qcol:qcol + 128],
                            k_fm[pr:pr + 64, kcol:kcol + 256],
                            start=True, stop=True)
                tS.inc(ins)
            for h in heads:
                for qh in range(2):
                    bank_w[p3_sbank(g, h, qh)] = (tS, tS.v)
            t_sc = tS.v

            @blk.vector
            def _(v, c=c, g=g, heads=heads, t_sc=t_sc):
                wt(v, tS, t_sc)
                for h in heads:
                    for qh in range(2):
                        i = h * 2 + qh
                        v.reduce_max(scal[:, i:i + 1],
                                     pb[p3_sbank(g, h, qh)][:][:, p3_scol(h):p3_scol(h) + 256],
                                     axis=AX)
                v.drain()
                for h in heads:
                    for qh in range(2):
                        i = h * 2 + qh
                        v.tensor_scalar_mul(scal[:, 16 + i:17 + i],
                                            scal[:, i:i + 1], -0.125)
                vS.inc(v.engine_nop())
            v_mx = vS.v

            @blk.scalar
            def _(s, c=c, g=g, heads=heads, v_mx=v_mx):
                wt(s, vS, v_mx)
                for h in heads:
                    for qh in range(2):
                        i = h * 2 + qh
                        ins = s.activation(
                            e_sb[:, i * 256:(i + 1) * 256],
                            pb[p3_sbank(g, h, qh)][:][:, p3_scol(h):p3_scol(h) + 256],
                            AF.Exp, bias=scal[:, 16 + i:17 + i], scale=0.125,
                            accum_out=ssum[:, i:i + 1])
                sS.inc(ins)
            for h in heads:
                for qh in range(2):
                    bank_r[p3_sbank(g, h, qh)] = (sS, sS.v)
            s_exp = sS.v

            @blk.vector
            def _(v, g=g, s_exp=s_exp):
                wt(v, sS, s_exp)
                vS.inc(v.reciprocal(rsum[:, 8 * g:8 * g + 8], ssum[:, 8 * g:8 * g + 8]))
            v_rs = vS.v

            @blk.tensor
            def _(t, g=g, s_exp=s_exp):
                wt(t, sS, s_exp)
                wr_bank(t, 4 * g)
                wr_bank(t, 4 * g + 2)
                for k in range(16):
                    i = 8 * g + k // 2
                    j = k % 2
                    b = 4 * g + 2 * (k // 8)
                    ins = t.transpose(pbb[b][:, (k % 8) * 128:(k % 8 + 1) * 128],
                                      e_sb[:, i * 256 + j * 128:i * 256 + (j + 1) * 128],
                                      ident_bf[:])
                tS.inc(ins)
            bank_w[4 * g] = (tS, tS.v)
            bank_w[4 * g + 2] = (tS, tS.v)
            t_eT = tS.v

            @blk.vector
            def _(v, g=g, t_eT=t_eT):
                wt(v, tS, t_eT)
                v.tensor_copy(eT_sb[:, g * 2048:g * 2048 + 1024],
                              pbb[4 * g][:, 0:1024])
                vS.inc(v.tensor_copy(eT_sb[:, g * 2048 + 1024:g * 2048 + 2048],
                                     pbb[4 * g + 2][:, 0:1024]))
            bank_r[4 * g] = (vS, vS.v)
            bank_r[4 * g + 2] = (vS, vS.v)
            v_eT = vS.v

            @blk.tensor
            def _(t, c=c, g=g, heads=heads, v_eT=v_eT):
                wt(t, vS, v_eT)
                wt(t, *s_v_evac[2 * c + 1])
                wr_bank(t, 4 * g + 1)
                wr_bank(t, 4 * g + 3)
                for h in heads:
                    for qh in range(2):
                        i = h * 2 + qh
                        k0 = (i - 8 * g) * 2
                        for kb in range(2):
                            k = k0 + kb
                            col = g * 2048 + (k // 8) * 1024 + (k % 8) * 128
                            ins = t.matmul(
                                pb[4 * g + 1 + 2 * qh][:][:, (h - 4 * g) * 64:(h - 4 * g + 1) * 64],
                                eT_sb[:, col:col + 128],
                                v_tm[:, (2 * c + kb) * 520 + h * 65:
                                     (2 * c + kb) * 520 + h * 65 + 64],
                                start=(kb == 0), stop=(kb == 1))
                tS.inc(ins)
            bank_w[4 * g + 1] = (tS, tS.v)
            bank_w[4 * g + 3] = (tS, tS.v)
            t_av = tS.v
            grp_state[g] = t_av

            @blk.scalar
            def _(s, c=c, g=g, heads=heads, t_av=t_av):
                wt(s, tS, t_av)
                for qh in range(2):
                    tb = 2 * c + qh
                    for h in heads:
                        ins = s.activation(
                            att_tm[:, tb * HD + h * 64:tb * HD + (h + 1) * 64],
                            pb[4 * g + 1 + 2 * qh][:][:, (h - 4 * g) * 64:(h - 4 * g + 1) * 64],
                            AF.Identity,
                            scale=rsum[:, h * 2 + qh:h * 2 + qh + 1])
                sS.inc(ins)
            bank_r[4 * g + 1] = (sS, sS.v)
            bank_r[4 * g + 3] = (sS, sS.v)
            if g == 1:
                s_att_evac[2 * c] = sS.v
                s_att_evac[2 * c + 1] = sS.v
        t_av = tS.v

        @blk.vector
        def _(v, c=c, t_av=t_av):
            wt(v, tS, t_av)
            if c >= 1:
                wt(v, sS, s_qexp[c - 1])
            for hb in range(NHD):
                col = hb * T + c * 256
                v.tensor_single_scalar(elu_sc[:, hb * 256:(hb + 1) * 256],
                                       q_fm[:, col:col + 256], 0.0, ALU.min)
            vS.inc(v.engine_nop())
        v_min = vS.v

        @blk.scalar
        def _(s, v_min=v_min):
            wt(s, vS, v_min)
            for hb in range(NHD):
                ins = s.activation(elu_sc[:, 1024 + hb * 256:1024 + (hb + 1) * 256],
                                   elu_sc[:, hb * 256:(hb + 1) * 256], AF.Exp)
            sS.inc(ins)
        s_eexp = sS.v
        s_qexp[c] = s_eexp

        @blk.vector
        def _(v, c=c, s_eexp=s_eexp):
            wt(v, sS, s_eexp)
            for hb in range(NHD):
                col = hb * T + c * 256
                v.tensor_single_scalar(elu_sc[:, 2048 + hb * 256:2048 + (hb + 1) * 256],
                                       q_fm[:, col:col + 256], 0.0, ALU.max)
            v.drain()
            for hb in range(NHD):
                col = hb * T + c * 256
                v.tensor_tensor(q_fm[:, col:col + 256],
                                elu_sc[:, 1024 + hb * 256:1024 + (hb + 1) * 256],
                                elu_sc[:, 2048 + hb * 256:2048 + (hb + 1) * 256],
                                ALU.add)
            vS.inc(v.engine_nop())
    v_sq_done = vS.v

    dump_reg.update(att_tm=att_tm[:], sq_fm=q_fm[:], ssum=ssum[:],
                    rsum=rsum[:], e_sb=e_sb[:])
    if stop == "p3b" or stop.startswith("p3b:"):
        return _finish()

    dg_ag3 = [None] * N_CORES
    v_stt = [0] * N_CORES

    @blk.vector
    def _(v):
        v.memset(corr[0:64, 0:520], 0.0)
        v.drain()
        ca = corr[0:64, 0:520].rearrange("p (a b) -> p a b", b=65)
        v.memset(ca[:, :, 64:65], 1.0 / 64.0)
        vS.inc(v.drain())
    v_corr0 = vS.v

    for cc in range(N_CORES):
        @blk.gpsimd
        def _(g, cc=cc):
            if cc == 0:
                wt(g, cS, c_mem)
            if cc >= 2:
                wt(g, vS, v_stt[cc - 2])
            sem = ags[cc % 2]
            g.dma_start(out=agt[0:64, (cc % 2) * 520:(cc % 2 + 1) * 520],
                        in_=ag3_v[cc:cc + 1].rearrange("r p q -> (r p) q")
                        ).then_inc(sem.h, 16)
            sem.v += 16
            dg_ag3[cc] = (sem, sem.v)

        @blk.vector
        def _(v, cc=cc):
            wt(v, *dg_ag3[cc])
            src = (cc % 2) * 520
            d = ((cc + 1) % 2) * 520
            v.scalar_tensor_tensor(
                corr[0:64, d:d + 520], agt[0:64, src:src + 520],
                mask_sb[0:64, cc:cc + 1], corr[0:64, (cc % 2) * 520:(cc % 2) * 520 + 520],
                ALU.mult, ALU.add)
            vS.inc(v.drain())
            v_stt[cc] = vS.v

    @blk.vector
    def _(v):
        wt(v, vS, v_snap[NCH])
        mz = memz[0:64, :].rearrange("p (a b) -> p a b", b=9 * 65)
        for c in range(NCH):
            v.tensor_tensor(
                memz_bf[0:64, c * 520:(c + 1) * 520].rearrange(
                    "p (a b) -> p a b", b=65),
                mz[:, 0:8, c * 65:(c + 1) * 65],
                corr[0:64, 0:520].rearrange("p (a b) -> p a b", b=65),
                ALU.add)
        vS.inc(v.engine_nop())
    v_corr = vS.v

    @blk.sync
    def _(s):
        wt(s, vS, v_corr)
        s.dma_start(out=memz_bf[64:128, :], in_=memz_bf[0:64, :]).then_inc(dX.h, 16)
        dX.v += 16
    dx_dup = dX.v

    dump_reg.update(memz_bf=memz_bf[:], corr=corr[:])
    if stop == "p5a":
        return _finish()

    x2_ld = [None] * NT

    @blk.sync
    def _(s):
        for tb in range(2):
            sem = x2s[tb % 2]
            s.dma_start(out=x2_sb[:, tb * D:(tb + 1) * D],
                        in_=x_in[tb * 128:(tb + 1) * 128, :]).then_inc(sem.h, 16)
            sem.v += 16
            x2_ld[tb] = (sem, sem.v)

    dx_out = [None] * NT
    p5b_lim = NT
    p5b_step = "st"
    if stop.startswith("p5b:"):
        parts = stop.split(":")
        p5b_lim = int(parts[1])
        p5b_step = parts[2] if len(parts) > 2 else "st"
    for c in range(NCH):
        for qh in range(2):
            tb = 2 * c + qh
            if tb >= p5b_lim:
                continue
            last = (tb == p5b_lim - 1) and stop.startswith("p5b:")
            mb0 = 0
            for par in range(2):
                b = mb0 + par

                @blk.tensor
                def _(t, c=c, qh=qh, par=par, b=b):
                    wt(t, vS, v_sq_done)
                    wt(t, dX, dx_dup)
                    wr_bank(t, b)
                    pr = par * 64
                    for hh in range(4):
                        h = 2 * hh + par
                        qcol = (h // 2) * T + c * 256 + qh * 128
                        ins = t.matmul(
                            pb[b][:][:, hh * 72:hh * 72 + 65],
                            q_fm[pr:pr + 64, qcol:qcol + 128],
                            memz_bf[pr:pr + 64, c * 520 + h * 65:c * 520 + (h + 1) * 65],
                            start=True, stop=True)
                    tS.inc(ins)
                bank_w[b] = (tS, tS.v)
            t_mem = tS.v
            if last and p5b_step.startswith("mem"):
                continue

            @blk.vector
            def _(v, tb=tb, mb0=mb0, t_mem=t_mem):
                wt(v, tS, t_mem)
                for par in range(2):
                    b = mb0 + par
                    zview = pb[b][:][:, 0:288].rearrange(
                        "p (a b) -> p a b", b=72)[:, :, 64:65]
                    dst = scal[:, 21 + par * 4:25 + par * 4].rearrange(
                        "p (a b) -> p a b", b=1)
                    v.reciprocal(dst, zview)
                v.drain()
                for par in range(2):
                    b = mb0 + par
                    for hh in range(4):
                        h = 2 * hh + par
                        v.tensor_scalar_mul(mix_sc[:, h * 64:(h + 1) * 64],
                                            pb[b][:][:, hh * 72:hh * 72 + 64],
                                            scal[:, 21 + par * 4 + hh:22 + par * 4 + hh])
                wt(v, sS, s_att_evac[tb])
                wt(v, vS, v_gate)
                v.tensor_tensor(mix_sc[:, HD:2 * HD],
                                att_tm[:, tb * HD:(tb + 1) * HD], gate1m[:],
                                ALU.mult)
                v.tensor_tensor(mix_sc[:, 2 * HD:3 * HD], mix_sc[:, 0:HD],
                                gate_sb[:], ALU.mult)
                v.drain()
                vS.inc(v.tensor_tensor(att_tm[:, tb * HD:(tb + 1) * HD],
                                       mix_sc[:, HD:2 * HD],
                                       mix_sc[:, 2 * HD:3 * HD], ALU.add))
            bank_r[mb0] = (vS, vS.v)
            bank_r[mb0 + 1] = (vS, vS.v)
            v_mix = vS.v
            if last and p5b_step == "mix":
                continue

            b = 4 + (tb % 2)

            @blk.tensor
            def _(t, tb=tb, b=b, v_mix=v_mix):
                wt(t, vS, v_mix)
                wr_bank(t, b)
                for j in range(NHD):
                    ins = t.transpose(pbb[b][:, j * 128:(j + 1) * 128],
                                      att_tm[:, tb * HD + j * 128:tb * HD + (j + 1) * 128],
                                      ident_bf[:])
                tS.inc(ins)
            bank_w[b] = (tS, tS.v)
            t_atr = tS.v

            @blk.vector
            def _(v, tb=tb, b=b, t_atr=t_atr):
                wt(v, tS, t_atr)
                dst = att_fm[:, 0:4 * T].rearrange(
                    "p (a b) -> p a b", b=T)[:, :, tb * 128:(tb + 1) * 128]
                src = pbb[b][:, 0:512].rearrange("p (a b) -> p a b", b=128)
                vS.inc(v.tensor_copy(dst, src))
            bank_r[b] = (vS, vS.v)
            v_afm = vS.v
            if last and p5b_step == "tr":
                continue

            wob = (6, 7) if tb % 2 == 0 else (2, 3)

            @blk.tensor
            def _(t, tb=tb, wob=wob, v_afm=v_afm):
                wt(t, vS, v_afm)
                wt(t, dG, dg_setup)
                wr_bank(t, wob[0])
                wr_bank(t, wob[1])
                for nb in range(2):
                    for hb in range(NHD):
                        ins = t.matmul(
                            pb[wob[nb]][:],
                            att_fm[:, hb * T + tb * 128:hb * T + (tb + 1) * 128],
                            wo_bf[:, hb * 1024 + nb * 512:hb * 1024 + (nb + 1) * 512],
                            start=(hb == 0), stop=(hb == NHD - 1))
                tS.inc(ins)
            bank_w[wob[0]] = (tS, tS.v)
            bank_w[wob[1]] = (tS, tS.v)
            t_wo = tS.v
            if last and p5b_step == "wo":
                continue

            @blk.vector
            def _(v, tb=tb, wob=wob, t_wo=t_wo):
                wt(v, tS, t_wo)
                wt(v, *x2_ld[tb])
                if tb >= 2:
                    wt(v, *dx_out[tb - 2])
                sl = (tb % 2) * D
                v.tensor_tensor(stage[:, sl:sl + 512], pb[wob[0]][:],
                                x2_sb[:, sl:sl + 512], ALU.add)
                vS.inc(v.tensor_tensor(stage[:, sl + 512:sl + D], pb[wob[1]][:],
                                       x2_sb[:, sl + 512:sl + D], ALU.add))
            bank_r[wob[0]] = (vS, vS.v)
            bank_r[wob[1]] = (vS, vS.v)
            v_stage = vS.v

            @blk.sync
            def _(s, tb=tb, v_stage=v_stage):
                wt(s, vS, v_stage)
                sl = (tb % 2) * D
                sem = osm[tb % 2]
                s.dma_start(out=out[tb * 128:(tb + 1) * 128, :],
                            in_=stage[:, sl:sl + D]).then_inc(sem.h, 16)
                sem.v += 16
                dx_out[tb] = (sem, sem.v)
                if tb + 2 < NT:
                    nb = tb + 2
                    sem2 = x2s[nb % 2]
                    s.dma_start(out=x2_sb[:, (nb % 2) * D:(nb % 2 + 1) * D],
                                in_=x_in[nb * 128:(nb + 1) * 128, :]).then_inc(sem2.h, 16)
                    sem2.v += 16
                    x2_ld[nb] = (sem2, sem2.v)
    dx_attn_out = [(osm[0], osm[0].v), (osm[1], osm[1].v),
                   (x2s[0], x2s[0].v), (x2s[1], x2s[1].v)]

    def wt_attn_out(eng):
        for sem, vv in dx_attn_out:
            if vv:
                eng.wait_ge(sem.h, vv)

    dump_reg.update(att_mix=att_tm[:])
    if stop == "p5b" or stop.startswith("p5b:"):
        return _finish()

    for t in reversed(ga2_cm):
        t.__exit__(None, None, None)
        _stack.remove(t)
    for t in reversed(ga1_cm):
        t.__exit__(None, None, None)
        _stack.remove(t)

    v_bar = vS.v
    s_bar = sS.v
    t_bar = tS.v

    p4_cm = [
        nc.sbuf_tensor("h_hg", [128, HG * I_FFN], F32),
        nc.sbuf_tensor("hq_i8", [128, I_FFN], I8),
        nc.sbuf_tensor("hq_bf", [128, I_FFN], BF16),
        nc.sbuf_tensor("hq_fm", [128, 2 * I_FFN], BF16),
        nc.sbuf_tensor("xq_fm", [128, HG * ND * 128], BF16),
        nc.sbuf_tensor("xqbf_t", [128, D], BF16),
        nc.sbuf_tensor("wgu_buf", [128, 4 * I_FFN], BF16),
        nc.sbuf_tensor("wd_buf", [128, 2 * 4 * 1024], BF16),
        nc.sbuf_tensor("hsc", [128, 4 * 512], F32),
        nc.sbuf_tensor("hsc2", [128, 512], F32),
        nc.sbuf_tensor("sfin", [128, HG], F32),
        nc.sbuf_tensor("p4mx", [128, HG], F32),
    ]
    (h_hg, hq_i8, hq_bf, hq_fm, xq_fm, xqbf_t, wgu_buf, wd_buf, hsc, hsc2,
     sfin, p4mx) = [t.__enter__() for t in p4_cm]
    _stack.extend(p4_cm)

    NOC = 8
    ds_slab = {}
    t_guslab_done = {}
    hq_state = {"t_hqbf_readers": 0, "t_down_prev": 0}
    ds_wd = {}
    t_wd_done = {}
    dg_out = {}

    def emit_gu_dma(hg, oc, mat):
        sl = ((oc % 2) * 2 + mat) * I_FFN

        @blk.scalar
        def _(s, hg=hg, oc=oc, mat=mat, sl=sl):
            wt(s, cS, c_tern)
            wt(s, vS, v_bar)
            wt_attn_out(s)
            key = (hg, oc, mat)
            gi = hg * NOC + oc
            if gi >= 2:
                pkey = ((gi - 2) // NOC, (gi - 2) % NOC, mat)
                assert pkey in t_guslab_done, pkey
                wt(s, tS, t_guslab_done[pkey])
            src = ag2_v[0:8, :, mat * I_FFN + oc * 512:mat * I_FFN + (oc + 1) * 512
                        ].rearrange("b p c -> p b c")
            dst = wgu_buf[:, sl:sl + I_FFN].rearrange("p (b c) -> p b c", b=8)
            sem = gus[(oc % 2) * 2 + mat] if len(gus) == 4 else gus[oc % 2]
            s.dma_start(out=dst, in_=src).then_inc(sem.h, 16)
            sem.v += 16
            ds_slab[key] = (sem, sem.v)

    v_hevac = {}
    v_h_done = {}

    for hg in range(NHG):
        v_xqfm = {}
        for ti in range(HG):
            tb = hg * HG + ti

            @blk.vector
            def _(v, tb=tb):
                wt(v, vS, v_bar)
                if hg > 0 or ti > 0:
                    wt(v, tS, hq_state["t_hqbf_readers"] or t_bar)
                vS.inc(v.tensor_copy(xqbf_t[:], xq_tm[:, tb * D:(tb + 1) * D]))
            v_cast = vS.v

            for w in range(2):
                b = w

                @blk.tensor
                def _(t, ti=ti, w=w, b=b, v_cast=v_cast):
                    wt(t, vS, v_cast)
                    wt(t, sS, s_bar)
                    wr_bank(t, b)
                    for j in range(4):
                        ins = t.transpose(pbb[b][:, j * 128:(j + 1) * 128],
                                          xqbf_t[:, (w * 4 + j) * 128:(w * 4 + j + 1) * 128],
                                          ident_bf[:])
                    tS.inc(ins)
                bank_w[b] = (tS, tS.v)
                t_tr = tS.v

                @blk.vector
                def _(v, ti=ti, w=w, b=b, t_tr=t_tr):
                    wt(v, tS, t_tr)
                    dst = xq_fm[:, (ti * ND + w * 4) * 128:(ti * ND + w * 4 + 4) * 128]
                    vS.inc(v.tensor_copy(dst, pbb[b][:, 0:512]))
                bank_r[b] = (vS, vS.v)
            v_xqfm[ti] = vS.v
            hq_state["t_hqbf_readers"] = tS.v

        emit_gu_dma(hg, 0, 0)
        emit_gu_dma(hg, 0, 1)
        for oc in range(NOC):
            if oc + 1 < NOC:
                emit_gu_dma(hg, oc + 1, 0)
                emit_gu_dma(hg, oc + 1, 1)
            for mat in range(2):
                sl = ((oc % 2) * 2 + mat) * I_FFN
                for ti in range(HG):
                    b = mat * 4 + ti

                    @blk.tensor
                    def _(t, hg=hg, oc=oc, mat=mat, ti=ti, b=b, sl=sl):
                        wt(t, vS, v_xqfm[ti])
                        wt(t, *ds_slab[(hg, oc, mat)])
                        wr_bank(t, b)
                        for db in range(ND):
                            ins = t.matmul(
                                pb[b][:][:],
                                xq_fm[:, (ti * ND + db) * 128:(ti * ND + db + 1) * 128],
                                wgu_buf[:, sl + db * 512:sl + (db + 1) * 512],
                                start=(db == 0), stop=(db == ND - 1))
                        tS.inc(ins)
                    bank_w[b] = (tS, tS.v)
                    if ti == HG - 1:
                        t_guslab_done[(hg, oc, mat)] = tS.v

            for ti in range(HG):
                @blk.vector
                def _(v, hg=hg, oc=oc, ti=ti):
                    wt(v, bank_w[ti][0], bank_w[ti][1])
                    wt(v, bank_w[4 + ti][0], bank_w[4 + ti][1])
                    hs = hsc[:, 0:512]
                    hs2 = hsc2[:, 0:512]
                    v.tensor_single_scalar(hs, pb[ti][:], 0.0, ALU.max)
                    v.tensor_tensor(hs2, hs, hs, ALU.mult)
                    vS.inc(v.tensor_tensor(
                        h_hg[:, ti * I_FFN + oc * 512:ti * I_FFN + (oc + 1) * 512],
                        hs2, pb[4 + ti][:], ALU.mult))
                bank_r[ti] = (vS, vS.v)
                bank_r[4 + ti] = (vS, vS.v)
                v_hevac[(hg, oc, ti)] = vS.v

        for sub in range(2):
            def emit_wd_dma(hg, sub, islab):
                sl = (islab % 2) * 4 * 1024

                @blk.scalar
                def _(s, hg=hg, sub=sub, islab=islab, sl=sl):
                    wt(s, cS, c_tern)
                    wt(s, vS, v_bar)
                    wt_attn_out(s)
                    idx = (hg * 2 + sub) * 8 + islab
                    if idx >= 2:
                        p = idx - 2
                        pkey = (p // 16, (p // 8) % 2, p % 8)
                        assert pkey in t_wd_done, pkey
                        wt(s, tS, t_wd_done[pkey])
                    sem = wds[islab % 2]
                    src = ag2_v[islab:islab + 1, :,
                                2 * I_FFN:2 * I_FFN + 4096].rearrange(
                                    "r p q -> (r p) q")
                    s.dma_start(out=wd_buf[:, sl:sl + 4096],
                                in_=src).then_inc(sem.h, 16)
                    sem.v += 16
                    ds_wd[(hg, sub, islab)] = (sem, sem.v)
            emit_wd_dma(hg, sub, 0)
            emit_wd_dma(hg, sub, 1)

            for tt in range(2):
                ti = sub * 2 + tt
                tb = hg * HG + ti

                @blk.vector
                def _(v, hg=hg, ti=ti, tb=tb):
                    wt(v, vS, v_hevac[(hg, NOC - 1, ti)])
                    if hq_state.get("s_hq_prev"):
                        wt(v, sS, hq_state["s_hq_prev"])
                    v.reduce_max(scal[:, 23:24], h_hg[:, ti * I_FFN:(ti + 1) * I_FFN],
                                 axis=AX, apply_absolute_value=True)
                    v.drain()
                    mx = mxc_all[:, tb:tb + 1]
                    v.scalar_tensor_tensor(scal[:, 25:26], mx, mx, mx,
                                           ALU.mult, ALU.mult)
                    v.drain()
                    v.scalar_tensor_tensor(scal[:, 26:27], scal[:, 25:26],
                                           kc[:, 0:1], scal[:, 23:24],
                                           ALU.mult, ALU.mult)
                    v.tensor_tensor(scal[:, 25:26], scal[:, 25:26], kc[:, 0:1],
                                    ALU.mult)
                    v.drain()
                    v.tensor_scalar_max(scal[:, 27:28], scal[:, 26:27], EPS)
                    v.drain()
                    v.reciprocal(scal[:, 29:30], scal[:, 27:28])
                    v.tensor_tensor(sfin[:, ti:ti + 1], scal[:, 27:28], kc[:, 1:2],
                                    ALU.mult)
                    v.drain()
                    v.scalar_tensor_tensor(scal[:, 31:32], scal[:, 29:30],
                                           127.0, scal[:, 25:26],
                                           ALU.mult, ALU.mult)
                    vS.inc(v.engine_nop())
                v_sh = vS.v
                hq_state["g_mx_prev"] = (vS, v_sh)

                @blk.scalar
                def _(s, ti=ti, v_sh=v_sh):
                    wt(s, vS, v_sh)
                    sS.inc(s.activation(hq_i8[:], h_hg[:, ti * I_FFN:(ti + 1) * I_FFN],
                                        AF.Identity, scale=scal[:, 31:32]))
                s_hq = sS.v
                hq_state["s_hq_prev"] = s_hq

                @blk.vector
                def _(v, s_hq=s_hq):
                    wt(v, sS, s_hq)
                    wt(v, tS, hq_state["t_hqbf_readers"])
                    vS.inc(v.tensor_copy(hq_bf[:], hq_i8[:]))
                v_hqbf = (vS, vS.v)

                for w in range(8):
                    b = w % 2

                    @blk.tensor
                    def _(t, w=w, b=b, v_hqbf=v_hqbf):
                        wt(t, *v_hqbf)
                        wr_bank(t, b)
                        for j in range(4):
                            ins = t.transpose(pbb[b][:, j * 128:(j + 1) * 128],
                                              hq_bf[:, (w * 4 + j) * 128:(w * 4 + j + 1) * 128],
                                              ident_bf[:])
                        tS.inc(ins)
                    bank_w[b] = (tS, tS.v)
                    t_tr = tS.v

                    @blk.vector
                    def _(v, tt=tt, w=w, b=b, t_tr=t_tr):
                        wt(v, tS, t_tr)
                        wt(v, tS, hq_state["t_down_prev"])
                        dst = hq_fm[:, tt * I_FFN + w * 512:tt * I_FFN + (w + 1) * 512]
                        vS.inc(v.tensor_copy(dst, pbb[b][:, 0:512]))
                    bank_r[b] = (vS, vS.v)
                hq_state["t_hqbf_readers"] = tS.v
                v_h_done[(hg, ti)] = vS.v

            for islab in range(8):
                sl = (islab % 2) * 4 * 1024

                @blk.tensor
                def _(t, hg=hg, sub=sub, islab=islab, sl=sl):
                    wt(t, *ds_wd[(hg, sub, islab)])
                    for tt in range(2):
                        wt(t, vS, v_h_done[(hg, sub * 2 + tt)])
                        if islab == 0:
                            wr_bank(t, 4 + 2 * tt)
                            wr_bank(t, 5 + 2 * tt)
                    for ib in range(4):
                        iblk = islab * 4 + ib
                        for tt in range(2):
                            for nb in range(2):
                                ins = t.matmul(
                                    pb[4 + 2 * tt + nb][:],
                                    hq_fm[:, tt * I_FFN + iblk * 128:tt * I_FFN + (iblk + 1) * 128],
                                    wd_buf[:, sl + ib * 1024 + nb * 512:sl + ib * 1024 + (nb + 1) * 512],
                                    start=(iblk == 0), stop=(iblk == NI - 1))
                    tS.inc(ins)
                t_wd_done[(hg, sub, islab)] = tS.v
                if islab + 2 < 8:
                    emit_wd_dma(hg, sub, islab + 2)
                if islab == 7:
                    for bb in (4, 5, 6, 7):
                        bank_w[bb] = (tS, tS.v)
            t_dn = tS.v
            hq_state["t_down_prev"] = t_dn

            for tt in range(2):
                ti = sub * 2 + tt
                tb = hg * HG + ti

                @blk.scalar
                def _(s, tb=tb, ti=ti, tt=tt, t_dn=t_dn):
                    wt(s, tS, t_dn)
                    wt_attn_out(s)
                    if tb >= 2 and (tb - 2) in dg_out:
                        wt(s, *dg_out[tb - 2])
                    sl = (tb % 2) * D
                    s.activation(stage[:, sl:sl + 512], pb[4 + 2 * tt][:], AF.Identity,
                                 scale=sfin[:, ti:ti + 1])
                    sS.inc(s.activation(stage[:, sl + 512:sl + D], pb[5 + 2 * tt][:],
                                        AF.Identity, scale=sfin[:, ti:ti + 1]))
                bank_r[4 + 2 * tt] = (sS, sS.v)
                bank_r[5 + 2 * tt] = (sS, sS.v)
                s_dn = sS.v

                @blk.gpsimd
                def _(g, tb=tb, s_dn=s_dn):
                    wt(g, sS, s_dn)
                    wt_attn_out(g)
                    sem = asm_[tb % 2]
                    sl = (tb % 2) * D
                    g.dma_start(out=out[tb * 128:(tb + 1) * 128, :],
                                in_=stage[:, sl:sl + D],
                                accum_op=ALU.add).then_inc(sem.h, 16)
                    sem.v += 16
                    dg_out[tb] = (sem, sem.v)

    dump_reg.update(h_hg=h_hg[:], hq_fm=hq_fm[:], sfin=sfin[:], kc=kc[:],
                    xq_fm=xq_fm[:], wgu=wgu_buf[:], wd=wd_buf[:])
    dump_reg.pop("memz", None)
    dump_reg.pop("att_tm", None)
    dump_reg.pop("sq_fm", None)
    return _finish()


_NC_CACHE = None
_STOP = "all"
_DEBUG = False


def kernel(L0_attn_x, L0_attn_Wq, L0_attn_Wk, L0_attn_Wv, L0_attn_betas,
           L0_attn_Wo, L0_ffn_w_gate, L0_ffn_w_up, L0_ffn_w_down):
    global _NC_CACHE
    x = np.ascontiguousarray(np.asarray(L0_attn_x, np.float32).reshape(16384, D))
    wq = np.ascontiguousarray(np.asarray(L0_attn_Wq, np.float32))
    wk = np.ascontiguousarray(np.asarray(L0_attn_Wk, np.float32))
    wv = np.ascontiguousarray(np.asarray(L0_attn_Wv, np.float32))
    wo = np.ascontiguousarray(np.asarray(L0_attn_Wo, np.float32))
    betas = np.asarray(L0_attn_betas, np.float32).reshape(HD)
    betas_bc = np.ascontiguousarray(np.broadcast_to(betas[None, :], (128, HD)))
    wgT = np.ascontiguousarray(np.asarray(L0_ffn_w_gate, np.float32).T)
    wuT = np.ascontiguousarray(np.asarray(L0_ffn_w_up, np.float32).T)
    wdT = np.ascontiguousarray(np.asarray(L0_ffn_w_down, np.float32).T)

    if _NC_CACHE is None:
        _NC_CACHE = build_nc(stop=_STOP, debug=_DEBUG)
    nc = _NC_CACHE

    in_maps = []
    for c in range(N_CORES):
        mask = np.ascontiguousarray(
            np.broadcast_to((np.arange(N_CORES) < c).astype(np.float32)[None, :],
                            (128, N_CORES)))
        in_maps.append({
            "x": np.ascontiguousarray(x[c * T:(c + 1) * T]),
            "wq": wq, "wk": wk, "wv": wv, "wo": wo,
            "betas_bc": betas_bc,
            "wg_slice": np.ascontiguousarray(wgT[c * 128:(c + 1) * 128]),
            "wu_slice": np.ascontiguousarray(wuT[c * 128:(c + 1) * 128]),
            "wd_slice": np.ascontiguousarray(wdT[c * 512:(c + 1) * 512]),
            "mask_bc": mask,
        })
    res = run_bass_kernel_spmd(nc, in_maps, core_ids=list(range(N_CORES)))
    if _DEBUG:
        global _LAST_RES
        _LAST_RES = res
    outp = np.concatenate([r["out"] for r in res.results], axis=0)
    return outp.reshape(1, 16384, D)


# version 28
# speedup vs baseline: 1.0283x; 1.0053x over previous
import numpy as np
import concourse.bass as bass
import concourse.mybir as mybir
from concourse.bass_utils import run_bass_kernel_spmd

F32 = mybir.dt.float32
BF16 = mybir.dt.bfloat16
I8 = mybir.dt.int8
AF = mybir.ActivationFunctionType
ALU = mybir.AluOpType
AX = mybir.AxisListType.X

T = 2048
NT = 16
D = 1024
ND = 8
HD = 512
NHD = 4
H = 8
L = 256
NCH = 8
I_FFN = 4096
NI = 32
N_CORES = 8
EPS = 1e-5
QPL = 12288
HG = 4
NHG = NT // HG
WCNT = 1024 * 4096


class Ctr:
    def __init__(self, nc, name):
        self.cm = nc.semaphore(name)
        self.h = self.cm.__enter__()
        self.v = 0

    def inc(self, instr, n=1):
        instr.then_inc(self.h, n)
        self.v += n
        return instr


def build_nc(stop="all", debug=False):
    nc = bass.Bass("TRN2", target_bir_lowering=False, debug=False)
    _stack = []
    dump_reg = {}

    x_in = nc.dram_tensor("x", [T, D], F32, kind="ExternalInput").ap()
    wq_in = nc.dram_tensor("wq", [D, HD], F32, kind="ExternalInput").ap()
    wk_in = nc.dram_tensor("wk", [D, HD], F32, kind="ExternalInput").ap()
    wv_in = nc.dram_tensor("wv", [D, HD], F32, kind="ExternalInput").ap()
    wo_in = nc.dram_tensor("wo", [HD, D], F32, kind="ExternalInput").ap()
    betas_in = nc.dram_tensor("betas_bc", [128, HD], F32, kind="ExternalInput").ap()
    wg_sl = nc.dram_tensor("wg_slice", [128, I_FFN], F32, kind="ExternalInput").ap()
    wu_sl = nc.dram_tensor("wu_slice", [128, I_FFN], F32, kind="ExternalInput").ap()
    wd_sl = nc.dram_tensor("wd_slice", [512, D], F32, kind="ExternalInput").ap()
    mask_in = nc.dram_tensor("mask_bc", [128, N_CORES], F32, kind="ExternalInput").ap()
    out = nc.dram_tensor("out", [T, D], F32, kind="ExternalOutput").ap()

    ag1_in = nc.dram_tensor("ag1_in", [1, 8], F32).ap()
    ag1_out = nc.dram_tensor("ag1_out", [N_CORES, 8], F32).ap()
    ag2_in = nc.dram_tensor("ag2_in", [1, 128 * QPL], BF16).ap()
    ag2_out = nc.dram_tensor("ag2_out", [N_CORES, 128 * QPL], BF16).ap()
    ag3_in = nc.dram_tensor("ag3_in", [1, 64 * 520], F32).ap()
    ag3_out = nc.dram_tensor("ag3_out", [N_CORES, 64 * 520], F32).ap()

    ag2_v = ag2_out.rearrange("r (p q) -> r p q", p=128)
    ag2i_v = ag2_in.rearrange("r (p q) -> r p q", p=128)
    ag3_v = ag3_out.rearrange("r (p q) -> r p q", p=64)

    dX = Ctr(nc, "dX")
    dS = Ctr(nc, "dS")
    dG = Ctr(nc, "dG")
    xs = [Ctr(nc, "xs0"), Ctr(nc, "xs1"), Ctr(nc, "xs2"), Ctr(nc, "xs3")]
    ws = [Ctr(nc, "ws0"), Ctr(nc, "ws1")]
    x2s = [Ctr(nc, "x2s0"), Ctr(nc, "x2s1")]
    osm = [Ctr(nc, "osm0"), Ctr(nc, "osm1")]
    asm_ = [Ctr(nc, "asm0"), Ctr(nc, "asm1")]
    gus = [Ctr(nc, "gus0"), Ctr(nc, "gus1"), Ctr(nc, "gus2"),
           Ctr(nc, "gus3")]
    wds = [Ctr(nc, "wds0"), Ctr(nc, "wds1")]
    ags = [Ctr(nc, "ags0"), Ctr(nc, "ags1")]
    tS = Ctr(nc, "tS")
    vS = Ctr(nc, "vS")
    sS = Ctr(nc, "sS")
    gS = Ctr(nc, "gS")
    cS = Ctr(nc, "cS")

    def wt(eng, ctr, val=None):
        v = ctr.v if val is None else val
        assert v <= ctr.v, "wait on a future value"
        if v > 0:
            eng.wait_ge(ctr.h, v)

    psum_cms = [nc.psum_tensor(f"pb{i}", [128, 512], F32) for i in range(8)]
    pb = [p.__enter__() for p in psum_cms]
    _stack.extend(psum_cms)
    pbb = [p[:].bitcast(BF16) for p in pb]
    bank_w = [None] * 8
    bank_r = [None] * 8

    def wr_bank(eng, b):
        if bank_r[b] is not None:
            wt(eng, *bank_r[b])

    pers_cm = [
        nc.sbuf_tensor("ident_bf", [128, 128], BF16),
        nc.sbuf_tensor("ones_col", [128, 1], F32),
        nc.sbuf_tensor("ones_row", [1, 128], F32),
        nc.sbuf_tensor("gate_sb", [128, HD], F32),
        nc.sbuf_tensor("gate1m", [128, HD], F32),
        nc.sbuf_tensor("wo_bf", [128, NHD * 1024], BF16),
        nc.sbuf_tensor("xq_tm", [128, NT * D], I8),
        nc.sbuf_tensor("mxc_all", [128, NT], F32),
        nc.sbuf_tensor("mbc", [128, 8], F32),
        nc.sbuf_tensor("mask_sb", [128, N_CORES], F32),
        nc.sbuf_tensor("stage", [128, 2 * D], F32),
        nc.sbuf_tensor("scal", [128, 32], F32),
        nc.sbuf_tensor("srow", [1, 128], F32),
        nc.sbuf_tensor("epsc", [128, 1], F32),
        nc.sbuf_tensor("negone", [128, 1], F32),
        nc.sbuf_tensor("kc", [128, 4], F32),
    ]
    (ident_bf, ones_col, ones_row, gate_sb, gate1m, wo_bf, xq_tm, mxc_all, mbc,
     mask_sb, stage, scal, srow, epsc, negone, kc) = [t.__enter__() for t in pers_cm]
    _stack.extend(pers_cm)

    ga1_cm = [
        nc.sbuf_tensor("q_fm", [128, NHD * T], BF16),
        nc.sbuf_tensor("k_fm", [128, NHD * T], BF16),
        nc.sbuf_tensor("v_tm", [128, NT * 520], BF16),
    ]
    q_fm, k_fm, v_tm = [t.__enter__() for t in ga1_cm]
    _stack.extend(ga1_cm)

    gq_cm = [
        nc.sbuf_tensor("wsl", [128, 2 * 2048], F32),
        nc.sbuf_tensor("q_tmp", [128, 2048], F32),
        nc.sbuf_tensor("q_i8", [128, 2 * 2048], I8),
        nc.sbuf_tensor("q_bf", [128, 2 * 2048], BF16),
    ]
    wsl, q_tmp, q_i8, q_bf = [t.__enter__() for t in gq_cm]
    _stack.extend(gq_cm)

    NXR = 4
    gp1_cm = [
        nc.sbuf_tensor("x_sb", [128, NXR * D], F32),
        nc.sbuf_tensor("xn_sc", [128, 2 * D], F32),
        nc.sbuf_tensor("xnbf_sc", [128, D], BF16),
        nc.sbuf_tensor("p1st", [128, 48], F32),
    ]
    x_sb, xn_sc, xnbf_sc, p1st = [t.__enter__() for t in gp1_cm]
    _stack.extend(gp1_cm)

    xnfm_cm = [nc.sbuf_tensor("xn_fm", [128, ND * T], BF16)]
    xn_fm = xnfm_cm[0].__enter__()
    _stack.extend(xnfm_cm)

    gb_cm = [
        nc.sbuf_tensor("wq_bf", [128, ND * 512], BF16),
        nc.sbuf_tensor("wk_bf", [128, ND * 512], BF16),
        nc.sbuf_tensor("wv_bf", [128, ND * 512], BF16),
    ]
    wq_bf, wk_bf, wv_bf = [t.__enter__() for t in gb_cm]
    _stack.extend(gb_cm)

    blk_cm = nc.Block()
    blk = blk_cm.__enter__()

    all_dma_sems = [dX, dS, dG] + xs + ws + x2s + osm + asm_ + gus + wds + ags

    def _finish():
        if debug and dump_reg:
            @blk.sync
            def _(x):
                for ctr in [tS, vS, sS, gS, cS] + all_dma_sems:
                    if ctr.v:
                        x.wait_ge(ctr.h, ctr.v)
                for name, ap in dump_reg.items():
                    o = nc.dram_tensor("dbg_" + name, list(ap.shape),
                                       ap.dtype, kind="ExternalOutput")
                    x.dma_start(out=o.ap(), in_=ap).then_inc(dX.h, 16)
                    dX.v += 16

        @blk.sync
        def _(x):
            for sem in all_dma_sems:
                if sem.v:
                    x.wait_ge(sem.h, sem.v)
        blk_cm.__exit__(None, None, None)
        for t in reversed(_stack):
            t.__exit__(None, None, None)
        for ctr in [tS, vS, sS, gS, cS] + all_dma_sems:
            ctr.cm.__exit__(None, None, None)
        return nc

    @blk.gpsimd
    def _(g):
        g.memset(ident_bf[:], 0.0)
        g.affine_select(out=ident_bf[:], in_=ident_bf[:],
                        compare_op=ALU.not_equal, fill=1.0,
                        base=0, pattern=[[-1, 128]], channel_multiplier=1)
        g.memset(ones_col[:], 1.0)
        g.memset(epsc[:], EPS)
        g.memset(negone[:], -1.0)
        gS.inc(g.memset(ones_row[:], 1.0))
        g.dma_start(out=wo_bf[:].rearrange("p (b c) -> p b c", b=4),
                    in_=wo_in.rearrange("(b p) c -> p b c", p=128)).then_inc(dG.h, 16)
        g.dma_start(out=mask_sb[:], in_=mask_in[:]).then_inc(dG.h, 16)
        g.dma_start(out=gate_sb[:], in_=betas_in[:]).then_inc(dG.h, 16)
        g.dma_start(out=wq_bf[:].rearrange("p (b c) -> p b c", b=8),
                    in_=wq_in.rearrange("(b p) c -> p b c", p=128)).then_inc(dG.h, 16)
        g.dma_start(out=wk_bf[:].rearrange("p (b c) -> p b c", b=8),
                    in_=wk_in.rearrange("(b p) c -> p b c", p=128)).then_inc(dG.h, 16)
        g.dma_start(out=wv_bf[:].rearrange("p (b c) -> p b c", b=8),
                    in_=wv_in.rearrange("(b p) c -> p b c", p=128)).then_inc(dG.h, 16)
        dG.v += 96
    g_setup = gS.v
    dg_setup = dG.v
    dg_wqkv = dG.v

    x_ld = [None] * NT

    @blk.sync
    def _(s):
        for tb in range(NXR):
            sem = xs[tb % NXR]
            s.dma_start(out=x_sb[:, tb * D:(tb + 1) * D],
                        in_=x_in[tb * 128:(tb + 1) * 128, :]).then_inc(sem.h, 16)
            sem.v += 16
            x_ld[tb] = (sem, sem.v)

    v_stats = [0] * NT
    for tb in range(NT):
        xap = x_sb[:, (tb % NXR) * D:(tb % NXR + 1) * D]
        xsq = xn_sc[:, (tb % 2) * D:(tb % 2 + 1) * D]

        @blk.scalar
        def _(s, xap=xap, xsq=xsq, tb=tb):
            wt(s, *x_ld[tb])
            if tb >= 2:
                wt(s, vS, v_stats[tb - 2])
            sS.inc(s.activation(xsq, xap, AF.Square,
                                accum_out=scal[:, 20 + tb % 2:21 + tb % 2]))
        s_sq = sS.v

        @blk.vector
        def _(v, xap=xap, tb=tb, s_sq=s_sq):
            wt(v, *x_ld[tb])
            v.reduce_sum(scal[:, 28:29], xap, axis=AX)
            v.drain()
            wt(v, sS, s_sq)
            v.tensor_scalar_mul(p1st[:, tb:tb + 1], scal[:, 28:29], 1.0 / D)
            v.tensor_scalar_mul(scal[:, 30:31], scal[:, 20 + tb % 2:21 + tb % 2],
                                1.0 / D)
            v.drain()
            vS.inc(v.scalar_tensor_tensor(p1st[:, 16 + tb:17 + tb],
                                          p1st[:, tb:tb + 1], p1st[:, tb:tb + 1],
                                          scal[:, 30:31], ALU.mult, ALU.subtract))
        v_stats[tb] = vS.v

        if tb + NXR < NT:
            @blk.sync
            def _(s, tb=tb):
                wt(s, vS, v_stats[tb])
                nb = tb + NXR
                sem = xs[nb % NXR]
                s.dma_start(out=x_sb[:, (nb % NXR) * D:(nb % NXR + 1) * D],
                            in_=x_in[nb * 128:(nb + 1) * 128, :]).then_inc(sem.h, 16)
                sem.v += 16
                x_ld[nb] = (sem, sem.v)

    qunits = []
    for hh in range(2):
        qunits.append((wg_sl[:, hh * 2048:(hh + 1) * 2048], 0, hh * 2048, False))
    for hh in range(2):
        qunits.append((wu_sl[:, hh * 2048:(hh + 1) * 2048], 1, 4096 + hh * 2048,
                       False))
    for hh in range(2):
        src = wd_sl[hh * 256:(hh + 1) * 256, :].rearrange("(b p) c -> p b c", p=128)
        qunits.append((src, 2, 8192 + hh * 2048, True))

    v_q1 = [0] * len(qunits)
    ds_q1 = [0] * len(qunits)

    for i, (src, m, ofs, is3d) in enumerate(qunits):
        @blk.scalar
        def _(s, src=src, i=i, is3d=is3d):
            if i >= 2:
                wt(s, vS, v_q1[i - 2])
            dst = wsl[:, (i % 2) * 2048:(i % 2 + 1) * 2048]
            if is3d:
                dst = dst.rearrange("p (b c) -> p b c", b=2)
            sem = ws[i % 2]
            s.dma_start(out=dst, in_=src).then_inc(sem.h, 16)
            sem.v += 16
            ds_q1[i] = (sem, sem.v)

        @blk.vector
        def _(v, i=i):
            wt(v, *ds_q1[i])
            vS.inc(v.reduce_sum(scal[:, i:i + 1],
                                wsl[:, (i % 2) * 2048:(i % 2 + 1) * 2048],
                                axis=AX, apply_absolute_value=True))
            v_q1[i] = vS.v
    v_asum = vS.v

    @blk.tensor
    def _(t):
        wt(t, vS, v_asum)
        wt(t, gS, g_setup)
        tS.inc(t.matmul(pb[0][:][0:6, 0:1], scal[:, 0:6], ones_col[:],
                        start=True, stop=True))
    bank_w[0] = (tS, tS.v)

    @blk.vector
    def _(v):
        wt(v, tS, tS.v)
        vS.inc(v.tensor_copy(scal[0:6, 6:7], pb[0][:][0:6, 0:1]))
    bank_r[0] = (vS, vS.v)
    v_psums = vS.v

    @blk.gpsimd
    def _(g):
        wt(g, vS, v_psums)
        g.dma_start(out=ag1_in.rearrange("r c -> c r")[0:6],
                    in_=scal[0:6, 6:7]).then_inc(dG.h, 16)
        dG.v += 16
        wt(g, dG, dG.v)
        cS.inc(g.collective_compute(
            "AllGather", ALU.bypass, replica_groups=[list(range(N_CORES))],
            ins=[ag1_in.opt()], outs=[ag1_out.opt()]))
        wt(g, cS, cS.v)
        g.dma_start(out=srow[0:1, 0:64],
                    in_=ag1_out.rearrange("a b -> (a b)")).then_inc(dG.h, 16)
        dG.v += 16
    dg_ag1 = dG.v

    @blk.vector
    def _(v):
        wt(v, dG, dg_ag1)
        for i in range(4):
            v.tensor_tensor(srow[0:1, 64 + 6 * i:70 + 6 * i],
                            srow[0:1, 16 * i:16 * i + 6],
                            srow[0:1, 16 * i + 8:16 * i + 14], ALU.add)
        v.drain()
        for i in range(2):
            v.tensor_tensor(srow[0:1, 88 + 6 * i:94 + 6 * i],
                            srow[0:1, 64 + 12 * i:70 + 12 * i],
                            srow[0:1, 70 + 12 * i:76 + 12 * i], ALU.add)
        v.drain()
        v.tensor_tensor(srow[0:1, 100:106], srow[0:1, 88:94],
                        srow[0:1, 94:100], ALU.add)
        v.drain()
        sr2 = srow[0:1, 100:106].rearrange("p (a b) -> p a b", b=2)
        v.tensor_tensor(srow[0:1, 106:109], sr2[:, :, 0:1].rearrange(
            "p a b -> p (a b)"), sr2[:, :, 1:2].rearrange("p a b -> p (a b)"),
            ALU.add)
        v.drain()
        v.tensor_scalar(srow[0:1, 110:113], srow[0:1, 106:109], 1.0 / WCNT, EPS,
                        ALU.mult, ALU.max)
        vS.inc(v.engine_nop())
    v_means = vS.v

    @blk.tensor
    def _(t):
        wt(t, vS, v_means)
        wr_bank(t, 1)
        tS.inc(t.matmul(pb[1][:][:, 0:3], ones_row[:], srow[0:1, 110:113],
                        start=True, stop=True))
    bank_w[1] = (tS, tS.v)

    @blk.vector
    def _(v):
        wt(v, tS, tS.v)
        v.tensor_copy(mbc[:, 0:3], pb[1][:][:, 0:3])
        v.drain()
        v.tensor_tensor(mbc[:, 4:5], mbc[:, 0:1], mbc[:, 1:2], ALU.mult)
        v.drain()
        v.tensor_tensor(mbc[:, 3:4], mbc[:, 4:5], mbc[:, 0:1], ALU.mult)
        v.drain()
        v.tensor_copy(kc[:, 0:1], mbc[:, 3:4])
        v.tensor_scalar_mul(kc[:, 1:2], mbc[:, 2:3], 1.0 / 127.0)
        vS.inc(v.reciprocal(scal[:, 5:8], mbc[:, 0:3]))
    bank_r[1] = (vS, vS.v)
    v_mbc = vS.v

    if stop == "setup":
        return _finish()

    v_q2 = [0] * len(qunits)
    g_q2 = [0] * len(qunits)
    ds_q2 = [0] * len(qunits)
    for i, (src, m, ofs, is3d) in enumerate(qunits):
        @blk.scalar
        def _(s, src=src, i=i, is3d=is3d):
            wt(s, vS, v_q1[len(qunits) - 1] if i < 2 else v_q2[i - 2])
            dst = wsl[:, (i % 2) * 2048:(i % 2 + 1) * 2048]
            dst3 = dst.rearrange("p (b c) -> p b c", b=2) if is3d else dst
            sem = ws[i % 2]
            s.dma_start(out=dst3, in_=src).then_inc(sem.h, 16)
            sem.v += 16
            ds_q2[i] = (sem, sem.v)

        @blk.vector
        def _(v, i=i, m=m):
            wt(v, *ds_q2[i])
            wt(v, vS, v_mbc)
            if i >= 2:
                wt(v, gS, g_q2[i - 2])
            sl = (i % 2) * 2048
            v.tensor_scalar(q_tmp[:], wsl[:, sl:sl + 2048],
                            scal[:, 5 + m:6 + m], 1.27, ALU.mult, ALU.min)
            v.drain()
            vS.inc(v.tensor_scalar_max(q_i8[:, sl:sl + 2048], q_tmp[:], -1.27))
            v_q2[i] = vS.v

        @blk.gpsimd
        def _(g, i=i, ofs=ofs):
            wt(g, vS, v_q2[i])
            gS.inc(g.tensor_copy(q_bf[:, (i % 2) * 2048:(i % 2 + 1) * 2048],
                                 q_i8[:, (i % 2) * 2048:(i % 2 + 1) * 2048]))
            g_q2[i] = gS.v
            g.dma_start(out=ag2i_v[0:1, :, ofs:ofs + 2048].rearrange(
                            "r p q -> (r p) q"),
                        in_=q_bf[:, (i % 2) * 2048:(i % 2 + 1) * 2048]
                        ).then_inc(dG.h, 16)
            dG.v += 16
    dg_q = dG.v

    @blk.gpsimd
    def _(g):
        wt(g, dG, dg_q)
        cS.inc(g.collective_compute(
            "AllGather", ALU.bypass, replica_groups=[list(range(N_CORES))],
            ins=[ag2_in.opt()], outs=[ag2_out.opt()]))
    c_tern = cS.v

    if stop == "quant":
        return _finish()

    @blk.scalar
    def _(s):
        wt(s, vS, vS.v)
        wt(s, gS, g_setup)
        s.activation(scal[:, 0:16], p1st[:, 16:32], AF.Ln, bias=epsc[:],
                     scale=-1.0)
        s.drain()
        sS.inc(s.activation(p1st[:, 32:48], scal[:, 0:16], AF.Exp, scale=-0.5))
    s_rstd_all = sS.v

    x_ld2 = [None] * NT

    @blk.sync
    def _(s):
        for tb in range(NXR):
            wt(s, vS, v_stats[NT - NXR + tb])
            sem = xs[tb % NXR]
            s.dma_start(out=x_sb[:, (tb % NXR) * D:(tb % NXR + 1) * D],
                        in_=x_in[tb * 128:(tb + 1) * 128, :]).then_inc(sem.h, 16)
            sem.v += 16
            x_ld2[tb] = (sem, sem.v)

    @blk.gpsimd
    def _(g):
        vt = v_tm[:].rearrange("p (a b c) -> p a b c", b=H, c=65)
        gS.inc(g.memset(vt[:, :, :, 64:65], 1.0))
    g_vones = gS.v

    rr = [2]

    def next_bank():
        b = rr[0]
        rr[0] = 2 + (rr[0] - 2 + 1) % 6
        return b

    s_v_evac = [0] * NT

    def emit_qk(tg):
        for (w_bf, dst) in ((wq_bf, q_fm), (wk_bf, k_fm)):
            for hb in range(NHD):
                b = next_bank()

                @blk.tensor
                def _(t, w_bf=w_bf, hb=hb, tg=tg, b=b):
                    wt(t, vS, v_xnfm[tg * 4 + 3])
                    wt(t, dG, dg_wqkv)
                    wr_bank(t, b)
                    for db in range(ND):
                        ins = t.matmul(
                            pb[b][:][:],
                            w_bf[:, db * 512 + hb * 128:db * 512 + (hb + 1) * 128],
                            xn_fm[:, db * T + tg * 512:db * T + (tg + 1) * 512],
                            start=(db == 0), stop=(db == ND - 1))
                    tS.inc(ins)
                bank_w[b] = (tS, tS.v)
                t_mm = tS.v

                @blk.scalar
                def _(s, dst=dst, hb=hb, tg=tg, b=b, t_mm=t_mm):
                    wt(s, tS, t_mm)
                    sS.inc(s.copy(dst[:, hb * T + tg * 512:hb * T + (tg + 1) * 512],
                                  pb[b][:][:]))
                bank_r[b] = (sS, sS.v)

    def emit_v(tb):
        b = next_bank()

        @blk.tensor
        def _(t, tb=tb, b=b):
            wt(t, vS, v_xnfm[tb])
            wt(t, dG, dg_wqkv)
            wr_bank(t, b)
            for db in range(ND):
                ins = t.matmul(pb[b][:][:],
                               xn_fm[:, db * T + tb * 128:db * T + (tb + 1) * 128],
                               wv_bf[:, db * 512:(db + 1) * 512],
                               start=(db == 0), stop=(db == ND - 1))
            tS.inc(ins)
        bank_w[b] = (tS, tS.v)
        t_mm = tS.v

        @blk.vector
        def _(v, tb=tb, b=b, t_mm=t_mm):
            wt(v, tS, t_mm)
            wt(v, gS, g_vones)
            dst = v_tm[:, tb * 520:(tb + 1) * 520].rearrange(
                "p (a b) -> p a b", b=65)[:, :, 0:64]
            src = pb[b][:][:, 0:512].rearrange("p (a b) -> p a b", b=64)
            vS.inc(v.tensor_copy(dst, src))
        bank_r[b] = (vS, vS.v)
        s_v_evac[tb] = (vS, vS.v)

    v_xnfm = [0] * NT
    t_xntr = [0] * NT
    s_xnq_done = [0] * NT
    for tb in range(NT):
        xap = x_sb[:, (tb % NXR) * D:(tb % NXR + 1) * D]
        xn = xn_sc[:, (tb % 2) * D:(tb % 2 + 1) * D]

        @blk.vector
        def _(v, xap=xap, xn=xn, tb=tb):
            wt(v, *x_ld2[tb])
            wt(v, sS, s_rstd_all)
            if tb >= 2:
                wt(v, vS, v_xnfm[tb - 2])
            if tb >= 1:
                wt(v, sS, s_xnq_done[tb - 1])
            v.tensor_scalar(xn[:], xap, p1st[:, tb:tb + 1], p1st[:, 32 + tb:33 + tb],
                            ALU.subtract, ALU.mult)
            v.drain()
            v.reduce_max(scal[:, 28:29], xn[:], axis=AX, apply_absolute_value=True)
            v.drain()
            v.tensor_scalar(mxc_all[:, tb:tb + 1], scal[:, 28:29], EPS,
                            1.0 / 127.0, ALU.max, ALU.mult)
            v.drain()
            v.reciprocal(scal[:, 24 + tb % 2:25 + tb % 2], mxc_all[:, tb:tb + 1])
            vS.inc(v.engine_nop())
        v_xnq = vS.v

        @blk.scalar
        def _(s, xn=xn, tb=tb, v_xnq=v_xnq):
            wt(s, vS, v_xnq)
            if tb >= 1:
                wt(s, tS, t_xntr[tb - 1])
            s.activation(xq_tm[:, tb * D:(tb + 1) * D], xn[:], AF.Identity,
                         scale=scal[:, 24 + tb % 2:25 + tb % 2])
            sS.inc(s.copy(xnbf_sc[:], xn[:]))
        s_xnq_done[tb] = sS.v
        v_xnq = (sS, s_xnq_done[tb])

        if tb + NXR < NT:
            @blk.sync
            def _(s, tb=tb):
                wt(s, vS, vS.v)
                nb = tb + NXR
                sem = xs[nb % NXR]
                s.dma_start(out=x_sb[:, (nb % NXR) * D:(nb % NXR + 1) * D],
                            in_=x_in[nb * 128:(nb + 1) * 128, :]).then_inc(sem.h, 16)
                sem.v += 16
                x_ld2[nb] = (sem, sem.v)

        for w in range(2):
            b = w

            @blk.tensor
            def _(t, w=w, b=b, tb=tb):
                wt(t, sS, s_xnq_done[tb])
                wr_bank(t, b)
                for j in range(4):
                    ins = t.transpose(pbb[b][:, j * 128:(j + 1) * 128],
                                      xnbf_sc[:, (w * 4 + j) * 128:(w * 4 + j + 1) * 128],
                                      ident_bf[:])
                tS.inc(ins)
            bank_w[b] = (tS, tS.v)
            t_tr = tS.v
            if w == 1:
                t_xntr[tb] = tS.v

            @blk.vector
            def _(v, w=w, b=b, tb=tb, t_tr=t_tr):
                wt(v, tS, t_tr)
                dst = xn_fm[:, w * 4 * T:(w * 4 + 4) * T].rearrange(
                    "p (a b) -> p a b", b=T)[:, :, tb * 128:(tb + 1) * 128]
                src = pbb[b][:, 0:512].rearrange("p (a b) -> p a b", b=128)
                vS.inc(v.tensor_copy(dst, src))
            bank_r[b] = (vS, vS.v)
            if w == 1:
                v_xnfm[tb] = vS.v
        v_xnfm_all = vS.v
        if tb % 4 == 3:
            emit_qk(tb // 4)
            for vtb in range(tb - 3, tb + 1):
                emit_v(vtb)

    dump_reg.update(xq_tm=xq_tm[:], mxc=mxc_all[:], xn_fm=xn_fm[:],
                    x_sb=x_sb[:], xn_sc=xn_sc[:], scal=scal[:])
    if stop == "p1":
        return _finish()

    @blk.scalar
    def _(s):
        wt(s, dG, dg_setup)
        sS.inc(s.activation(stage[:, 0:HD], gate_sb[:], AF.Exp))
    s_gexp = sS.v

    @blk.vector
    def _(v):
        wt(v, sS, s_gexp)
        v.tensor_scalar_add(stage[:, HD:2 * HD], stage[:, 0:HD], 1.0)
        v.drain()
        v.reciprocal(stage[:, 3 * HD:4 * HD], stage[:, HD:2 * HD])
        v.drain()
        v.tensor_tensor(gate_sb[:], stage[:, 0:HD], stage[:, 3 * HD:4 * HD],
                        ALU.mult)
        v.drain()
        v.drain()
        vS.inc(v.tensor_scalar(gate1m[:], gate_sb[:], -1.0, 1.0,
                               ALU.mult, ALU.add))
    v_gate = vS.v

    if stop == "gate":
        return _finish()

    s_qk_done = sS.v
    s_v_done = (vS, vS.v)

    dump_reg.update(q_fm=q_fm[:], k_fm=k_fm[:], v_tm=v_tm[:],
                    gate=gate_sb[:], mbc=mbc[:])
    if stop == "p2":
        return _finish()

    for t in reversed(gb_cm):
        t.__exit__(None, None, None)
        _stack.remove(t)
    xnfm_cm[0].__exit__(None, None, None)
    _stack.remove(xnfm_cm[0])
    for t in reversed(gp1_cm):
        t.__exit__(None, None, None)
        _stack.remove(t)
    for t in reversed(gq_cm):
        t.__exit__(None, None, None)
        _stack.remove(t)

    ga2_cm = [
        nc.sbuf_tensor("sk_fm", [128, 2 * 1024], BF16),
        nc.sbuf_tensor("sk_tm", [128, 2 * 1024], BF16),
        nc.sbuf_tensor("att_tm", [128, NT * HD], BF16),
        nc.sbuf_tensor("e_sb", [128, 16 * 256], BF16),
        nc.sbuf_tensor("eT_sb", [128, 16 * 256], BF16),
        nc.sbuf_tensor("elu_sc", [128, 12 * 256], BF16),
        nc.sbuf_tensor("ssum", [128, 16], F32),
        nc.sbuf_tensor("rsum", [128, 16], F32),
        nc.sbuf_tensor("memz", [128, H * 9 * 65], F32),
        nc.sbuf_tensor("memz_bf", [128, NCH * 520], BF16),
        nc.sbuf_tensor("corr", [128, 2 * 520], F32),
        nc.sbuf_tensor("agt", [128, 2 * 520], F32),
        nc.sbuf_tensor("att_fm", [128, NHD * T], BF16),
        nc.sbuf_tensor("mix_sc", [128, 3 * HD], F32),
        nc.sbuf_tensor("x2_sb", [128, 2 * D], F32),
    ]
    (sk_fm, sk_tm, att_tm, e_sb, eT_sb, elu_sc, ssum, rsum, memz, memz_bf,
     corr, agt, att_fm, mix_sc, x2_sb) = [t.__enter__() for t in ga2_cm]
    _stack.extend(ga2_cm)


    @blk.vector
    def _(v):
        wt(v, *s_v_done)
        mz = memz[:].rearrange("p (a b c) -> p a b c", b=9, c=65)
        vS.inc(v.memset(mz[0:64, :, 0:1, :], 0.0))
    v_mz0 = vS.v

    v_sk = [0] * NCH
    v_snap = [0] * (NCH + 1)
    v_snap[0] = v_mz0
    t_skTv = [0] * NCH
    t_sk_tr = [0] * NCH
    s_kexp = [0] * NCH
    for c in range(NCH):
        @blk.vector
        def _(v, c=c):
            wt(v, *s_v_done)
            if c >= 1:
                wt(v, sS, s_kexp[c - 1])
            for hb in range(NHD):
                col = hb * T + c * 256
                v.tensor_single_scalar(elu_sc[:, hb * 256:(hb + 1) * 256],
                                       k_fm[:, col:col + 256], 0.0, ALU.min)
            vS.inc(v.engine_nop())
        v_min = vS.v

        @blk.scalar
        def _(s, v_min=v_min):
            wt(s, vS, v_min)
            for hb in range(NHD):
                ins = s.activation(elu_sc[:, 1024 + hb * 256:1024 + (hb + 1) * 256],
                                   elu_sc[:, hb * 256:(hb + 1) * 256], AF.Exp)
            sS.inc(ins)
        s_exp = sS.v
        s_kexp[c] = s_exp

        @blk.vector
        def _(v, c=c, s_exp=s_exp):
            wt(v, sS, s_exp)
            if c >= 2:
                wt(v, tS, t_sk_tr[c - 2])
            for hb in range(NHD):
                col = hb * T + c * 256
                v.tensor_single_scalar(elu_sc[:, 2048 + hb * 256:2048 + (hb + 1) * 256],
                                       k_fm[:, col:col + 256], 0.0, ALU.max)
            v.drain()
            for hb in range(NHD):
                scol2 = (c % 2) * 1024 + hb * 256
                v.tensor_tensor(sk_fm[:, scol2:scol2 + 256],
                                elu_sc[:, 1024 + hb * 256:1024 + (hb + 1) * 256],
                                elu_sc[:, 2048 + hb * 256:2048 + (hb + 1) * 256],
                                ALU.add)
            vS.inc(v.engine_nop())
        v_sk[c] = vS.v

        for half in range(2):
            b = 6 + half

            @blk.tensor
            def _(t, c=c, half=half, b=b):
                wt(t, vS, v_sk[c])
                wr_bank(t, b)
                for hp in range(NHD):
                    col = (c % 2) * 1024 + hp * 256 + half * 128
                    ins = t.transpose(pbb[b][:, hp * 128:(hp + 1) * 128],
                                      sk_fm[:, col:col + 128], ident_bf[:])
                tS.inc(ins)
            bank_w[b] = (tS, tS.v)
            t_tr = tS.v
            t_sk_tr[c] = tS.v

            @blk.vector
            def _(v, c=c, half=half, b=b, t_tr=t_tr):
                wt(v, tS, t_tr)
                if c >= 2:
                    wt(v, tS, t_skTv[c - 2])
                vS.inc(v.tensor_copy(
                    sk_tm[:, (c % 2) * 1024 + half * 512:(c % 2) * 1024 + (half + 1) * 512],
                    pbb[b][:, 0:512]))
            bank_r[b] = (vS, vS.v)
        v_sktm = vS.v

        for gix in range(2):
            b = 4 + gix

            @blk.tensor
            def _(t, c=c, gix=gix, b=b, v_sktm=v_sktm):
                wt(t, vS, v_sktm)
                wt(t, *s_v_evac[2 * c + 1])
                wr_bank(t, b)
                for hh in range(4):
                    h = gix * 4 + hh
                    for half in range(2):
                        lcol = (c % 2) * 1024 + half * 512 + (h // 2) * 128 + (h % 2) * 64
                        ins = t.matmul(
                            pb[b][:][0:64, hh * 72:hh * 72 + 65],
                            sk_tm[:, lcol:lcol + 64],
                            v_tm[:, (2 * c + half) * 520 + h * 65:
                                 (2 * c + half) * 520 + (h + 1) * 65],
                            start=(half == 0), stop=(half == 1))
                tS.inc(ins)
            bank_w[b] = (tS, tS.v)
            t_ct = tS.v
            t_skTv[c] = tS.v

            @blk.vector
            def _(v, c=c, gix=gix, b=b, t_ct=t_ct):
                wt(v, tS, t_ct)
                wt(v, vS, v_snap[c])
                mz = memz[0:64, :].rearrange("p (a b) -> p a b", b=9 * 65)
                dst = mz[:, gix * 4:(gix + 1) * 4, (c + 1) * 65:(c + 2) * 65]
                src0 = mz[:, gix * 4:(gix + 1) * 4, c * 65:(c + 1) * 65]
                src1 = pb[b][:][0:64, 0:288].rearrange(
                    "p (a b) -> p a b", b=72)[:, :, 0:65]
                vS.inc(v.tensor_tensor(dst, src0, src1, ALU.add))
            bank_r[b] = (vS, vS.v)
        v_snap[c + 1] = vS.v

    @blk.gpsimd
    def _(g):
        wt(g, vS, v_snap[NCH])
        mz = memz[0:64, :].rearrange("p (a b) -> p a b", b=9 * 65)
        g.dma_start(out=ag3_in.rearrange("r (p q) -> r p q", p=64)[0:1].rearrange(
                        "r p q -> (r p) q").rearrange("p (a b) -> p a b", b=65),
                    in_=mz[:, 0:8, 8 * 65:9 * 65]).then_inc(dG.h, 16)
        dG.v += 16
        wt(g, dG, dG.v)
        cS.inc(g.collective_compute(
            "AllGather", ALU.bypass, replica_groups=[list(range(N_CORES))],
            ins=[ag3_in.opt()], outs=[ag3_out.opt()]))
    c_mem = cS.v

    dump_reg.pop("xn_fm", None)
    dump_reg.update(memz=memz[:], sk_fm=sk_fm[:])
    if stop == "p3a":
        return _finish()

    s_att_evac = [0] * NT
    s_qexp = [0] * NCH
    p3b_lim = NCH
    if stop.startswith("p3b:"):
        p3b_lim = int(stop.split(":")[1])

    def p3_sbank(g, h, qh):
        return 4 * g + qh * 2 + (h % 2)

    def p3_scol(h):
        return ((h % 4) // 2) * 256

    grp_state = {}
    for c in range(NCH):
        if c >= p3b_lim:
            continue
        for g in range(2):
            heads = range(4 * g, 4 * g + 4)

            @blk.tensor
            def _(t, c=c, g=g, heads=heads):
                wt(t, sS, s_qk_done)
                for h in heads:
                    for qh in range(2):
                        wr_bank(t, p3_sbank(g, h, qh))
                for h in heads:
                    for qh in range(2):
                        pr = (h % 2) * 64
                        qcol = (h // 2) * T + c * 256 + qh * 128
                        kcol = (h // 2) * T + c * 256
                        ins = t.matmul(
                            pb[p3_sbank(g, h, qh)][:][:, p3_scol(h):p3_scol(h) + 256],
                            q_fm[pr:pr + 64, qcol:qcol + 128],
                            k_fm[pr:pr + 64, kcol:kcol + 256],
                            start=True, stop=True)
                tS.inc(ins)
            for h in heads:
                for qh in range(2):
                    bank_w[p3_sbank(g, h, qh)] = (tS, tS.v)
            t_sc = tS.v

            @blk.vector
            def _(v, c=c, g=g, heads=heads, t_sc=t_sc):
                wt(v, tS, t_sc)
                for h in heads:
                    for qh in range(2):
                        i = h * 2 + qh
                        v.reduce_max(scal[:, i:i + 1],
                                     pb[p3_sbank(g, h, qh)][:][:, p3_scol(h):p3_scol(h) + 256],
                                     axis=AX)
                v.drain()
                for h in heads:
                    for qh in range(2):
                        i = h * 2 + qh
                        v.tensor_scalar_mul(scal[:, 16 + i:17 + i],
                                            scal[:, i:i + 1], -0.125)
                vS.inc(v.engine_nop())
            v_mx = vS.v

            @blk.scalar
            def _(s, c=c, g=g, heads=heads, v_mx=v_mx):
                wt(s, vS, v_mx)
                for h in heads:
                    for qh in range(2):
                        i = h * 2 + qh
                        ins = s.activation(
                            e_sb[:, i * 256:(i + 1) * 256],
                            pb[p3_sbank(g, h, qh)][:][:, p3_scol(h):p3_scol(h) + 256],
                            AF.Exp, bias=scal[:, 16 + i:17 + i], scale=0.125,
                            accum_out=ssum[:, i:i + 1])
                sS.inc(ins)
            for h in heads:
                for qh in range(2):
                    bank_r[p3_sbank(g, h, qh)] = (sS, sS.v)
            s_exp = sS.v

            @blk.vector
            def _(v, g=g, s_exp=s_exp):
                wt(v, sS, s_exp)
                vS.inc(v.reciprocal(rsum[:, 8 * g:8 * g + 8], ssum[:, 8 * g:8 * g + 8]))
            v_rs = vS.v

            @blk.tensor
            def _(t, g=g, s_exp=s_exp):
                wt(t, sS, s_exp)
                wr_bank(t, 4 * g)
                wr_bank(t, 4 * g + 2)
                for k in range(16):
                    i = 8 * g + k // 2
                    j = k % 2
                    b = 4 * g + 2 * (k // 8)
                    ins = t.transpose(pbb[b][:, (k % 8) * 128:(k % 8 + 1) * 128],
                                      e_sb[:, i * 256 + j * 128:i * 256 + (j + 1) * 128],
                                      ident_bf[:])
                tS.inc(ins)
            bank_w[4 * g] = (tS, tS.v)
            bank_w[4 * g + 2] = (tS, tS.v)
            t_eT = tS.v

            @blk.vector
            def _(v, g=g, t_eT=t_eT):
                wt(v, tS, t_eT)
                v.tensor_copy(eT_sb[:, g * 2048:g * 2048 + 1024],
                              pbb[4 * g][:, 0:1024])
                vS.inc(v.tensor_copy(eT_sb[:, g * 2048 + 1024:g * 2048 + 2048],
                                     pbb[4 * g + 2][:, 0:1024]))
            bank_r[4 * g] = (vS, vS.v)
            bank_r[4 * g + 2] = (vS, vS.v)
            v_eT = vS.v

            @blk.tensor
            def _(t, c=c, g=g, heads=heads, v_eT=v_eT):
                wt(t, vS, v_eT)
                wt(t, *s_v_evac[2 * c + 1])
                wr_bank(t, 4 * g + 1)
                wr_bank(t, 4 * g + 3)
                for h in heads:
                    for qh in range(2):
                        i = h * 2 + qh
                        k0 = (i - 8 * g) * 2
                        for kb in range(2):
                            k = k0 + kb
                            col = g * 2048 + (k // 8) * 1024 + (k % 8) * 128
                            ins = t.matmul(
                                pb[4 * g + 1 + 2 * qh][:][:, (h - 4 * g) * 64:(h - 4 * g + 1) * 64],
                                eT_sb[:, col:col + 128],
                                v_tm[:, (2 * c + kb) * 520 + h * 65:
                                     (2 * c + kb) * 520 + h * 65 + 64],
                                start=(kb == 0), stop=(kb == 1))
                tS.inc(ins)
            bank_w[4 * g + 1] = (tS, tS.v)
            bank_w[4 * g + 3] = (tS, tS.v)
            t_av = tS.v
            grp_state[g] = t_av

            @blk.scalar
            def _(s, c=c, g=g, heads=heads, t_av=t_av):
                wt(s, tS, t_av)
                for qh in range(2):
                    tb = 2 * c + qh
                    for h in heads:
                        ins = s.activation(
                            att_tm[:, tb * HD + h * 64:tb * HD + (h + 1) * 64],
                            pb[4 * g + 1 + 2 * qh][:][:, (h - 4 * g) * 64:(h - 4 * g + 1) * 64],
                            AF.Identity,
                            scale=rsum[:, h * 2 + qh:h * 2 + qh + 1])
                sS.inc(ins)
            bank_r[4 * g + 1] = (sS, sS.v)
            bank_r[4 * g + 3] = (sS, sS.v)
            if g == 1:
                s_att_evac[2 * c] = sS.v
                s_att_evac[2 * c + 1] = sS.v
        t_av = tS.v

        @blk.vector
        def _(v, c=c, t_av=t_av):
            wt(v, tS, t_av)
            if c >= 1:
                wt(v, sS, s_qexp[c - 1])
            for hb in range(NHD):
                col = hb * T + c * 256
                v.tensor_single_scalar(elu_sc[:, hb * 256:(hb + 1) * 256],
                                       q_fm[:, col:col + 256], 0.0, ALU.min)
            vS.inc(v.engine_nop())
        v_min = vS.v

        @blk.scalar
        def _(s, v_min=v_min):
            wt(s, vS, v_min)
            for hb in range(NHD):
                ins = s.activation(elu_sc[:, 1024 + hb * 256:1024 + (hb + 1) * 256],
                                   elu_sc[:, hb * 256:(hb + 1) * 256], AF.Exp)
            sS.inc(ins)
        s_eexp = sS.v
        s_qexp[c] = s_eexp

        @blk.vector
        def _(v, c=c, s_eexp=s_eexp):
            wt(v, sS, s_eexp)
            for hb in range(NHD):
                col = hb * T + c * 256
                v.tensor_single_scalar(elu_sc[:, 2048 + hb * 256:2048 + (hb + 1) * 256],
                                       q_fm[:, col:col + 256], 0.0, ALU.max)
            v.drain()
            for hb in range(NHD):
                col = hb * T + c * 256
                v.tensor_tensor(q_fm[:, col:col + 256],
                                elu_sc[:, 1024 + hb * 256:1024 + (hb + 1) * 256],
                                elu_sc[:, 2048 + hb * 256:2048 + (hb + 1) * 256],
                                ALU.add)
            vS.inc(v.engine_nop())
    v_sq_done = vS.v

    dump_reg.update(att_tm=att_tm[:], sq_fm=q_fm[:], ssum=ssum[:],
                    rsum=rsum[:], e_sb=e_sb[:])
    if stop == "p3b" or stop.startswith("p3b:"):
        return _finish()

    dg_ag3 = [None] * N_CORES
    v_stt = [0] * N_CORES

    @blk.vector
    def _(v):
        v.memset(corr[0:64, 0:520], 0.0)
        v.drain()
        ca = corr[0:64, 0:520].rearrange("p (a b) -> p a b", b=65)
        v.memset(ca[:, :, 64:65], 1.0 / 64.0)
        vS.inc(v.drain())
    v_corr0 = vS.v

    for cc in range(N_CORES):
        @blk.gpsimd
        def _(g, cc=cc):
            if cc == 0:
                wt(g, cS, c_mem)
            if cc >= 2:
                wt(g, vS, v_stt[cc - 2])
            sem = ags[cc % 2]
            g.dma_start(out=agt[0:64, (cc % 2) * 520:(cc % 2 + 1) * 520],
                        in_=ag3_v[cc:cc + 1].rearrange("r p q -> (r p) q")
                        ).then_inc(sem.h, 16)
            sem.v += 16
            dg_ag3[cc] = (sem, sem.v)

        @blk.vector
        def _(v, cc=cc):
            wt(v, *dg_ag3[cc])
            src = (cc % 2) * 520
            d = ((cc + 1) % 2) * 520
            v.scalar_tensor_tensor(
                corr[0:64, d:d + 520], agt[0:64, src:src + 520],
                mask_sb[0:64, cc:cc + 1], corr[0:64, (cc % 2) * 520:(cc % 2) * 520 + 520],
                ALU.mult, ALU.add)
            vS.inc(v.drain())
            v_stt[cc] = vS.v

    @blk.vector
    def _(v):
        wt(v, vS, v_snap[NCH])
        mz = memz[0:64, :].rearrange("p (a b) -> p a b", b=9 * 65)
        for c in range(NCH):
            v.tensor_tensor(
                memz_bf[0:64, c * 520:(c + 1) * 520].rearrange(
                    "p (a b) -> p a b", b=65),
                mz[:, 0:8, c * 65:(c + 1) * 65],
                corr[0:64, 0:520].rearrange("p (a b) -> p a b", b=65),
                ALU.add)
        vS.inc(v.engine_nop())
    v_corr = vS.v

    @blk.sync
    def _(s):
        wt(s, vS, v_corr)
        s.dma_start(out=memz_bf[64:128, :], in_=memz_bf[0:64, :]).then_inc(dX.h, 16)
        dX.v += 16
    dx_dup = dX.v

    dump_reg.update(memz_bf=memz_bf[:], corr=corr[:])
    if stop == "p5a":
        return _finish()

    x2_ld = [None] * NT

    @blk.sync
    def _(s):
        for tb in range(2):
            sem = x2s[tb % 2]
            s.dma_start(out=x2_sb[:, tb * D:(tb + 1) * D],
                        in_=x_in[tb * 128:(tb + 1) * 128, :]).then_inc(sem.h, 16)
            sem.v += 16
            x2_ld[tb] = (sem, sem.v)

    dx_out = [None] * NT
    p5b_lim = NT
    p5b_step = "st"
    if stop.startswith("p5b:"):
        parts = stop.split(":")
        p5b_lim = int(parts[1])
        p5b_step = parts[2] if len(parts) > 2 else "st"
    for c in range(NCH):
        for qh in range(2):
            tb = 2 * c + qh
            if tb >= p5b_lim:
                continue
            last = (tb == p5b_lim - 1) and stop.startswith("p5b:")
            mb0 = 0
            for par in range(2):
                b = mb0 + par

                @blk.tensor
                def _(t, c=c, qh=qh, par=par, b=b):
                    wt(t, vS, v_sq_done)
                    wt(t, dX, dx_dup)
                    wr_bank(t, b)
                    pr = par * 64
                    for hh in range(4):
                        h = 2 * hh + par
                        qcol = (h // 2) * T + c * 256 + qh * 128
                        ins = t.matmul(
                            pb[b][:][:, hh * 72:hh * 72 + 65],
                            q_fm[pr:pr + 64, qcol:qcol + 128],
                            memz_bf[pr:pr + 64, c * 520 + h * 65:c * 520 + (h + 1) * 65],
                            start=True, stop=True)
                    tS.inc(ins)
                bank_w[b] = (tS, tS.v)
            t_mem = tS.v
            if last and p5b_step.startswith("mem"):
                continue

            @blk.vector
            def _(v, tb=tb, mb0=mb0, t_mem=t_mem):
                wt(v, tS, t_mem)
                for par in range(2):
                    b = mb0 + par
                    zview = pb[b][:][:, 0:288].rearrange(
                        "p (a b) -> p a b", b=72)[:, :, 64:65]
                    dst = scal[:, 21 + par * 4:25 + par * 4].rearrange(
                        "p (a b) -> p a b", b=1)
                    v.reciprocal(dst, zview)
                v.drain()
                for par in range(2):
                    b = mb0 + par
                    for hh in range(4):
                        h = 2 * hh + par
                        v.tensor_scalar_mul(mix_sc[:, h * 64:(h + 1) * 64],
                                            pb[b][:][:, hh * 72:hh * 72 + 64],
                                            scal[:, 21 + par * 4 + hh:22 + par * 4 + hh])
                wt(v, sS, s_att_evac[tb])
                wt(v, vS, v_gate)
                v.tensor_tensor(mix_sc[:, HD:2 * HD],
                                att_tm[:, tb * HD:(tb + 1) * HD], gate1m[:],
                                ALU.mult)
                v.tensor_tensor(mix_sc[:, 2 * HD:3 * HD], mix_sc[:, 0:HD],
                                gate_sb[:], ALU.mult)
                v.drain()
                vS.inc(v.tensor_tensor(att_tm[:, tb * HD:(tb + 1) * HD],
                                       mix_sc[:, HD:2 * HD],
                                       mix_sc[:, 2 * HD:3 * HD], ALU.add))
            bank_r[mb0] = (vS, vS.v)
            bank_r[mb0 + 1] = (vS, vS.v)
            v_mix = vS.v
            if last and p5b_step == "mix":
                continue

            b = 4 + (tb % 2)

            @blk.tensor
            def _(t, tb=tb, b=b, v_mix=v_mix):
                wt(t, vS, v_mix)
                wr_bank(t, b)
                for j in range(NHD):
                    ins = t.transpose(pbb[b][:, j * 128:(j + 1) * 128],
                                      att_tm[:, tb * HD + j * 128:tb * HD + (j + 1) * 128],
                                      ident_bf[:])
                tS.inc(ins)
            bank_w[b] = (tS, tS.v)
            t_atr = tS.v

            @blk.vector
            def _(v, tb=tb, b=b, t_atr=t_atr):
                wt(v, tS, t_atr)
                dst = att_fm[:, 0:4 * T].rearrange(
                    "p (a b) -> p a b", b=T)[:, :, tb * 128:(tb + 1) * 128]
                src = pbb[b][:, 0:512].rearrange("p (a b) -> p a b", b=128)
                vS.inc(v.tensor_copy(dst, src))
            bank_r[b] = (vS, vS.v)
            v_afm = vS.v
            if last and p5b_step == "tr":
                continue

            wob = (6, 7) if tb % 2 == 0 else (2, 3)

            @blk.tensor
            def _(t, tb=tb, wob=wob, v_afm=v_afm):
                wt(t, vS, v_afm)
                wt(t, dG, dg_setup)
                wr_bank(t, wob[0])
                wr_bank(t, wob[1])
                for nb in range(2):
                    for hb in range(NHD):
                        ins = t.matmul(
                            pb[wob[nb]][:],
                            att_fm[:, hb * T + tb * 128:hb * T + (tb + 1) * 128],
                            wo_bf[:, hb * 1024 + nb * 512:hb * 1024 + (nb + 1) * 512],
                            start=(hb == 0), stop=(hb == NHD - 1))
                tS.inc(ins)
            bank_w[wob[0]] = (tS, tS.v)
            bank_w[wob[1]] = (tS, tS.v)
            t_wo = tS.v
            if last and p5b_step == "wo":
                continue

            @blk.vector
            def _(v, tb=tb, wob=wob, t_wo=t_wo):
                wt(v, tS, t_wo)
                wt(v, *x2_ld[tb])
                if tb >= 2:
                    wt(v, *dx_out[tb - 2])
                sl = (tb % 2) * D
                v.tensor_tensor(stage[:, sl:sl + 512], pb[wob[0]][:],
                                x2_sb[:, sl:sl + 512], ALU.add)
                vS.inc(v.tensor_tensor(stage[:, sl + 512:sl + D], pb[wob[1]][:],
                                       x2_sb[:, sl + 512:sl + D], ALU.add))
            bank_r[wob[0]] = (vS, vS.v)
            bank_r[wob[1]] = (vS, vS.v)
            v_stage = vS.v

            @blk.sync
            def _(s, tb=tb, v_stage=v_stage):
                wt(s, vS, v_stage)
                sl = (tb % 2) * D
                sem = osm[tb % 2]
                s.dma_start(out=out[tb * 128:(tb + 1) * 128, :],
                            in_=stage[:, sl:sl + D]).then_inc(sem.h, 16)
                sem.v += 16
                dx_out[tb] = (sem, sem.v)
                if tb + 2 < NT:
                    nb = tb + 2
                    sem2 = x2s[nb % 2]
                    s.dma_start(out=x2_sb[:, (nb % 2) * D:(nb % 2 + 1) * D],
                                in_=x_in[nb * 128:(nb + 1) * 128, :]).then_inc(sem2.h, 16)
                    sem2.v += 16
                    x2_ld[nb] = (sem2, sem2.v)
    dx_attn_out = [(osm[0], osm[0].v), (osm[1], osm[1].v),
                   (x2s[0], x2s[0].v), (x2s[1], x2s[1].v)]

    def wt_attn_out(eng):
        for sem, vv in dx_attn_out:
            if vv:
                eng.wait_ge(sem.h, vv)

    dump_reg.update(att_mix=att_tm[:])
    if stop == "p5b" or stop.startswith("p5b:"):
        return _finish()

    for t in reversed(ga2_cm):
        t.__exit__(None, None, None)
        _stack.remove(t)
    for t in reversed(ga1_cm):
        t.__exit__(None, None, None)
        _stack.remove(t)

    v_bar = vS.v
    s_bar = sS.v
    t_bar = tS.v

    p4_cm = [
        nc.sbuf_tensor("h_hg", [128, HG * I_FFN], F32),
        nc.sbuf_tensor("hq_i8", [128, I_FFN], I8),
        nc.sbuf_tensor("hq_bf", [128, I_FFN], BF16),
        nc.sbuf_tensor("hq_fm", [128, 2 * I_FFN], BF16),
        nc.sbuf_tensor("xq_fm", [128, HG * ND * 128], BF16),
        nc.sbuf_tensor("xqbf_t", [128, D], BF16),
        nc.sbuf_tensor("wgu_buf", [128, 4 * I_FFN], BF16),
        nc.sbuf_tensor("wd_buf", [128, 2 * 4 * 1024], BF16),
        nc.sbuf_tensor("hsc", [128, 4 * 512], F32),
        nc.sbuf_tensor("hsc2", [128, 512], F32),
        nc.sbuf_tensor("sfin", [128, HG], F32),
        nc.sbuf_tensor("p4mx", [128, HG], F32),
    ]
    (h_hg, hq_i8, hq_bf, hq_fm, xq_fm, xqbf_t, wgu_buf, wd_buf, hsc, hsc2,
     sfin, p4mx) = [t.__enter__() for t in p4_cm]
    _stack.extend(p4_cm)

    NOC = 8
    ds_slab = {}
    t_guslab_done = {}
    hq_state = {"t_hqbf_readers": 0, "t_down_prev": 0}
    ds_wd = {}
    t_wd_done = {}
    dg_out = {}

    def emit_gu_dma(hg, oc, mat):
        sl = ((oc % 2) * 2 + mat) * I_FFN

        @blk.scalar
        def _(s, hg=hg, oc=oc, mat=mat, sl=sl):
            wt(s, cS, c_tern)
            wt(s, vS, v_bar)
            wt_attn_out(s)
            key = (hg, oc, mat)
            gi = hg * NOC + oc
            if gi >= 2:
                pkey = ((gi - 2) // NOC, (gi - 2) % NOC, mat)
                assert pkey in t_guslab_done, pkey
                wt(s, tS, t_guslab_done[pkey])
            src = ag2_v[0:8, :, mat * I_FFN + oc * 512:mat * I_FFN + (oc + 1) * 512
                        ].rearrange("b p c -> p b c")
            dst = wgu_buf[:, sl:sl + I_FFN].rearrange("p (b c) -> p b c", b=8)
            sem = gus[(oc % 2) * 2 + mat] if len(gus) == 4 else gus[oc % 2]
            s.dma_start(out=dst, in_=src).then_inc(sem.h, 16)
            sem.v += 16
            ds_slab[key] = (sem, sem.v)

    v_hevac = {}
    v_h_done = {}

    for hg in range(NHG):
        v_xqfm = {}
        for ti in range(HG):
            tb = hg * HG + ti

            @blk.vector
            def _(v, tb=tb):
                wt(v, vS, v_bar)
                if hg > 0 or ti > 0:
                    wt(v, tS, hq_state["t_hqbf_readers"] or t_bar)
                vS.inc(v.tensor_copy(xqbf_t[:], xq_tm[:, tb * D:(tb + 1) * D]))
            v_cast = vS.v

            for w in range(2):
                b = w

                @blk.tensor
                def _(t, ti=ti, w=w, b=b, v_cast=v_cast):
                    wt(t, vS, v_cast)
                    wt(t, sS, s_bar)
                    wr_bank(t, b)
                    for j in range(4):
                        ins = t.transpose(pbb[b][:, j * 128:(j + 1) * 128],
                                          xqbf_t[:, (w * 4 + j) * 128:(w * 4 + j + 1) * 128],
                                          ident_bf[:])
                    tS.inc(ins)
                bank_w[b] = (tS, tS.v)
                t_tr = tS.v

                @blk.vector
                def _(v, ti=ti, w=w, b=b, t_tr=t_tr):
                    wt(v, tS, t_tr)
                    dst = xq_fm[:, (ti * ND + w * 4) * 128:(ti * ND + w * 4 + 4) * 128]
                    vS.inc(v.tensor_copy(dst, pbb[b][:, 0:512]))
                bank_r[b] = (vS, vS.v)
            v_xqfm[ti] = vS.v
            hq_state["t_hqbf_readers"] = tS.v

        emit_gu_dma(hg, 0, 0)
        emit_gu_dma(hg, 0, 1)
        for oc in range(NOC):
            if oc + 1 < NOC:
                emit_gu_dma(hg, oc + 1, 0)
                emit_gu_dma(hg, oc + 1, 1)
            for mat in range(2):
                sl = ((oc % 2) * 2 + mat) * I_FFN
                for ti in range(HG):
                    b = mat * 4 + ti

                    @blk.tensor
                    def _(t, hg=hg, oc=oc, mat=mat, ti=ti, b=b, sl=sl):
                        wt(t, vS, v_xqfm[ti])
                        wt(t, *ds_slab[(hg, oc, mat)])
                        wr_bank(t, b)
                        for db in range(ND):
                            ins = t.matmul(
                                pb[b][:][:],
                                xq_fm[:, (ti * ND + db) * 128:(ti * ND + db + 1) * 128],
                                wgu_buf[:, sl + db * 512:sl + (db + 1) * 512],
                                start=(db == 0), stop=(db == ND - 1))
                        tS.inc(ins)
                    bank_w[b] = (tS, tS.v)
                    if ti == HG - 1:
                        t_guslab_done[(hg, oc, mat)] = tS.v

            for ti in range(HG):
                @blk.vector
                def _(v, hg=hg, oc=oc, ti=ti):
                    wt(v, bank_w[ti][0], bank_w[ti][1])
                    wt(v, bank_w[4 + ti][0], bank_w[4 + ti][1])
                    hs = hsc[:, 0:512]
                    hs2 = hsc2[:, 0:512]
                    v.tensor_single_scalar(hs, pb[ti][:], 0.0, ALU.max)
                    v.tensor_tensor(hs2, hs, hs, ALU.mult)
                    vS.inc(v.tensor_tensor(
                        h_hg[:, ti * I_FFN + oc * 512:ti * I_FFN + (oc + 1) * 512],
                        hs2, pb[4 + ti][:], ALU.mult))
                bank_r[ti] = (vS, vS.v)
                bank_r[4 + ti] = (vS, vS.v)
                v_hevac[(hg, oc, ti)] = vS.v

        for sub in range(2):
            def emit_wd_dma(hg, sub, islab):
                sl = (islab % 2) * 4 * 1024

                @blk.scalar
                def _(s, hg=hg, sub=sub, islab=islab, sl=sl):
                    wt(s, cS, c_tern)
                    wt(s, vS, v_bar)
                    wt_attn_out(s)
                    idx = (hg * 2 + sub) * 8 + islab
                    if idx >= 2:
                        p = idx - 2
                        pkey = (p // 16, (p // 8) % 2, p % 8)
                        assert pkey in t_wd_done, pkey
                        wt(s, tS, t_wd_done[pkey])
                    sem = wds[islab % 2]
                    src = ag2_v[islab:islab + 1, :,
                                2 * I_FFN:2 * I_FFN + 4096].rearrange(
                                    "r p q -> (r p) q")
                    s.dma_start(out=wd_buf[:, sl:sl + 4096],
                                in_=src).then_inc(sem.h, 16)
                    sem.v += 16
                    ds_wd[(hg, sub, islab)] = (sem, sem.v)
            emit_wd_dma(hg, sub, 0)
            emit_wd_dma(hg, sub, 1)

            for tt in range(2):
                ti = sub * 2 + tt
                tb = hg * HG + ti

                @blk.vector
                def _(v, hg=hg, ti=ti, tb=tb):
                    wt(v, vS, v_hevac[(hg, NOC - 1, ti)])
                    if hq_state.get("s_hq_prev"):
                        wt(v, sS, hq_state["s_hq_prev"])
                    v.reduce_max(scal[:, 23:24], h_hg[:, ti * I_FFN:(ti + 1) * I_FFN],
                                 axis=AX, apply_absolute_value=True)
                    v.drain()
                    mx = mxc_all[:, tb:tb + 1]
                    v.scalar_tensor_tensor(scal[:, 25:26], mx, mx, mx,
                                           ALU.mult, ALU.mult)
                    v.drain()
                    v.scalar_tensor_tensor(scal[:, 26:27], scal[:, 25:26],
                                           kc[:, 0:1], scal[:, 23:24],
                                           ALU.mult, ALU.mult)
                    v.tensor_tensor(scal[:, 25:26], scal[:, 25:26], kc[:, 0:1],
                                    ALU.mult)
                    v.drain()
                    v.tensor_scalar_max(scal[:, 27:28], scal[:, 26:27], EPS)
                    v.drain()
                    v.reciprocal(scal[:, 29:30], scal[:, 27:28])
                    v.tensor_tensor(sfin[:, ti:ti + 1], scal[:, 27:28], kc[:, 1:2],
                                    ALU.mult)
                    v.drain()
                    v.scalar_tensor_tensor(scal[:, 31:32], scal[:, 29:30],
                                           127.0, scal[:, 25:26],
                                           ALU.mult, ALU.mult)
                    vS.inc(v.engine_nop())
                v_sh = vS.v
                hq_state["g_mx_prev"] = (vS, v_sh)

                @blk.scalar
                def _(s, ti=ti, v_sh=v_sh):
                    wt(s, vS, v_sh)
                    sS.inc(s.activation(hq_i8[:], h_hg[:, ti * I_FFN:(ti + 1) * I_FFN],
                                        AF.Identity, scale=scal[:, 31:32]))
                s_hq = sS.v
                hq_state["s_hq_prev"] = s_hq

                @blk.vector
                def _(v, s_hq=s_hq):
                    wt(v, sS, s_hq)
                    wt(v, tS, hq_state["t_hqbf_readers"])
                    vS.inc(v.tensor_copy(hq_bf[:], hq_i8[:]))
                v_hqbf = (vS, vS.v)

                for w in range(8):
                    b = w % 2

                    @blk.tensor
                    def _(t, w=w, b=b, v_hqbf=v_hqbf):
                        wt(t, *v_hqbf)
                        wr_bank(t, b)
                        for j in range(4):
                            ins = t.transpose(pbb[b][:, j * 128:(j + 1) * 128],
                                              hq_bf[:, (w * 4 + j) * 128:(w * 4 + j + 1) * 128],
                                              ident_bf[:])
                        tS.inc(ins)
                    bank_w[b] = (tS, tS.v)
                    t_tr = tS.v

                    @blk.vector
                    def _(v, tt=tt, w=w, b=b, t_tr=t_tr):
                        wt(v, tS, t_tr)
                        wt(v, tS, hq_state["t_down_prev"])
                        dst = hq_fm[:, tt * I_FFN + w * 512:tt * I_FFN + (w + 1) * 512]
                        vS.inc(v.tensor_copy(dst, pbb[b][:, 0:512]))
                    bank_r[b] = (vS, vS.v)
                hq_state["t_hqbf_readers"] = tS.v
                v_h_done[(hg, ti)] = vS.v

            for islab in range(8):
                sl = (islab % 2) * 4 * 1024

                @blk.tensor
                def _(t, hg=hg, sub=sub, islab=islab, sl=sl):
                    wt(t, *ds_wd[(hg, sub, islab)])
                    for tt in range(2):
                        wt(t, vS, v_h_done[(hg, sub * 2 + tt)])
                        if islab == 0:
                            wr_bank(t, 4 + 2 * tt)
                            wr_bank(t, 5 + 2 * tt)
                    for ib in range(4):
                        iblk = islab * 4 + ib
                        for tt in range(2):
                            for nb in range(2):
                                ins = t.matmul(
                                    pb[4 + 2 * tt + nb][:],
                                    hq_fm[:, tt * I_FFN + iblk * 128:tt * I_FFN + (iblk + 1) * 128],
                                    wd_buf[:, sl + ib * 1024 + nb * 512:sl + ib * 1024 + (nb + 1) * 512],
                                    start=(iblk == 0), stop=(iblk == NI - 1))
                    tS.inc(ins)
                t_wd_done[(hg, sub, islab)] = tS.v
                if islab + 2 < 8:
                    emit_wd_dma(hg, sub, islab + 2)
                if islab == 7:
                    for bb in (4, 5, 6, 7):
                        bank_w[bb] = (tS, tS.v)
            t_dn = tS.v
            hq_state["t_down_prev"] = t_dn

            for tt in range(2):
                ti = sub * 2 + tt
                tb = hg * HG + ti

                @blk.scalar
                def _(s, tb=tb, ti=ti, tt=tt, t_dn=t_dn):
                    wt(s, tS, t_dn)
                    wt_attn_out(s)
                    if tb >= 2 and (tb - 2) in dg_out:
                        wt(s, *dg_out[tb - 2])
                    sl = (tb % 2) * D
                    s.activation(stage[:, sl:sl + 512], pb[4 + 2 * tt][:], AF.Identity,
                                 scale=sfin[:, ti:ti + 1])
                    sS.inc(s.activation(stage[:, sl + 512:sl + D], pb[5 + 2 * tt][:],
                                        AF.Identity, scale=sfin[:, ti:ti + 1]))
                bank_r[4 + 2 * tt] = (sS, sS.v)
                bank_r[5 + 2 * tt] = (sS, sS.v)
                s_dn = sS.v

                @blk.gpsimd
                def _(g, tb=tb, s_dn=s_dn):
                    wt(g, sS, s_dn)
                    wt_attn_out(g)
                    sem = asm_[tb % 2]
                    sl = (tb % 2) * D
                    g.dma_start(out=out[tb * 128:(tb + 1) * 128, :],
                                in_=stage[:, sl:sl + D],
                                accum_op=ALU.add).then_inc(sem.h, 16)
                    sem.v += 16
                    dg_out[tb] = (sem, sem.v)

    dump_reg.update(h_hg=h_hg[:], hq_fm=hq_fm[:], sfin=sfin[:], kc=kc[:],
                    xq_fm=xq_fm[:], wgu=wgu_buf[:], wd=wd_buf[:])
    dump_reg.pop("memz", None)
    dump_reg.pop("att_tm", None)
    dump_reg.pop("sq_fm", None)
    return _finish()


_NC_CACHE = None
_STOP = "all"
_DEBUG = False


def kernel(L0_attn_x, L0_attn_Wq, L0_attn_Wk, L0_attn_Wv, L0_attn_betas,
           L0_attn_Wo, L0_ffn_w_gate, L0_ffn_w_up, L0_ffn_w_down):
    global _NC_CACHE
    x = np.ascontiguousarray(np.asarray(L0_attn_x, np.float32).reshape(16384, D))
    wq = np.ascontiguousarray(np.asarray(L0_attn_Wq, np.float32))
    wk = np.ascontiguousarray(np.asarray(L0_attn_Wk, np.float32))
    wv = np.ascontiguousarray(np.asarray(L0_attn_Wv, np.float32))
    wo = np.ascontiguousarray(np.asarray(L0_attn_Wo, np.float32))
    betas = np.asarray(L0_attn_betas, np.float32).reshape(HD)
    betas_bc = np.ascontiguousarray(np.broadcast_to(betas[None, :], (128, HD)))
    wgT = np.ascontiguousarray(np.asarray(L0_ffn_w_gate, np.float32).T)
    wuT = np.ascontiguousarray(np.asarray(L0_ffn_w_up, np.float32).T)
    wdT = np.ascontiguousarray(np.asarray(L0_ffn_w_down, np.float32).T)

    if _NC_CACHE is None:
        _NC_CACHE = build_nc(stop=_STOP, debug=_DEBUG)
    nc = _NC_CACHE

    in_maps = []
    for c in range(N_CORES):
        mask = np.ascontiguousarray(
            np.broadcast_to((np.arange(N_CORES) < c).astype(np.float32)[None, :],
                            (128, N_CORES)))
        in_maps.append({
            "x": np.ascontiguousarray(x[c * T:(c + 1) * T]),
            "wq": wq, "wk": wk, "wv": wv, "wo": wo,
            "betas_bc": betas_bc,
            "wg_slice": np.ascontiguousarray(wgT[c * 128:(c + 1) * 128]),
            "wu_slice": np.ascontiguousarray(wuT[c * 128:(c + 1) * 128]),
            "wd_slice": np.ascontiguousarray(wdT[c * 512:(c + 1) * 512]),
            "mask_bc": mask,
        })
    res = run_bass_kernel_spmd(nc, in_maps, core_ids=list(range(N_CORES)))
    if _DEBUG:
        global _LAST_RES
        _LAST_RES = res
    outp = np.concatenate([r["out"] for r in res.results], axis=0)
    return outp.reshape(1, 16384, D)
